# version 02
# baseline (speedup 1.0000x reference)
import contextlib
import numpy as np
import ml_dtypes
import concourse.bass as bass
import concourse.mybir as mybir
from concourse.bass_utils import run_bass_kernel_spmd

F32 = mybir.dt.float32
BF16 = mybir.dt.bfloat16
AF = mybir.ActivationFunctionType
ALU = mybir.AluOpType
AX = mybir.AxisListType
BF = ml_dtypes.bfloat16

D = 2048
NT = 1024
DE = 8192
DI = 4096
CONVD = 6144
NF = 81
NCH = 8
KD = 16
KE = 64
KI = 32
LN_EPS = 1e-5
GN_EPS = 1e-5
TERN_EPS = 1e-8
NW = float(D * DE)

STAGE = 3
LNCUT = 4


def build(stage=STAGE):
    nc = bass.Bass("TRN2", target_bir_lowering=False, debug=False, detect_race_conditions=True)

    u_fm = nc.dram_tensor("u_fm", [D, NT], F32, kind="ExternalInput").ap()
    w_in_t = nc.dram_tensor("w_in_t", [NF * 16 * 128, 128], F32, kind="ExternalInput").ap()
    w_out_t = nc.dram_tensor("w_out_t", [16 * 32 * 128, 128], F32, kind="ExternalInput").ap()
    wup_sh = nc.dram_tensor("wup_sh", [D, 1024], F32, kind="ExternalInput").ap()
    wg_sh = nc.dram_tensor("wg_sh", [D, 1024], F32, kind="ExternalInput").ap()
    wd_sh = nc.dram_tensor("wd_sh", [DE, 256], F32, kind="ExternalInput").ap()
    conv_w = nc.dram_tensor("conv_w", [CONVD, 4], F32, kind="ExternalInput").ap()
    conv_b = nc.dram_tensor("conv_b", [CONVD, 1], F32, kind="ExternalInput").ap()
    dt_bias = nc.dram_tensor("dt_bias", [64, 1], F32, kind="ExternalInput").ap()
    a_log = nc.dram_tensor("a_log", [64, 1], F32, kind="ExternalInput").ap()
    d_exp = nc.dram_tensor("d_exp", [128, 32], F32, kind="ExternalInput").ap()
    norm_w = nc.dram_tensor("norm_w", [128, 32], F32, kind="ExternalInput").ap()
    identb = nc.dram_tensor("identb", [128, 128], BF16, kind="ExternalInput").ap()
    identf = nc.dram_tensor("identf", [128, 128], F32, kind="ExternalInput").ap()
    trilb = nc.dram_tensor("trilb", [128, 128], BF16, kind="ExternalInput").ap()
    onesf_in = nc.dram_tensor("onesf", [128, 128], F32, kind="ExternalInput").ap()
    onesb_in = nc.dram_tensor("onesb", [128, 128], BF16, kind="ExternalInput").ap()
    gn_oh_in = nc.dram_tensor("gn_oh", [128, 32 * 32], BF16, kind="ExternalInput").ap()
    hmask_in = nc.dram_tensor("hmask", [128, 8], F32, kind="ExternalInput").ap()
    fconst_in = nc.dram_tensor("fconst", [128, 8], F32, kind="ExternalInput").ap()
    out = nc.dram_tensor("out", [D, NT], F32, kind="ExternalOutput").ap()
    dbg = nc.dram_tensor("dbg", [128, 16], F32, kind="ExternalOutput").ap()

    ar0_in = nc.dram_tensor("ar0_in", [1, 8], F32)
    ar0_out = nc.dram_tensor("ar0_out", [1, 8], F32)
    ag1_in = nc.dram_tensor("ag1_in", [3 * 16384, 128], BF16)
    ag1_out = nc.dram_tensor("ag1_out", [8 * 3 * 16384, 128], BF16, addr_space="Shared")
    ag2_in = nc.dram_tensor("ag2_in", [D, 4], F32)
    ag2_out = nc.dram_tensor("ag2_out", [8 * D, 4], F32, addr_space="Shared")
    ag3_in = nc.dram_tensor("ag3_in", [128, 4096], BF16)
    ag3_out = nc.dram_tensor("ag3_out", [8 * 128, 4096], BF16, addr_space="Shared")
    xs1 = nc.dram_tensor("xs1", [D, NT], F32)
    xs2 = nc.dram_tensor("xs2", [D, NT], F32)
    zsc = nc.dram_tensor("zsc", [DI, NT], BF16)
    xbcsc = nc.dram_tensor("xbcsc", [CONVD, NT], BF16)

    rg8 = [[0, 1, 2, 3, 4, 5, 6, 7]]

    def ag1_rows(off, n):
        return ag1_out.ap()[off:off + n, :]

    def tup_block(e):
        r, el = e // 8, e % 8
        return ag1_rows(r * 49152 + el * 2048, 2048)

    def tg_block(e):
        r, el = e // 8, e % 8
        return ag1_rows(r * 49152 + 16384 + el * 2048, 2048)

    def td_block(m):
        r, ml = m // 2, m % 2
        return ag1_rows(r * 49152 + 32768 + ml * 8192, 8192)

    def r3(dram_slice, kdim):
        return dram_slice.rearrange("(k p) c -> p k c", p=128)

    def s3(sbuf_ap, kdim):
        return sbuf_ap.rearrange("p (k c) -> p k c", k=kdim)

    ctx = contextlib.ExitStack()
    sems = {}

    def sem(name):
        if name not in sems:
            sems[name] = ctx.enter_context(nc.semaphore(name))
        return sems[name]

    PH = [0]

    with ctx:
        cst = ctx.enter_context(nc.sbuf_tensor("c_identb", [128, 128], BF16))
        c_identb = cst
        c_identf = ctx.enter_context(nc.sbuf_tensor("c_identf", [128, 128], F32))
        c_trilb = ctx.enter_context(nc.sbuf_tensor("c_trilb", [128, 128], BF16))
        c_onesf = ctx.enter_context(nc.sbuf_tensor("c_onesf", [128, 128], F32))
        c_onesb = ctx.enter_context(nc.sbuf_tensor("c_onesb", [128, 128], BF16))
        c_gnoh = ctx.enter_context(nc.sbuf_tensor("c_gnoh", [128, 32 * 32], BF16))
        c_hmask = ctx.enter_context(nc.sbuf_tensor("c_hmask", [128, 8], F32))
        c_convw = ctx.enter_context(nc.sbuf_tensor("c_convw", [128, 48 * 4], F32))
        c_convb = ctx.enter_context(nc.sbuf_tensor("c_convb", [128, 48], F32))
        c_dtb = ctx.enter_context(nc.sbuf_tensor("c_dtb", [64, 1], F32))
        c_av = ctx.enter_context(nc.sbuf_tensor("c_av", [64, 1], F32))
        c_dexp = ctx.enter_context(nc.sbuf_tensor("c_dexp", [128, 32], F32))
        c_normw = ctx.enter_context(nc.sbuf_tensor("c_normw", [128, 32], F32))
        c_diagd = ctx.enter_context(nc.sbuf_tensor("c_diagd", [128, 32 * 128], BF16))
        v_thu = ctx.enter_context(nc.sbuf_tensor("v_thu", [128, 1], F32))
        v_thun = ctx.enter_context(nc.sbuf_tensor("v_thun", [128, 1], F32))
        v_thg = ctx.enter_context(nc.sbuf_tensor("v_thg", [128, 1], F32))
        v_thgn = ctx.enter_context(nc.sbuf_tensor("v_thgn", [128, 1], F32))
        v_thd = ctx.enter_context(nc.sbuf_tensor("v_thd", [128, 1], F32))
        v_thdn = ctx.enter_context(nc.sbuf_tensor("v_thdn", [128, 1], F32))
        v_su = ctx.enter_context(nc.sbuf_tensor("v_su", [128, 1], F32))
        v_sdg = ctx.enter_context(nc.sbuf_tensor("v_sdg", [128, 1], F32))
        s_row = ctx.enter_context(nc.sbuf_tensor("s_row", [1, 16], F32))
        c_fc = ctx.enter_context(nc.sbuf_tensor("c_fc", [128, 8], F32))
        drv = ctx.enter_context(nc.sbuf_tensor("drv", [128, 2], F32))
        drs = ctx.enter_context(nc.sbuf_tensor("drs", [128, 2], F32))
        drg = ctx.enter_context(nc.sbuf_tensor("drg", [128, 2], F32))
        block = ctx.enter_context(nc.Block())

        consts = dict(identb=c_identb, identf=c_identf, trilb=c_trilb, onesf=c_onesf,
                      onesb=c_onesb, gnoh=c_gnoh, hmask=c_hmask, convw=c_convw,
                      convb=c_convb, dtb=c_dtb, av=c_av, dexp=c_dexp, normw=c_normw,
                      diagd=c_diagd)

        ph_sems = {e: sem("ph_" + e) for e in ("sy", "g", "t", "v", "s")}

        def phase_end(sync_wait=None, gp_wait=None, scalar_wait=None):
            tgt = PH[0] + 1

            def _aslist(x):
                if x is None:
                    return []
                if isinstance(x, list):
                    return x
                return [x]

            def wait_all(eng):
                for e in ("sy", "g", "t", "v", "s"):
                    eng.wait_ge(ph_sems[e], tgt)

            @block.sync
            def _(sy):
                for sw in _aslist(sync_wait):
                    sy.wait_ge(sw[0], sw[1])
                sy.sem_inc(ph_sems["sy"], 1)
                wait_all(sy)

            @block.gpsimd
            def _(g):
                for sw in _aslist(gp_wait):
                    g.wait_ge(sw[0], sw[1])
                g.tensor_copy(drg[:, 1:2], drg[:, 0:1]).then_inc(ph_sems["g"], 1)
                wait_all(g)

            @block.tensor
            def _(t):
                t.ldweights(c_onesb[:, 0:128]).then_inc(ph_sems["t"], 1)
                wait_all(t)

            @block.vector
            def _(v):
                v.tensor_copy(drv[:, 1:2], drv[:, 0:1]).then_inc(ph_sems["v"], 1)
                wait_all(v)

            @block.scalar
            def _(s):
                for sw in _aslist(scalar_wait):
                    s.wait_ge(sw[0], sw[1])
                s.copy(drs[:, 1:2], drs[:, 0:1]).then_inc(ph_sems["s"], 1)
                wait_all(s)

            PH[0] = tgt

        s_cld = sem("cld")
        s_e0s = sem("e0s")
        s_e0v = sem("e0v")
        s_e0g = sem("e0g")

        NCLD = [0]

        @block.sync
        def _(sy):
            for dst, srcc in ((c_identb, identb), (c_identf, identf),
                              (c_trilb, trilb), (c_onesf, onesf_in),
                              (c_onesb, onesb_in), (c_gnoh, gn_oh_in),
                              (c_hmask, hmask_in), (c_fc, fconst_in), (c_dtb, dt_bias),
                              (c_av, a_log), (c_dexp, d_exp), (c_normw, norm_w)):
                sy.dma_start(out=dst[:], in_=srcc).then_inc(s_cld, 16)
                NCLD[0] += 1
            for cf in range(48):
                sy.dma_start(out=c_convw[:, cf * 4:cf * 4 + 4],
                             in_=conv_w[cf * 128:cf * 128 + 128, :]).then_inc(s_cld, 16)
                sy.dma_start(out=c_convb[:, cf:cf + 1],
                             in_=conv_b[cf * 128:cf * 128 + 128, :]).then_inc(s_cld, 16)
                NCLD[0] += 2

        @block.scalar
        def _(s):
            s.memzero(drs[:])
            s.wait_ge(s_cld, 16 * NCLD[0])
            s.activation(c_av[:], c_av[:], AF.Exp).then_inc(s_e0s, 1)
            s.wait_ge(s_e0s, 1)
            s.mul(c_av[:], c_av[:], -1.0).then_inc(s_e0s, 1)

        @block.vector
        def _(v):
            v.memset(s_row[:], 0.0)
            v.memset(drv[:], 0.0)
            v.wait_ge(s_cld, 16 * NCLD[0])
            s_dg0 = sem("dg0")
            for j in range(32):
                v.tensor_scalar(c_diagd[:, j * 128:(j + 1) * 128], c_onesb[:],
                                c_dexp[:, j:j + 1], None, ALU.mult).then_inc(s_dg0, 1)
            v.wait_ge(s_dg0, 32)
            v.tensor_copy(drv[:, 1:2], drv[:, 0:1]).then_inc(s_e0v, 1)

        @block.gpsimd
        def _(g):
            g.memset(drg[:], 0.0)
            g.wait_ge(s_e0v, 1)
            for j in range(32):
                g.affine_select(c_diagd[:, j * 128:(j + 1) * 128],
                                c_diagd[:, j * 128:(j + 1) * 128],
                                pattern=[[1, 128]], base=0, channel_multiplier=-1,
                                compare_op=ALU.is_equal, fill=0.0).then_inc(s_e0g, 1)

        shard_plan = [("u", wup_sh, 16, 1024), ("g", wg_sh, 16, 1024),
                      ("d", wd_sh, 64, 256)]
        s_wl = (sem("wl0"), sem("wl1"))
        s_ar = sem("ar")
        s_wred = sem("wred")
        s_p0t = sem("p0t")
        s_p0v = sem("p0v")
        s_ag = sem("ag")
        s_tv = sem("tv")
        s_ts = (sem("ts0"), sem("ts1"))
        win_per = []
        for nm, dram, nl, ncols in shard_plan:
            win_per += [4096 // ncols] * 4
        win_per = win_per * 2
        wcum = [[0] * 24, [0] * 24]
        c0 = c1 = 0
        for w, per in enumerate(win_per):
            if w % 2 == 0:
                c0 += per
            else:
                c1 += per
            wcum[0][w] = c0
            wcum[1][w] = c1

        with (
            nc.sbuf_tensor("tw_win", [128, 2 * 4096], F32) as tw_win,
            nc.sbuf_tensor("tw_acc", [128, 12], F32) as tw_acc,
            nc.sbuf_tensor("tw_red", [128, 3], F32) as tw_red,
            nc.psum_tensor("tw_ps", [1, 4], F32) as tw_ps,
            nc.psum_tensor("tw_pb", [128, 16], F32) as tw_pb,
        ):
            @block.sync
            def _(sy):
                w = 0
                for nm, dram, nl, ncols in shard_plan:
                    per = 4096 // ncols
                    for w0 in range(0, nl, per):
                        if w >= 2:
                            sy.wait_ge(s_wred, w - 1)
                        buf = tw_win[:, (w % 2) * 4096:(w % 2) * 4096 + 4096]
                        for i in range(per):
                            k = w0 + i
                            sy.dma_start(out=buf[:, i * ncols:(i + 1) * ncols],
                                         in_=dram[k * 128:(k + 1) * 128, :]).then_inc(s_wl[w % 2], 16)
                        w += 1

            @block.vector
            def _(v):
                w = 0
                for nm, dram, nl, ncols in shard_plan:
                    per = 4096 // ncols
                    for wi in range(4):
                        v.wait_ge(s_wl[w % 2], 16 * wcum[w % 2][w])
                        buf = tw_win[:, (w % 2) * 4096:(w % 2) * 4096 + 4096]
                        v.tensor_reduce(tw_acc[:, w:w + 1], buf[:], AX.X, ALU.add,
                                        apply_absolute_value=True).then_inc(s_wred, 1)
                        w += 1
                v.wait_ge(s_wred, 12)
                for mi in range(3):
                    v.tensor_reduce(tw_red[:, mi:mi + 1], tw_acc[:, mi * 4:mi * 4 + 4],
                                    AX.X, ALU.add).then_inc(s_wred, 1)

            @block.tensor
            def _(t):
                t.wait_ge(s_wred, 15)
                t.matmul(tw_ps[0:1, 0:3], c_onesf[:, 0:1], tw_red[:, 0:3],
                         start=True, stop=True).then_inc(s_p0t, 1)

            @block.vector
            def _(v):
                v.wait_ge(s_p0t, 1)
                s_ms = sem("ms")
                v.memset(s_row[:, 4:12], 0.0).then_inc(s_ms, 1)
                v.wait_ge(s_ms, 1)
                v.tensor_copy(s_row[:, 4:7], tw_ps[0:1, 0:3]).then_inc(s_p0v, 1)

            @block.sync
            def _(sy):
                sy.wait_ge(s_p0v, 1)
                sy.dma_start(out=ar0_in.ap()[0:1, 0:8], in_=s_row[:, 4:12]).then_inc(s_ar, 16)

            @block.gpsimd
            def _(g):
                g.wait_ge(s_ar, 16)
                g.collective_compute("AllReduce", ALU.add, ins=[ar0_in.ap().opt()],
                                     outs=[ar0_out.ap().opt()],
                                     replica_groups=rg8).then_inc(s_ag, 1)

            @block.sync
            def _(sy):
                sy.wait_ge(s_ag, 1)
                sy.dma_start(out=s_row[:, 8:11], in_=ar0_out.ap()[0:1, 0:3]).then_inc(s_ar, 16)

            s_chn = sem("chn")

            @block.vector
            def _(v):
                v.wait_ge(s_ar, 32)
                v.tensor_scalar(s_row[:, 8:11], s_row[:, 8:11], 1.0 / NW, TERN_EPS,
                                ALU.mult, ALU.add).then_inc(s_chn, 1)
                v.wait_ge(s_chn, 1)
                v.tensor_scalar(s_row[:, 11:14], s_row[:, 8:11], 0.5, None, ALU.mult)
                v.tensor_tensor(s_row[:, 14:15], s_row[:, 9:10], s_row[:, 10:11],
                                ALU.mult).then_inc(s_chn, 1)
                v.wait_ge(s_chn, 2)
                v.tensor_scalar(s_row[:, 14:15], s_row[:, 14:15], 0.5, None,
                                ALU.mult).then_inc(s_chn, 1)
                v.wait_ge(s_chn, 3)
                v.tensor_copy(drv[:, 1:2], drv[:, 0:1]).then_inc(s_p0v, 1)

            @block.tensor
            def _(t):
                t.wait_ge(s_p0v, 2)
                cols5 = (11, 12, 13, 8, 14)
                for i, col in enumerate(cols5):
                    mm = t.matmul(tw_pb[:, i:i + 1], c_onesf[0:1, 0:128],
                                  s_row[:, col:col + 1], start=True, stop=True)
                    if i == len(cols5) - 1:
                        mm.then_inc(s_p0t, 1)

            @block.vector
            def _(v):
                v.wait_ge(s_p0t, 2)
                s_thv = sem("thv")
                for i, dst in enumerate((v_thu, v_thg, v_thd, v_su, v_sdg)):
                    v.tensor_copy(dst[:], tw_pb[:, i:i + 1]).then_inc(s_thv, 1)
                for i, dst in enumerate((v_thun, v_thgn, v_thdn)):
                    v.tensor_scalar(dst[:], tw_pb[:, i:i + 1], -1.0, None,
                                    ALU.mult).then_inc(s_thv, 1)
                v.wait_ge(s_thv, 8)
                v.tensor_copy(drv[:, 1:2], drv[:, 0:1]).then_inc(s_p0v, 1)

            with (
                nc.sbuf_tensor("tn_a", [128, 4096], F32) as tn_a,
                nc.sbuf_tensor("tn_t", [128, 2 * 4096], BF16) as tn_t,
            ):
                thvec = {"u": (v_thu, v_thun), "g": (v_thg, v_thgn), "d": (v_thd, v_thdn)}

                @block.sync
                def _(sy):
                    sy.wait_ge(s_wred, 15)
                    w = 12
                    for nm, dram, nl, ncols in shard_plan:
                        per = 4096 // ncols
                        for w0 in range(0, nl, per):
                            wt = w - 12
                            if wt >= 2:
                                sy.wait_ge(s_tv, 2 * (wt - 1))
                            buf = tw_win[:, (w % 2) * 4096:(w % 2) * 4096 + 4096]
                            for i in range(per):
                                k = w0 + i
                                sy.dma_start(out=buf[:, i * ncols:(i + 1) * ncols],
                                             in_=dram[k * 128:(k + 1) * 128, :]).then_inc(s_wl[w % 2], 16)
                            w += 1

                @block.vector
                def _(v):
                    v.wait_ge(s_p0v, 3)
                    w = 12
                    for nm, dram, nl, ncols in shard_plan:
                        per = 4096 // ncols
                        thp, thn = thvec[nm]
                        for wi in range(4):
                            wt = w - 12
                            v.wait_ge(s_wl[w % 2], 16 * wcum[w % 2][w])
                            if wt >= 2:
                                v.wait_ge(s_ts[wt % 2], 16 * 32 * ((wt - 2) // 2 + 1))
                            buf = tw_win[:, (w % 2) * 4096:(w % 2) * 4096 + 4096]
                            tb = tn_t[:, (w % 2) * 4096:(w % 2) * 4096 + 4096]
                            if wt >= 1:
                                v.wait_ge(s_tv, 2 * wt)
                            v.tensor_scalar(tn_a[:], buf[:], thn[:], None,
                                            ALU.is_lt).then_inc(s_tv, 1)
                            v.wait_ge(s_tv, 2 * wt + 1)
                            v.scalar_tensor_tensor(tb[:], buf[:], thp[:], tn_a[:],
                                                   ALU.is_gt, ALU.subtract).then_inc(s_tv, 1)
                            w += 1

                @block.scalar
                def _(s):
                    w = 0
                    for nm, dram, nl, ncols in shard_plan:
                        per = 4096 // ncols
                        for wi in range(4):
                            s.wait_ge(s_tv, 2 * (w + 1))
                            tb = tn_t[:, (w % 2) * 4096:(w % 2) * 4096 + 4096]
                            for i in range(per):
                                k = wi * per + i
                                for el in range(ncols // 128):
                                    if nm == "u":
                                        roff = el * 2048 + k * 128
                                    elif nm == "g":
                                        roff = 16384 + el * 2048 + k * 128
                                    else:
                                        roff = 32768 + el * 8192 + k * 128
                                    s.dma_start(out=ag1_in.ap()[roff:roff + 128, :],
                                                in_=tb[:, i * ncols + el * 128:i * ncols + el * 128 + 128]
                                                ).then_inc(s_ts[w % 2], 16)
                            w += 1

                @block.gpsimd
                def _(g):
                    g.wait_ge(s_ts[0], 16 * 32 * 6)
                    g.wait_ge(s_ts[1], 16 * 32 * 6)
                    g.collective_compute("AllGather", ALU.bypass, ins=[ag1_in.ap().opt()],
                                         outs=[ag1_out.ap().opt()],
                                         replica_groups=rg8).then_inc(s_ag, 1)

        phase_end(scalar_wait=[(s_ts[0], 16 * 32 * 6), (s_ts[1], 16 * 32 * 6)])

        def emit_dummy_out():
            s_do = sem("dout")

            @block.sync
            def _(sy):
                for k in range(KD):
                    sy.dma_start(out=out[k * 128:(k + 1) * 128, :],
                                 in_=u_fm[k * 128:(k + 1) * 128, :]).then_inc(s_do, 16)
                sy.wait_ge(s_do, 16 * KD)
            phase_end()

        if stage < 0.4:
            emit_dummy_out()
            return nc

        SEMB = {}

        def takebase(name, final):
            b = SEMB.get(name, 0)
            SEMB[name] = b + final
            return b

        def emit_ln(tag, src_dram, xh, ncols=1024, colstride=None, dst_off=0):
            if colstride is None:
                colstride = ncols
            s_lt = sem("lnlt")
            s_lv = sem("lnlv")
            s_ls = sem("lnls")
            s_lw = (sem("lnlw0"), sem("lnlw1"))
            blt = takebase("lnlt", KD + 1)
            blv = takebase("lnlv", 6 * KD + 3)
            bls = takebase("lnls", 1)
            blw = (takebase("lnlw0", 16 * 16), takebase("lnlw1", 16 * 16))
            nsp = 2
            with (
                nc.sbuf_tensor(tag + "win", [128, 2 * 1024], F32) as win,
                nc.sbuf_tensor(tag + "wb", [128, 2 * 1024], BF16) as winb,
                nc.sbuf_tensor(tag + "sq", [128, 2 * 1024], BF16) as sqb,
                nc.sbuf_tensor(tag + "mr", [1, 2 * ncols], F32) as mrow,
                nc.sbuf_tensor(tag + "dd", [128, 1024], F32) as dbuf,
                nc.psum_tensor(tag + "psxA", [1, 512], F32) as psxA,
                nc.psum_tensor(tag + "psxB", [1, 512], F32) as psxB,
                nc.psum_tensor(tag + "psqA", [1, 512], F32) as psqA,
                nc.psum_tensor(tag + "psqB", [1, 512], F32) as psqB,
                nc.psum_tensor(tag + "pmuA", [128, 512], F32) as pmuA,
                nc.psum_tensor(tag + "pmuB", [128, 512], F32) as pmuB,
                nc.psum_tensor(tag + "prA", [128, 512], F32) as prA,
                nc.psum_tensor(tag + "prB", [128, 512], F32) as prB,
            ):
                psx = (psxA, psxB)
                psq = (psqA, psqB)
                pmu = (pmuA, pmuB)
                pr = (prA, prB)
                splits = [(0, 512), (512, 1024)]

                @block.sync
                def _(sy):
                    for k in range(KD):
                        if k >= 2:
                            sy.wait_ge(s_lt, blt + k - 1)
                        sy.dma_start(out=win[:, (k % 2) * 1024:(k % 2) * 1024 + ncols],
                                     in_=src_dram[k * 128:(k + 1) * 128, :]).then_inc(s_lw[k % 2], 16)

                @block.vector
                def _(v):
                    for k in range(KD):
                        v.wait_ge(s_lw[k % 2], blw[k % 2] + 16 * (k // 2 + 1))
                        if k >= 2:
                            v.wait_ge(s_lt, blt + k - 1)
                        v.tensor_copy(winb[:, (k % 2) * 1024:(k % 2) * 1024 + ncols],
                                      win[:, (k % 2) * 1024:(k % 2) * 1024 + ncols]).then_inc(s_lv, 1)
                        v.tensor_tensor(sqb[:, (k % 2) * 1024:(k % 2) * 1024 + ncols],
                                        win[:, (k % 2) * 1024:(k % 2) * 1024 + ncols],
                                        win[:, (k % 2) * 1024:(k % 2) * 1024 + ncols],
                                        ALU.mult).then_inc(s_lv, 1)

                @block.tensor
                def _(t):
                    for k in range(KD):
                        t.wait_ge(s_lv, blv + 2 * (k + 1))
                        for sp, (lo, hi) in enumerate(splits):
                            t.matmul(psx[sp][0:1, :], c_onesb[:, 0:1],
                                     winb[:, (k % 2) * 1024 + lo:(k % 2) * 1024 + hi],
                                     start=(k == 0), stop=(k == KD - 1))
                            mm = t.matmul(psq[sp][0:1, :], c_onesb[:, 0:1],
                                          sqb[:, (k % 2) * 1024 + lo:(k % 2) * 1024 + hi],
                                          start=(k == 0), stop=(k == KD - 1))
                            if sp == len(splits) - 1:
                                mm.then_inc(s_lt, 1)

                if LNCUT < 2:
                    return

                s_lx = sem("lnlx")
                blx = takebase("lnlx", 4)

                @block.vector
                def _(v):
                    v.wait_ge(s_lt, blt + KD)
                    for sp, (lo, hi) in enumerate(splits):
                        v.tensor_scalar(mrow[:, lo:hi], psx[sp][0:1, :], 1.0 / D,
                                        None, ALU.mult).then_inc(s_lx, 1)
                    v.wait_ge(s_lx, blx + 2)
                    for sp, (lo, hi) in enumerate(splits):
                        v.tensor_tensor(mrow[:, ncols + lo:ncols + hi], mrow[:, lo:hi],
                                        mrow[:, lo:hi], ALU.mult).then_inc(s_lx, 1)
                    v.wait_ge(s_lx, blx + 4)
                    for sp, (lo, hi) in enumerate(splits):
                        v.scalar_tensor_tensor(mrow[:, ncols + lo:ncols + hi], psq[sp][0:1, :],
                                               1.0 / D, mrow[:, ncols + lo:ncols + hi],
                                               ALU.mult, ALU.subtract).then_inc(s_lv, 1)

                @block.scalar
                def _(s):
                    s.wait_ge(s_lv, blv + 2 * KD + 2)
                    s.activation(mrow[:, ncols:2 * ncols], mrow[:, ncols:2 * ncols],
                                 AF.Sqrt, bias=c_fc[0:1, 0:1]).then_inc(s_ls, 1)

                @block.vector
                def _(v):
                    v.wait_ge(s_ls, bls + 1)
                    v.reciprocal(mrow[:, ncols:2 * ncols],
                                 mrow[:, ncols:2 * ncols]).then_inc(s_lv, 1)

                if LNCUT < 3:
                    return

                @block.tensor
                def _(t):
                    t.wait_ge(s_lv, blv + 2 * KD + 3)
                    for sp, (lo, hi) in enumerate(splits):
                        t.matmul(pmu[sp][:, :], c_onesf[0:1, 0:128],
                                 mrow[:, lo:hi], start=True, stop=True)
                        mm = t.matmul(pr[sp][:, :], c_onesf[0:1, 0:128],
                                      mrow[:, ncols + lo:ncols + hi], start=True, stop=True)
                        if sp == len(splits) - 1:
                            mm.then_inc(s_lt, 1)

                if LNCUT < 4:
                    return

                @block.sync
                def _(sy):
                    for k in range(KD):
                        if k == 0:
                            sy.wait_ge(s_lt, blt + KD)
                        if k >= 2:
                            sy.wait_ge(s_lv, blv + 2 * KD + 3 + 4 * (k - 1))
                        sy.dma_start(out=win[:, (k % 2) * 1024:(k % 2) * 1024 + ncols],
                                     in_=src_dram[k * 128:(k + 1) * 128, :]).then_inc(s_lw[k % 2], 16)

                @block.vector
                def _(v):
                    v.wait_ge(s_lt, blt + KD + 1)
                    base = blv + 2 * KD + 3
                    for k in range(KD):
                        v.wait_ge(s_lw[k % 2], blw[k % 2] + 16 * (KD // 2 + k // 2 + 1))
                        if k >= 1:
                            v.wait_ge(s_lv, base + 4 * k)
                        for sp, (lo, hi) in enumerate(splits):
                            v.tensor_tensor(dbuf[:, sp * 512:sp * 512 + hi - lo],
                                            win[:, (k % 2) * 1024 + lo:(k % 2) * 1024 + hi],
                                            pmu[sp][:, :], ALU.subtract).then_inc(s_lv, 1)
                        v.wait_ge(s_lv, base + 4 * k + 2)
                        for sp, (lo, hi) in enumerate(splits):
                            v.tensor_tensor(xh[:, k * colstride + dst_off + lo:k * colstride + dst_off + hi],
                                            dbuf[:, sp * 512:sp * 512 + hi - lo],
                                            pr[sp][:, :], ALU.mult).then_inc(s_lv, 1)

        def emit_ffn_u(tag, xh, hid):
            s_fw = (sem("fuw0"), sem("fuw1"))
            s_ft = sem("fut")
            s_fs = sem("fus")
            s_fv = sem("fuv")
            bfw = (takebase("fuw0", 16 * 64), takebase("fuw1", 16 * 64))
            bft = takebase("fut", 4 * KE)
            bfs = takebase("fus", 2 * KE)
            bfv = takebase("fuv", 2 * KE)
            with (
                nc.sbuf_tensor(tag + "tu", [128, 2 * 2048], BF16) as tub,
                nc.sbuf_tensor(tag + "tg", [128, 2 * 2048], BF16) as tgb,
                nc.sbuf_tensor(tag + "sil", [128, 2048], F32) as sil,
                nc.psum_tensor(tag + "pu0a", [128, 512], F32) as pu0a,
                nc.psum_tensor(tag + "pu0b", [128, 512], F32) as pu0b,
                nc.psum_tensor(tag + "pu1a", [128, 512], F32) as pu1a,
                nc.psum_tensor(tag + "pu1b", [128, 512], F32) as pu1b,
                nc.psum_tensor(tag + "pg0a", [128, 512], F32) as pg0a,
                nc.psum_tensor(tag + "pg0b", [128, 512], F32) as pg0b,
                nc.psum_tensor(tag + "pg1a", [128, 512], F32) as pg1a,
                nc.psum_tensor(tag + "pg1b", [128, 512], F32) as pg1b,
            ):
                pu = ((pu0a, pu0b), (pu1a, pu1b))
                pg = ((pg0a, pg0b), (pg1a, pg1b))

                @block.sync
                def _(sy):
                    sy.wait_ge(s_ag, 2)
                    for e in range(KE):
                        if e >= 2:
                            sy.wait_ge(s_ft, bft + 4 * (e - 1))
                        sy.dma_start(out=s3(tub[:, (e % 2) * 2048:(e % 2) * 2048 + 2048], 16),
                                     in_=r3(tup_block(e), 16)).then_inc(s_fw[e % 2], 16)
                        sy.dma_start(out=s3(tgb[:, (e % 2) * 2048:(e % 2) * 2048 + 2048], 16),
                                     in_=r3(tg_block(e), 16)).then_inc(s_fw[e % 2], 16)

                @block.tensor
                def _(t):
                    for e in range(KE):
                        t.wait_ge(s_fw[e % 2], bfw[e % 2] + 32 * (e // 2 + 1))
                        if e >= 2:
                            t.wait_ge(s_fs, bfs + 2 * (e - 1))
                            t.wait_ge(s_fv, bfv + 2 * (e - 1))
                        tu = tub[:, (e % 2) * 2048:(e % 2) * 2048 + 2048]
                        tg = tgb[:, (e % 2) * 2048:(e % 2) * 2048 + 2048]
                        for sp in range(2):
                            for k in range(KD):
                                mm = t.matmul(pu[e % 2][sp][:, :],
                                              tu[:, k * 128:(k + 1) * 128],
                                              xh[:, k * 1024 + sp * 512:k * 1024 + sp * 512 + 512],
                                              start=(k == 0), stop=(k == KD - 1))
                            mm.then_inc(s_ft, 1)
                        for sp in range(2):
                            for k in range(KD):
                                mm = t.matmul(pg[e % 2][sp][:, :],
                                              tg[:, k * 128:(k + 1) * 128],
                                              xh[:, k * 1024 + sp * 512:k * 1024 + sp * 512 + 512],
                                              start=(k == 0), stop=(k == KD - 1))
                            mm.then_inc(s_ft, 1)

                @block.scalar
                def _(s):
                    for e in range(KE):
                        if e >= 2:
                            s.wait_ge(s_fv, bfv + 2 * (e - 1))
                        for sp in range(2):
                            s.wait_ge(s_ft, bft + 4 * e + sp + 1)
                            s.activation(sil[:, (e % 2) * 1024 + sp * 512:(e % 2) * 1024 + sp * 512 + 512],
                                         pu[e % 2][sp][:, :],
                                         AF.Silu, scale=v_su[:]).then_inc(s_fs, 1)

                @block.vector
                def _(v):
                    for e in range(KE):
                        for sp in range(2):
                            v.wait_ge(s_fs, bfs + 2 * e + sp + 1)
                            v.wait_ge(s_ft, bft + 4 * e + 2 + sp + 1)
                            v.tensor_tensor(hid[:, e * 1024 + sp * 512:e * 1024 + sp * 512 + 512],
                                            sil[:, (e % 2) * 1024 + sp * 512:(e % 2) * 1024 + sp * 512 + 512],
                                            pg[e % 2][sp][:, :],
                                            ALU.mult).then_inc(s_fv, 1)

                phase_end()

        def emit_ffn_d(tag, hid, resid_dram, dst_dram, also_out, save_tail):
            s_dw = (sem("fdw0"), sem("fdw1"))
            s_du = (sem("fdu0"), sem("fdu1"))
            s_dt = sem("fdt")
            s_dv = sem("fdv")
            s_dst = (sem("fdst0"), sem("fdst1"))
            s_done = sem("fddone")
            per_m = 1 + (1 if also_out else 0) + (1 if save_tail else 0)
            stc = [[0] * 16, [0] * 16]
            c0 = c1 = 0
            for m in range(16):
                if m % 2 == 0:
                    c0 += per_m
                else:
                    c1 += per_m
                stc[0][m] = c0
                stc[1][m] = c1
            bdw = (takebase("fdw0", 16 * 8), takebase("fdw1", 16 * 8))
            bdu = (takebase("fdu0", 16 * 8), takebase("fdu1", 16 * 8))
            bdt = takebase("fdt", 32)
            bdv = takebase("fdv", 32)
            bdst = (takebase("fdst0", 16 * stc[0][-1]), takebase("fdst1", 16 * stc[1][-1]))
            bdone = takebase("fddone", 1)
            with (
                nc.sbuf_tensor(tag + "td", [128, 2 * 8192], BF16) as tdb,
                nc.sbuf_tensor(tag + "ub", [128, 2 * 1024], F32) as ub,
                nc.sbuf_tensor(tag + "x1b", [128, 2 * 1024], F32) as x1b,
                nc.psum_tensor(tag + "pd0a", [128, 512], F32) as pd0a,
                nc.psum_tensor(tag + "pd0b", [128, 512], F32) as pd0b,
                nc.psum_tensor(tag + "pd1a", [128, 512], F32) as pd1a,
                nc.psum_tensor(tag + "pd1b", [128, 512], F32) as pd1b,
            ):
                pd = ((pd0a, pd0b), (pd1a, pd1b))

                @block.sync
                def _(sy):
                    for m in range(16):
                        if m >= 2:
                            sy.wait_ge(s_dt, bdt + 2 * (m - 1))
                        sy.dma_start(out=s3(tdb[:, (m % 2) * 8192:(m % 2) * 8192 + 8192], 64),
                                     in_=r3(td_block(m), 64)).then_inc(s_dw[m % 2], 16)
                        if m >= 2:
                            sy.wait_ge(s_dv, bdv + 2 * (m - 1))
                        sy.dma_start(out=ub[:, (m % 2) * 1024:(m % 2) * 1024 + 1024],
                                     in_=resid_dram[m * 128:(m + 1) * 128, :]).then_inc(s_du[m % 2], 16)

                @block.tensor
                def _(t):
                    for m in range(16):
                        t.wait_ge(s_dw[m % 2], bdw[m % 2] + 16 * (m // 2 + 1))
                        if m >= 2:
                            t.wait_ge(s_dv, bdv + 2 * (m - 1))
                        td = tdb[:, (m % 2) * 8192:(m % 2) * 8192 + 8192]
                        for sp in range(2):
                            for e in range(KE):
                                mm = t.matmul(pd[m % 2][sp][:, :],
                                              td[:, e * 128:(e + 1) * 128],
                                              hid[:, e * 1024 + sp * 512:e * 1024 + sp * 512 + 512],
                                              start=(e == 0), stop=(e == KE - 1))
                            mm.then_inc(s_dt, 1)

                @block.vector
                def _(v):
                    for m in range(16):
                        v.wait_ge(s_du[m % 2], bdu[m % 2] + 16 * (m // 2 + 1))
                        if m >= 2:
                            v.wait_ge(s_dst[m % 2], bdst[m % 2] + 16 * stc[m % 2][m - 2])
                        for sp in range(2):
                            v.wait_ge(s_dt, bdt + 2 * m + sp + 1)
                            v.scalar_tensor_tensor(
                                x1b[:, (m % 2) * 1024 + sp * 512:(m % 2) * 1024 + sp * 512 + 512],
                                pd[m % 2][sp][:, :],
                                v_sdg[:],
                                ub[:, (m % 2) * 1024 + sp * 512:(m % 2) * 1024 + sp * 512 + 512],
                                ALU.mult, ALU.add).then_inc(s_dv, 1)

                @block.scalar
                def _(s):
                    for m in range(16):
                        s.wait_ge(s_dv, bdv + 2 * (m + 1))
                        xb = x1b[:, (m % 2) * 1024:(m % 2) * 1024 + 1024]
                        s.dma_start(out=dst_dram[m * 128:(m + 1) * 128, :],
                                    in_=xb).then_inc(s_dst[m % 2], 16)
                        if also_out:
                            s.dma_start(out=out[m * 128:(m + 1) * 128, :],
                                        in_=xb).then_inc(s_dst[m % 2], 16)
                        if save_tail:
                            s.dma_start(out=ag2_in.ap()[m * 128:(m + 1) * 128, 0:4],
                                        in_=xb[:, 1020:1024]).then_inc(s_dst[m % 2], 16)
                    s.wait_ge(s_dst[0], bdst[0] + 16 * stc[0][-1])
                    s.wait_ge(s_dst[1], bdst[1] + 16 * stc[1][-1])
                    s.sem_inc(s_done, 1)
            return (s_done, bdone + 1)

        with nc.sbuf_tensor("hid1", [128, KE * 1024], BF16) as hid1:
            with nc.sbuf_tensor("xh1", [128, KD * 1024], BF16) as xh1:
                emit_ln("l1", u_fm, xh1)
                phase_end()
                if stage < 0.7:
                    emit_dummy_out()
                    return nc
                emit_ffn_u("f1", xh1, hid1)
                if stage < 0.9:
                    emit_dummy_out()
                    return nc
            f1done, f1cnt = emit_ffn_d("f1", hid1, u_fm, xs1, stage == 1, True)

        @block.gpsimd
        def _(g):
            g.wait_ge(f1done, f1cnt)
            g.collective_compute("AllGather", ALU.bypass, ins=[ag2_in.ap().opt()],
                                 outs=[ag2_out.ap().opt()],
                                 replica_groups=rg8).then_inc(s_ag, 1)

        phase_end()

        s_dbg = sem("dbg")

        @block.sync
        def _(sy):
            sy.dma_start(out=dbg[0:1, :], in_=s_row[:, :]).then_inc(s_dbg, 16)
            sy.dma_start(out=dbg[1:2, 0:1], in_=v_su[0:1, :]).then_inc(s_dbg, 16)
            sy.dma_start(out=dbg[2:3, 0:1], in_=v_sdg[0:1, :]).then_inc(s_dbg, 16)
            sy.dma_start(out=dbg[3:4, 0:1], in_=v_thu[0:1, :]).then_inc(s_dbg, 16)
            sy.wait_ge(s_dbg, 64)

        def emit_scan(stage):
            import collections
            CNT = collections.defaultdict(int)

            def E(instr, s, key, n=1):
                CNT[key] += n
                instr.then_inc(s, n)
                return CNT[key]

            s2v = sem("s2v")
            s2s = sem("s2s")
            s2t = sem("s2t")
            s2l = sem("s2l")

            @block.vector
            def _(v):
                E(v.tensor_scalar(dAf[:], dtf[:], c_av[:], None, ALU.mult), s2v, "2v")
                v.wait_ge(s2v, CNT["2v"])
                for c in range(NCH):
                    E(v.tensor_tensor_scan(dAcs[:, 128 * c:128 * c + 128],
                                           dAf[:, 128 * c:128 * c + 128],
                                           dAf[:, 128 * c:128 * c + 128],
                                           0.0, ALU.add, ALU.bypass), s2v, "2v")
                v.wait_ge(s2v, CNT["2v"])
                dv3 = dAcs[:].rearrange("h (c l) -> h c l", c=NCH)
                E(v.tensor_copy(lasts[:, 0:8].rearrange("h (c o) -> h c o", o=1),
                                dv3[:, :, 127:128]), s2v, "2v")
                v.wait_ge(s2v, CNT["2v"])
                E(v.tensor_tensor_scan(lasts[:, 8:16], lasts[:, 0:8], lasts[:, 0:8],
                                       0.0, ALU.add, ALU.bypass), s2v, "2v")
                v.wait_ge(s2v, CNT["2v"])
                E(v.tensor_scalar(lasts[:, 16:24], lasts[:, 8:16], -1.0,
                                  lasts[:, 15:16], ALU.mult, ALU.add), s2v, "2v")
            V_DACS = 9
            V_TAIL = CNT["2v"]

            @block.scalar
            def _(s):
                s.wait_ge(s2v, V_DACS)
                for c in range(NCH):
                    E(s.activation(decayf[:, 128 * c:128 * c + 128],
                                   dAcs[:, 128 * c:128 * c + 128], AF.Exp,
                                   bias=dAcs[:, 128 * c + 127:128 * c + 128],
                                   scale=c_fc[0:64, 3:4]), s2s, "2s")
                E(s.activation(expd[:], dAcs[:], AF.Exp), s2s, "2s")
                s.wait_ge(s2v, V_TAIL)
                E(s.activation(tpf[:], lasts[:, 16:24], AF.Exp), s2s, "2s")
                E(s.activation(cdf[:], lasts[:, 0:8], AF.Exp), s2s, "2s")
            S_DEC = 8
            S_CD = CNT["2s"]

            @block.vector
            def _(v):
                v.wait_ge(s2s, S_DEC)
                E(v.tensor_tensor(dtdyf[:], dtf[:], decayf[:], ALU.mult), s2v, "2v")
                v.wait_ge(s2v, CNT["2v"])
                v.wait_ge(s2s, S_DEC + 2)
                for c in range(NCH):
                    E(v.tensor_scalar(dtdtf[:, 128 * c:128 * c + 128],
                                      dtdyf[:, 128 * c:128 * c + 128],
                                      tpf[:, c:c + 1], None, ALU.mult), s2v, "2v")
            V_DTDT = CNT["2v"]

            @block.sync
            def _(sy):
                sy.wait_ge(s2s, S_CD)
                sy.dma_start(out=cdrow[:], in_=cdf[:]).then_inc(s2l, 16)

            with (
                nc.psum_tensor("p2c", [128, 512], F32) as p2c,
                nc.psum_tensor("p2t0", [128, 512], F32) as p2t0,
                nc.psum_tensor("p2t1", [128, 512], F32) as p2t1,
            ):
                p2t = (p2t0, p2t1)

                @block.tensor
                def _(t):
                    t.wait_ge(s2l, 16)
                    E(t.matmul(p2c[:], c_onesf[0:1, 0:128], cdrow[:],
                               start=True, stop=True), s2t, "2t")
                    t.wait_ge(s2v, V_DTDT)
                    for c in range(NCH):
                        if c >= 2:
                            t.wait_ge(s2v, V_DTDT + 1 + 4 * (c - 1))
                        for si, srcm in enumerate((dAcs, dtf, dtdyf, dtdtf)):
                            E(t.matmul(p2t[c % 2][:, 128 * si:128 * si + 64],
                                       srcm[:, 128 * c:128 * c + 128],
                                       c_identf[0:64, 0:64], start=True, stop=True,
                                       is_transpose=True), s2t, "2t")

                @block.vector
                def _(v):
                    v.wait_ge(s2t, 1)
                    E(v.tensor_copy(cdb[:], p2c[:]), s2v, "2v")
                    for c in range(NCH):
                        v.wait_ge(s2t, 1 + 4 * (c + 1))
                        for si, dstm in enumerate((datok, dttok, dctok, ddtok)):
                            E(v.tensor_copy(dstm[:, 64 * c:64 * c + 64],
                                            p2t[c % 2][:, 128 * si:128 * si + 64]),
                              s2v, "2v")
                phase_end()

            s3l = (sem("s3l0"), sem("s3l1"))
            s3t = sem("s3t")
            s3e = sem("s3e")
            s3v2 = sem("s3v2")
            s3t2 = sem("s3t2")
            s3a = sem("s3a")
            with (
                nc.sbuf_tensor("xbw", [128, 2 * 2048], BF16) as xbw,
                nc.sbuf_tensor("xdb1", [128, 2 * 512], BF16) as xdb1,
                nc.sbuf_tensor("scac", [128, 4096], F32) as scac,
                nc.sbuf_tensor("ag3st", [128, 4096], BF16) as ag3st,
                nc.psum_tensor("pT0", [128, 512], BF16) as pT0,
                nc.psum_tensor("pT1", [128, 512], BF16) as pT1,
                nc.psum_tensor("p3a0", [128, 512], F32) as p3a0,
                nc.psum_tensor("p3a1", [128, 512], F32) as p3a1,
            ):
                pT = (pT0, pT1)
                p3a = (p3a0, p3a1)

                @block.sync
                def _(sy):
                    for c in range(NCH):
                        if c >= 2:
                            sy.wait_ge(s3t, 16 * (c - 1))
                        sy.dma_start(
                            out=xbw[:, (c % 2) * 2048:(c % 2) * 2048 + 2048].rearrange(
                                "p (j w) -> p j w", j=16),
                            in_=xbcsc.ap()[0:2048, 128 * c:128 * c + 128].rearrange(
                                "(j p) w -> p j w", p=128)).then_inc(s3l[c % 2], 16)

                @block.tensor
                def _(t):
                    for c in range(NCH):
                        t.wait_ge(s3l[c % 2], 16 * (c // 2 + 1))
                        for f in range(4):
                            Tf = 4 * c + f
                            if Tf >= 2:
                                t.wait_ge(s3e, Tf - 1)
                            for i in range(4):
                                E(t.matmul(pT[Tf % 2][:, 128 * i:128 * i + 128],
                                           xbw[:, (c % 2) * 2048 + 128 * (4 * f + i):
                                               (c % 2) * 2048 + 128 * (4 * f + i) + 128],
                                           c_identb[:], start=True, stop=True,
                                           is_transpose=True), s3t, "3t")

                @block.vector
                def _(v):
                    for c in range(NCH):
                        for f in range(4):
                            Tf = 4 * c + f
                            v.wait_ge(s3t, 4 * (Tf + 1))
                            if f < 2:
                                dst = xtokb[:, 1024 * c + 512 * f:1024 * c + 512 * f + 512]
                            else:
                                dst = btokb[:, 1024 * c + 512 * (f - 2):
                                            1024 * c + 512 * (f - 2) + 512]
                            E(v.tensor_copy(dst, pT[Tf % 2][:]), s3e, "3e")

                def emit_accum(it2):
                    c, hg = it2 // 8, it2 % 8

                    @block.vector
                    def _(v):
                        v.wait_ge(s3t2, it2 + 1)
                        if c == 0:
                            E(v.tensor_copy(scac[:, 512 * hg:512 * hg + 512],
                                            p3a[it2 % 2][:]), s3a, "3a")
                        else:
                            v.wait_ge(s3a, 8 * (c - 1) + hg + 1)
                            E(v.scalar_tensor_tensor(scac[:, 512 * hg:512 * hg + 512],
                                                     p3a[it2 % 2][:], 1.0,
                                                     scac[:, 512 * hg:512 * hg + 512],
                                                     ALU.mult, ALU.add), s3a, "3a")

                for it2 in range(64):
                    c, hg = it2 // 8, it2 % 8

                    @block.vector
                    def _(v):
                        if it2 == 0:
                            v.wait_ge(s3e, 32)
                        if it2 >= 2:
                            v.wait_ge(s3t2, it2 - 1)
                        for hh in range(8):
                            h = 8 * hg + hh
                            v.tensor_scalar(
                                xdb1[:, (it2 % 2) * 512 + 64 * hh:(it2 % 2) * 512 + 64 * hh + 64],
                                xtokb[:, 1024 * c + 128 * hg + 64 * (hh % 2):
                                      1024 * c + 128 * hg + 64 * (hh % 2) + 64],
                                ddtok[:, 64 * c + h:64 * c + h + 1], None,
                                ALU.mult).then_inc(s3v2, 1)

                    @block.tensor
                    def _(t):
                        t.wait_ge(s3v2, 8 * (it2 + 1))
                        if it2 >= 2:
                            t.wait_ge(s3a, it2 - 1)
                        for hh in range(8):
                            mm = t.matmul(p3a[it2 % 2][:, 64 * hh:64 * hh + 64],
                                          btokb[:, 1024 * c + 128 * hg:
                                                1024 * c + 128 * hg + 128],
                                          xdb1[:, (it2 % 2) * 512 + 64 * hh:
                                               (it2 % 2) * 512 + 64 * hh + 64],
                                          start=True, stop=True)
                        mm.then_inc(s3t2, 1)

                    if it2 >= 1:
                        emit_accum(it2 - 1)
                emit_accum(63)

                @block.vector
                def _(v):
                    v.wait_ge(s3a, 64)
                    E(v.tensor_copy(ag3st[:], scac[:]), s3a, "3a")

                @block.sync
                def _(sy):
                    sy.wait_ge(s3a, 65)
                    sy.dma_start(out=ag3_in.ap(), in_=ag3st[:]).then_inc(s3l[0], 16)

                @block.gpsimd
                def _(g):
                    g.wait_ge(s3l[0], 16 * 5)
                    g.collective_compute("AllGather", ALU.bypass, ins=[ag3_in.ap().opt()],
                                         outs=[ag3_out.ap().opt()],
                                         replica_groups=rg8).then_inc(s_ag, 1)
                phase_end()

            s4v = sem("s4v")
            s3l_base = (16 * 5, 16 * 4)
            with nc.sbuf_tensor("agw", [128, 2 * 4096], BF16) as agw:
                @block.sync
                def _(sy):
                    sy.wait_ge(s_ag, 4)
                    for r in range(8):
                        if r >= 2:
                            sy.wait_ge(s4v, r - 1)
                        sy.dma_start(out=agw[:, (r % 2) * 4096:(r % 2) * 4096 + 4096],
                                     in_=ag3_out.ap()[128 * r:128 * r + 128, :]
                                     ).then_inc(s3l[r % 2], 16)

                @block.vector
                def _(v):
                    for r in range(8):
                        v.wait_ge(s3l[r % 2], s3l_base[r % 2] + 16 * (r // 2 + 1))
                        if r == 0:
                            E(v.tensor_scalar(carry[:], agw[:, 0:4096],
                                              c_hmask[:, 0:1], None, ALU.mult),
                              s4v, "4v")
                        else:
                            v.wait_ge(s4v, r)
                            E(v.scalar_tensor_tensor(carry[:],
                                                     agw[:, (r % 2) * 4096:(r % 2) * 4096 + 4096],
                                                     c_hmask[:, r:r + 1], carry[:],
                                                     ALU.mult, ALU.add), s4v, "4v")
                    v.wait_ge(s4v, 8)
                    E(v.tensor_copy(carryb[:], carry[:]), s4v, "4v")
                phase_end()

            s5l = sem("s5l")
            s5g = (sem("s5g0"), sem("s5g1"))
            s5t = sem("s5t")
            s5v = sem("s5v")
            s5s = sem("s5s")
            s5r = sem("s5r")
            s5st = sem("s5st")
            s5w = (sem("s5w0"), sem("s5w1"))
            m5ctx = contextlib.ExitStack()
            with m5ctx:
                xw = m5ctx.enter_context(nc.sbuf_tensor("xw", [128, 48 * 128], BF16))
                szb = m5ctx.enter_context(nc.sbuf_tensor("szb", [128, 4096], BF16))
                x1w = m5ctx.enter_context(nc.sbuf_tensor("x1w", [128, 16 * 128], F32))
                dar = m5ctx.enter_context(nc.sbuf_tensor("dar", [1, 2 * 1024], F32))
                exr = m5ctx.enter_context(nc.sbuf_tensor("exr", [1, 2 * 1024], BF16))
                rcb = m5ctx.enter_context(nc.sbuf_tensor("rcb", [128, 2 * 128], F32))
                ebb = m5ctx.enter_context(nc.sbuf_tensor("ebb", [128, 2 * 128], BF16))
                gmb = m5ctx.enter_context(nc.sbuf_tensor("gmb", [128, 128], BF16))
                glb = m5ctx.enter_context(nc.sbuf_tensor("glb", [128, 2 * 128], BF16))
                cvb = m5ctx.enter_context(nc.sbuf_tensor("cvb", [128, 2 * 128], BF16))
                xdb = m5ctx.enter_context(nc.sbuf_tensor("xdb", [128, 2 * 64], BF16))
                sqb2 = m5ctx.enter_context(nc.sbuf_tensor("sqb2", [128, 2 * 128], BF16))
                ybb = m5ctx.enter_context(nc.sbuf_tensor("ybb", [128, 4096], BF16))
                yhb = m5ctx.enter_context(nc.sbuf_tensor("yhb", [128, 4096], BF16))
                sdb = m5ctx.enter_context(nc.sbuf_tensor("sdb", [32, 128], F32))
                sdbb = m5ctx.enter_context(nc.sbuf_tensor("sdbb", [32, 128], BF16))
                rrow = m5ctx.enter_context(nc.sbuf_tensor("rrow", [1, 4096], BF16))
                wob = m5ctx.enter_context(nc.sbuf_tensor("wob", [128, 2 * 4096], BF16))
                x2w = m5ctx.enter_context(nc.sbuf_tensor("x2w", [128, 16 * 128], F32))
                pP = (m5ctx.enter_context(nc.psum_tensor("pP0", [128, 512], F32)),
                      m5ctx.enter_context(nc.psum_tensor("pP1", [128, 512], F32)))
                pY = (m5ctx.enter_context(nc.psum_tensor("pY0", [128, 512], F32)),
                      m5ctx.enter_context(nc.psum_tensor("pY1", [128, 512], F32)))
                pGN = m5ctx.enter_context(nc.psum_tensor("pGN", [32, 512], F32))
                for c in range(NCH):
                    @block.sync
                    def _(sy):
                        sy.dma_start(
                            out=xw[:].rearrange("p (j w) -> p j w", j=48),
                            in_=xbcsc.ap()[:, 128 * c:128 * c + 128].rearrange(
                                "(j p) w -> p j w", p=128)).then_inc(s5l, 16)
                        sy.dma_start(
                            out=szb[:].rearrange("p (j w) -> p j w", j=32),
                            in_=zsc.ap()[:, 128 * c:128 * c + 128].rearrange(
                                "(j p) w -> p j w", p=128)).then_inc(s5l, 16)
                        sy.dma_start(
                            out=x1w[:].rearrange("p (j w) -> p j w", j=16),
                            in_=xs1.ap()[:, 128 * c:128 * c + 128].rearrange(
                                "(j p) w -> p j w", p=128)).then_inc(s5l, 16)
                    CNT["5l"] += 3
                    lw_all = CNT["5l"]

                    @block.scalar
                    def _(s):
                        s.wait_ge(s5l, 16 * lw_all)
                        E(s.activation(szb[:], szb[:], AF.Silu), s5s, "5s")
                    sz_cnt = CNT["5s"]

                    e_after = {}
                    carry_after = {}
                    ty_after = {}
                    yevac_after = {}
                    gn_after = {}
                    mid_after = {}
                    tcnt_hg = {}
                    for h in range(64):
                        hg, hh, pr, j = h // 8, h % 8, h % 2, h // 2
                        ro = 64 * pr
                        if hh == 0:
                            gate = tcnt_hg.get(hg - 2)

                            @block.sync
                            def _(sy):
                                if gate is not None:
                                    sy.wait_ge(s5t, gate)
                                sy.dma_start(out=dar[0:1, (hg % 2) * 1024:(hg % 2) * 1024 + 1024],
                                             in_=dAcs[8 * hg:8 * hg + 8, 128 * c:128 * c + 128]
                                             ).then_inc(s5g[hg % 2], 16)
                                sy.dma_start(out=exr[0:1, (hg % 2) * 1024:(hg % 2) * 1024 + 1024],
                                             in_=expd[8 * hg:8 * hg + 8, 128 * c:128 * c + 128]
                                             ).then_inc(s5g[hg % 2], 16)
                            CNT["5g%d" % (hg % 2)] += 2
                        gw = CNT["5g%d" % (hg % 2)]
                        cg = carry_after.get(h - 2)

                        @block.tensor
                        def _(t):
                            t.wait_ge(s5g[hg % 2], 16 * gw)
                            t.wait_ge(s5l, 16 * lw_all)
                            if cg is not None:
                                t.wait_ge(s5v, cg)
                            t.matmul(pP[pr][:, 0:128], c_onesf[0:1, 0:128],
                                     dar[0:1, (hg % 2) * 1024 + 128 * hh:
                                         (hg % 2) * 1024 + 128 * hh + 128],
                                     start=True, stop=True)
                            mm = t.matmul(pP[pr][:, 128:256], c_onesb[0:1, 0:128],
                                          exr[0:1, (hg % 2) * 1024 + 128 * hh:
                                              (hg % 2) * 1024 + 128 * hh + 128],
                                          start=True, stop=True)
                            if pr == 0:
                                mm = t.matmul(pP[pr][:, 256:384],
                                              xw[:, 128 * (8 + hg):128 * (8 + hg) + 128],
                                              xw[:, 128 * (16 + j):128 * (16 + j) + 128],
                                              start=True, stop=True)
                            E(mm, s5t, "5t")
                        t1 = CNT["5t"]
                        tcnt_hg[hg] = t1
                        ep = e_after.get(h - 2)

                        @block.vector
                        def _(v):
                            v.wait_ge(s5t, t1)
                            if ep is not None:
                                v.wait_ge(s5s, ep)
                            E(v.tensor_scalar(rcb[:, pr * 128:pr * 128 + 128],
                                              pP[pr][:, 0:128],
                                              datok[:, 64 * c + h:64 * c + h + 1], 0.0,
                                              ALU.subtract, ALU.min), s5v, "5v")
                        vrc = CNT["5v"]

                        @block.scalar
                        def _(s):
                            s.wait_ge(s5v, vrc)
                            E(s.activation(ebb[:, pr * 128:pr * 128 + 128],
                                           rcb[:, pr * 128:pr * 128 + 128], AF.Exp),
                              s5s, "5s")
                        e_after[h] = CNT["5s"]
                        typ = ty_after.get(h - 2)
                        mp = mid_after.get(h - 1)

                        @block.vector
                        def _(v):
                            v.wait_ge(s5s, e_after[h])
                            if typ is not None:
                                v.wait_ge(s5t, typ)
                            if pr == 0:
                                if mp is not None:
                                    v.wait_ge(s5v, mp)
                                E(v.tensor_tensor(gmb[:], pP[pr][:, 256:384],
                                                  c_trilb[:], ALU.mult), s5v, "5v")
                                v.wait_ge(s5v, CNT["5v"])
                            E(v.scalar_tensor_tensor(glb[:, pr * 128:pr * 128 + 128],
                                                     ebb[:, pr * 128:pr * 128 + 128],
                                                     dttok[:, 64 * c + h:64 * c + h + 1],
                                                     gmb[:], ALU.mult, ALU.mult),
                              s5v, "5v")
                            E(v.tensor_tensor(cvb[:, pr * 128:pr * 128 + 128],
                                              xw[:, 128 * (16 + j):128 * (16 + j) + 128],
                                              pP[pr][:, 128:256], ALU.mult), s5v, "5v")
                            E(v.tensor_scalar(xdb[:, pr * 64:pr * 64 + 64],
                                              xtokb[:, 1024 * c + 128 * hg + ro:
                                                    1024 * c + 128 * hg + ro + 64],
                                              dctok[:, 64 * c + h:64 * c + h + 1], None,
                                              ALU.mult), s5v, "5v")
                        mid_after[h] = CNT["5v"]
                        yg = yevac_after.get(j - 2) if pr == 0 else None

                        @block.tensor
                        def _(t):
                            t.wait_ge(s5v, mid_after[h])
                            if yg is not None:
                                t.wait_ge(s5v, yg)
                            t.matmul(pY[j % 2][ro:ro + 64, 0:128],
                                     xtokb[:, 1024 * c + 128 * hg + ro:
                                           1024 * c + 128 * hg + ro + 64],
                                     glb[:, pr * 128:pr * 128 + 128],
                                     start=True, stop=False)
                            t.matmul(pY[j % 2][ro:ro + 64, 0:128],
                                     carryb[:, 64 * h:64 * h + 64],
                                     cvb[:, pr * 128:pr * 128 + 128],
                                     start=False, stop=False)
                            t.matmul(pY[j % 2][ro:ro + 64, 0:128],
                                     c_diagd[ro:ro + 64, 128 * j + ro:128 * j + ro + 64],
                                     xw[ro:ro + 64, 128 * hg:128 * hg + 128],
                                     start=False, stop=True)
                            E(t.matmul(pP[pr][:, 384:448],
                                       btokb[:, 1024 * c + 128 * hg:1024 * c + 128 * hg + 128],
                                       xdb[:, pr * 64:pr * 64 + 64],
                                       start=True, stop=True), s5t, "5t")
                        ty_after[h] = CNT["5t"]

                        @block.vector
                        def _(v):
                            v.wait_ge(s5t, ty_after[h])
                            E(v.tensor_scalar(carry[:, 64 * h:64 * h + 64],
                                              carry[:, 64 * h:64 * h + 64],
                                              cdb[:, 8 * h + c:8 * h + c + 1], None,
                                              ALU.mult), s5v, "5v")
                            v.wait_ge(s5v, CNT["5v"])
                            E(v.scalar_tensor_tensor(carry[:, 64 * h:64 * h + 64],
                                                     pP[pr][:, 384:448], 1.0,
                                                     carry[:, 64 * h:64 * h + 64],
                                                     ALU.mult, ALU.add), s5v, "5v")
                            v.wait_ge(s5v, CNT["5v"])
                            E(v.tensor_copy(carryb[:, 64 * h:64 * h + 64],
                                            carry[:, 64 * h:64 * h + 64]), s5v, "5v")
                        carry_after[h] = CNT["5v"]

                        if pr == 1:
                            gp = gn_after.get(j - 2)

                            @block.vector
                            def _(v):
                                v.wait_ge(s5t, ty_after[h])
                                v.wait_ge(s5s, sz_cnt)
                                if gp is not None:
                                    v.wait_ge(s5t, gp)
                                E(v.tensor_tensor(ybb[:, 128 * j:128 * j + 128],
                                                  pY[j % 2][:, 0:128],
                                                  szb[:, 128 * j:128 * j + 128],
                                                  ALU.mult), s5v, "5v")
                                v.wait_ge(s5v, CNT["5v"])
                                E(v.tensor_tensor(sqb2[:, (j % 2) * 128:(j % 2) * 128 + 128],
                                                  ybb[:, 128 * j:128 * j + 128],
                                                  ybb[:, 128 * j:128 * j + 128],
                                                  ALU.mult), s5v, "5v")
                            yevac_after[j] = CNT["5v"]

                            @block.tensor
                            def _(t):
                                t.wait_ge(s5v, yevac_after[j])
                                E(t.matmul(pGN[:, 0:128], c_gnoh[:, 32 * j:32 * j + 32],
                                           sqb2[:, (j % 2) * 128:(j % 2) * 128 + 128],
                                           start=(j == 0), stop=(j == 31)), s5t, "5t")
                            gn_after[j] = CNT["5t"]

                    tgn = CNT["5t"]

                    @block.scalar
                    def _(s):
                        s.wait_ge(s5t, tgn)
                        E(s.activation(sdb[:], pGN[:, 0:128], AF.Sqrt,
                                       bias=c_fc[0:32, 0:1], scale=c_fc[0:32, 1:2]),
                          s5s, "5s")
                    srs = CNT["5s"]

                    @block.vector
                    def _(v):
                        v.wait_ge(s5s, srs)
                        E(v.reciprocal(sdb[:], sdb[:]), s5v, "5v")
                        v.wait_ge(s5v, CNT["5v"])
                        E(v.tensor_copy(sdbb[:], sdb[:]), s5v, "5v")
                    vrs = CNT["5v"]

                    @block.sync
                    def _(sy):
                        sy.wait_ge(s5v, vrs)
                        sy.dma_start(out=rrow[:], in_=sdbb[:]).then_inc(s5r, 16)
                    CNT["5r"] += 1

                    yh_after = {}
                    for g2 in range(32):
                        pv = yh_after.get(g2 - 2)

                        @block.tensor
                        def _(t):
                            t.wait_ge(s5r, 16 * CNT["5r"])
                            if pv is not None:
                                t.wait_ge(s5v, pv)
                            E(t.matmul(pP[g2 % 2][:, 0:128], c_onesb[0:1, 0:128],
                                       rrow[0:1, 128 * g2:128 * g2 + 128],
                                       start=True, stop=True), s5t, "5t")
                        trb = CNT["5t"]

                        @block.vector
                        def _(v):
                            v.wait_ge(s5t, trb)
                            E(v.scalar_tensor_tensor(yhb[:, 128 * g2:128 * g2 + 128],
                                                     ybb[:, 128 * g2:128 * g2 + 128],
                                                     c_normw[:, g2:g2 + 1],
                                                     pP[g2 % 2][:, 0:128],
                                                     ALU.mult, ALU.mult), s5v, "5v")
                        yh_after[g2] = CNT["5v"]

                    yv_all = CNT["5v"]
                    tom_after = {}
                    oev_after = {}
                    for m in range(16):
                        tp = tom_after.get(m - 2)

                        @block.gpsimd
                        def _(gp):
                            if tp is not None:
                                gp.wait_ge(s5t, tp)
                            gp.dma_start(
                                out=s3(wob[:, (m % 2) * 4096:(m % 2) * 4096 + 4096], 32),
                                in_=r3(w_out_t[m * 4096:(m + 1) * 4096, :], 32)
                                ).then_inc(s5w[m % 2], 16)
                        CNT["5w%d" % (m % 2)] += 1
                        wn = CNT["5w%d" % (m % 2)]
                        ev = oev_after.get(m - 2)

                        @block.tensor
                        def _(t):
                            t.wait_ge(s5w[m % 2], 16 * wn)
                            t.wait_ge(s5v, yv_all)
                            if ev is not None:
                                t.wait_ge(s5v, ev)
                            for kc in range(32):
                                mm = t.matmul(pY[m % 2][:, 0:128],
                                              wob[:, (m % 2) * 4096 + 128 * kc:
                                                  (m % 2) * 4096 + 128 * (kc + 1)],
                                              yhb[:, 128 * kc:128 * kc + 128],
                                              start=(kc == 0), stop=(kc == 31))
                            E(mm, s5t, "5t")
                        tom_after[m] = CNT["5t"]

                        @block.vector
                        def _(v):
                            v.wait_ge(s5t, tom_after[m])
                            E(v.tensor_tensor(x2w[:, 128 * m:128 * m + 128],
                                              pY[m % 2][:, 0:128], x1w[:, 128 * m:128 * m + 128],
                                              ALU.add), s5v, "5v")
                        oev_after[m] = CNT["5v"]

                    vall = CNT["5v"]

                    @block.sync
                    def _(sy):
                        sy.wait_ge(s5v, vall)
                        sy.dma_start(out=xs2.ap()[:, 128 * c:128 * c + 128].rearrange(
                                         "(j p) w -> p j w", p=128),
                                     in_=x2w[:].rearrange("p (j w) -> p j w", j=16)
                                     ).then_inc(s5st, 16)
                        if stage == 2:
                            sy.dma_start(out=out[:, 128 * c:128 * c + 128].rearrange(
                                             "(j p) w -> p j w", p=128),
                                         in_=x2w[:].rearrange("p (j w) -> p j w", j=16)
                                         ).then_inc(s5st, 16)
                    CNT["5st"] += 2 if stage == 2 else 1

                    phase_end(sync_wait=(s5st, 16 * CNT["5st"]))

        if stage >= 2:
            mctx = contextlib.ExitStack()
            xctx = contextlib.ExitStack()
            dtf = mctx.enter_context(nc.sbuf_tensor("dtf", [64, 1024], F32))
            dAf = mctx.enter_context(nc.sbuf_tensor("dAf", [64, 1024], F32))
            dAcs = mctx.enter_context(nc.sbuf_tensor("dAcs", [64, 1024], F32))
            decayf = mctx.enter_context(nc.sbuf_tensor("decayf", [64, 1024], F32))
            expd = mctx.enter_context(nc.sbuf_tensor("expd", [64, 1024], BF16))
            dtdyf = mctx.enter_context(nc.sbuf_tensor("dtdyf", [64, 1024], F32))
            dtdtf = mctx.enter_context(nc.sbuf_tensor("dtdtf", [64, 1024], F32))
            lasts = mctx.enter_context(nc.sbuf_tensor("lasts", [64, 24], F32))
            tpf = mctx.enter_context(nc.sbuf_tensor("tpf", [64, 8], F32))
            cdf = mctx.enter_context(nc.sbuf_tensor("cdf", [64, 8], F32))
            cdb = mctx.enter_context(nc.sbuf_tensor("cdb", [128, 512], F32))
            cdrow = mctx.enter_context(nc.sbuf_tensor("cdrow", [1, 512], F32))
            datok = mctx.enter_context(nc.sbuf_tensor("datok", [128, 512], F32))
            dttok = mctx.enter_context(nc.sbuf_tensor("dttok", [128, 512], F32))
            dctok = mctx.enter_context(nc.sbuf_tensor("dctok", [128, 512], F32))
            ddtok = mctx.enter_context(nc.sbuf_tensor("ddtok", [128, 512], F32))
            xtokb = mctx.enter_context(nc.sbuf_tensor("xtokb", [128, 8 * 1024], BF16))
            btokb = mctx.enter_context(nc.sbuf_tensor("btokb", [128, 8 * 1024], BF16))
            carry = mctx.enter_context(nc.sbuf_tensor("carry", [128, 4096], F32))
            carryb = mctx.enter_context(nc.sbuf_tensor("carryb", [128, 4096], BF16))
            xh2 = xctx.enter_context(nc.sbuf_tensor("xh2", [128, KD * 1027], BF16))

            s_h = sem("mh")
            s_hs = sem("mhs")
            s_ht = sem("mht")
            s_hl = sem("mhl")
            with (
                nc.sbuf_tensor("hg", [128, 512], F32) as hg,
                nc.sbuf_tensor("h64", [128, 64], F32) as h64,
                nc.sbuf_tensor("hb", [128, 64], BF16) as hb,
                nc.sbuf_tensor("hsq", [128, 64], BF16) as hsq,
                nc.sbuf_tensor("hrow", [1, 16], F32) as hrow,
                nc.sbuf_tensor("hd", [128, 4], F32) as hd,
                nc.psum_tensor("phSx", [1, 512], F32) as phSx,
                nc.psum_tensor("phSq", [1, 512], F32) as phSq,
                nc.psum_tensor("phM", [128, 8], F32) as phM,
            ):
                @block.sync
                def _(sy):
                    sy.wait_ge(s_ag, 3)
                    sy.dma_start(out=hg[:].rearrange("p (r k c) -> p r k c", r=8, k=16),
                                 in_=ag2_out.ap().rearrange("(r k p) c -> p r k c", p=128, k=16)
                                 ).then_inc(s_hl, 16)

                @block.vector
                def _(v):
                    v.wait_ge(s_hl, 16)
                    v.tensor_scalar(h64[:], hg[:, 0:64], c_hmask[:, 0:1], None,
                                    ALU.mult).then_inc(s_h, 1)
                    for r in range(1, 8):
                        v.wait_ge(s_h, r)
                        v.scalar_tensor_tensor(h64[:], hg[:, 64 * r:64 * r + 64],
                                               c_hmask[:, r:r + 1], h64[:],
                                               ALU.mult, ALU.add).then_inc(s_h, 1)
                    v.wait_ge(s_h, 8)
                    v.tensor_copy(hb[:], h64[:]).then_inc(s_h, 1)
                    v.tensor_tensor(hsq[:], h64[:], h64[:], ALU.mult).then_inc(s_h, 1)

                @block.tensor
                def _(t):
                    t.wait_ge(s_h, 10)
                    for k in range(KD):
                        t.matmul(phSx[0:1, 0:4], c_onesb[:, 0:1], hb[:, 4 * k:4 * k + 4],
                                 start=(k == 0), stop=(k == KD - 1))
                        mm = t.matmul(phSq[0:1, 0:4], c_onesb[:, 0:1], hsq[:, 4 * k:4 * k + 4],
                                      start=(k == 0), stop=(k == KD - 1))
                    mm.then_inc(s_ht, 1)

                @block.vector
                def _(v):
                    v.wait_ge(s_ht, 1)
                    v.tensor_scalar(hrow[:, 0:4], phSx[0:1, 0:4], 1.0 / D, None,
                                    ALU.mult).then_inc(s_h, 1)
                    v.wait_ge(s_h, 11)
                    v.tensor_tensor(hrow[:, 8:12], hrow[:, 0:4], hrow[:, 0:4],
                                    ALU.mult).then_inc(s_h, 1)
                    v.wait_ge(s_h, 12)
                    v.scalar_tensor_tensor(hrow[:, 4:8], phSq[0:1, 0:4], 1.0 / D,
                                           hrow[:, 8:12], ALU.mult,
                                           ALU.subtract).then_inc(s_h, 1)

                @block.scalar
                def _(s):
                    s.wait_ge(s_h, 13)
                    s.activation(hrow[:, 4:8], hrow[:, 4:8], AF.Sqrt,
                                 bias=c_fc[0:1, 0:1]).then_inc(s_hs, 1)

                @block.vector
                def _(v):
                    v.wait_ge(s_hs, 1)
                    v.reciprocal(hrow[:, 4:8], hrow[:, 4:8]).then_inc(s_h, 1)

                @block.tensor
                def _(t):
                    t.wait_ge(s_h, 14)
                    t.matmul(phM[:, 0:4], c_onesf[0:1, 0:128], hrow[:, 0:4],
                             start=True, stop=True)
                    t.matmul(phM[:, 4:8], c_onesf[0:1, 0:128], hrow[:, 4:8],
                             start=True, stop=True).then_inc(s_ht, 1)

                @block.vector
                def _(v):
                    v.wait_ge(s_ht, 2)
                    for k in range(KD):
                        if k >= 1:
                            v.wait_ge(s_h, 14 + 2 * k)
                        v.tensor_tensor(hd[:], h64[:, 4 * k:4 * k + 4],
                                        phM[:, 0:4], ALU.subtract).then_inc(s_h, 1)
                        v.wait_ge(s_h, 14 + 2 * k + 1)
                        v.tensor_tensor(xh2[:, k * 1027:k * 1027 + 3],
                                        hd[:, 1:4], phM[:, 5:8], ALU.mult).then_inc(s_h, 1)

            phase_end()
            emit_ln("l2", xs1.ap(), xh2, ncols=1024, colstride=1027, dst_off=3)
            phase_end()

            s_mw = (sem("mw0"), sem("mw1"))
            s_mt = sem("mt")
            s_mv = sem("mv")
            s_msc = sem("msc")
            s_mz = (sem("mz0"), sem("mz1"))
            s_mx = (sem("mx0"), sem("mx1"))
            Tc = [0] * (NF + 1)
            Vc = [0] * (NF + 1)
            Sc = [0] * (NF + 1)
            for f in range(NF):
                Tc[f + 1] = Tc[f] + (2 if (f < 32 or f == 80) else 3)
                Vc[f + 1] = Vc[f] + (2 if f < 32 else (4 if f < 80 else 0))
                Sc[f + 1] = Sc[f] + (0 if f < 32 else (4 if f < 80 else 3))
            zcum = [[0] * NF, [0] * NF]
            xcum = [[0] * NF, [0] * NF]
            a = b = c2 = d2 = 0
            for f in range(NF):
                if f < 32:
                    if f % 2 == 0:
                        a += 1
                    else:
                        b += 1
                elif f < 80:
                    if f % 2 == 0:
                        c2 += 1
                    else:
                        d2 += 1
                zcum[0][f] = a
                zcum[1][f] = b
                xcum[0][f] = c2
                xcum[1][f] = d2

            with (
                nc.sbuf_tensor("wib", [128, 2 * 2048], BF16) as wib,
                nc.sbuf_tensor("zstb", [128, 2 * 1024], BF16) as zstb,
                nc.sbuf_tensor("xbcf", [128, 2 * 1027], F32) as xbcf,
                nc.sbuf_tensor("accf", [128, 2 * 1024], F32) as accf,
                nc.sbuf_tensor("xbst", [128, 2 * 1024], BF16) as xbst,
                nc.sbuf_tensor("ef", [64, 1024], F32) as ef,
                nc.psum_tensor("mpA0", [128, 512], F32) as mpA0,
                nc.psum_tensor("mpA1", [128, 512], F32) as mpA1,
                nc.psum_tensor("mpB0", [128, 512], F32) as mpB0,
                nc.psum_tensor("mpB1", [128, 512], F32) as mpB1,
                nc.psum_tensor("mpC0", [128, 4], F32) as mpC0,
                nc.psum_tensor("mpC1", [128, 4], F32) as mpC1,
            ):
                mpA = (mpA0, mpA1)
                mpB = (mpB0, mpB1)
                mpC = (mpC0, mpC1)

                @block.gpsimd
                def _(g):
                    for f in range(NF):
                        if f >= 2:
                            g.wait_ge(s_mt, Tc[f - 1])
                        g.dma_start(out=s3(wib[:, (f % 2) * 2048:(f % 2) * 2048 + 2048], 16),
                                    in_=r3(w_in_t[f * 2048:(f + 1) * 2048, :], 16)
                                    ).then_inc(s_mw[f % 2], 16)

                @block.tensor
                def _(t):
                    for f in range(NF):
                        t.wait_ge(s_mw[f % 2], 16 * (f // 2 + 1))
                        if f >= 2:
                            t.wait_ge(s_mv, Vc[f - 1])
                            t.wait_ge(s_msc, Sc[f - 1])
                        wt = wib[:, (f % 2) * 2048:(f % 2) * 2048 + 2048]
                        if f < 32 or f == 80:
                            spl = [(3, 515, mpA), (515, 1027, mpB)]
                        else:
                            spl = [(0, 512, mpA), (512, 1024, mpB), (1024, 1027, mpC)]
                        for si, (lo, hi, ps) in enumerate(spl):
                            for k in range(KD):
                                mm = t.matmul(ps[f % 2][:, 0:hi - lo],
                                              wt[:, k * 128:(k + 1) * 128],
                                              xh2[:, k * 1027 + lo:k * 1027 + hi],
                                              start=(k == 0), stop=(k == KD - 1))
                            mm.then_inc(s_mt, 1)

                @block.vector
                def _(v):
                    for f in range(NF):
                        if f < 32:
                            if f >= 2:
                                v.wait_ge(s_mz[f % 2], 16 * zcum[f % 2][f - 2])
                            for si, ps in ((0, mpA), (1, mpB)):
                                v.wait_ge(s_mt, Tc[f] + si + 1)
                                v.tensor_copy(zstb[:, (f % 2) * 1024 + si * 512:(f % 2) * 1024 + si * 512 + 512],
                                              ps[f % 2][:, :]).then_inc(s_mv, 1)
                        elif f < 80:
                            cf = f - 32
                            base = Vc[f]
                            v.wait_ge(s_msc, Sc[f] + 3)
                            if f >= 34:
                                v.wait_ge(s_msc, Sc[f - 1])
                            xb = xbcf[:, (f % 2) * 1027:(f % 2) * 1027 + 1027]
                            ac = accf[:, (f % 2) * 1024:(f % 2) * 1024 + 1024]
                            v.tensor_scalar(ac[:, :], xb[:, 0:1024],
                                            c_convw[:, cf * 4:cf * 4 + 1],
                                            c_convb[:, cf:cf + 1], ALU.mult,
                                            ALU.add).then_inc(s_mv, 1)
                            for kk in range(1, 4):
                                v.wait_ge(s_mv, base + kk)
                                v.scalar_tensor_tensor(ac[:, :], xb[:, kk:kk + 1024],
                                                       c_convw[:, cf * 4 + kk:cf * 4 + kk + 1],
                                                       ac[:, :], ALU.mult,
                                                       ALU.add).then_inc(s_mv, 1)

                @block.scalar
                def _(s):
                    for f in range(32, NF):
                        if f < 80:
                            if f >= 34:
                                s.wait_ge(s_mv, Vc[f - 1])
                            for si, (lo, hi, ps) in enumerate(((0, 512, mpA), (512, 1024, mpB),
                                                               (1024, 1027, mpC))):
                                s.wait_ge(s_mt, Tc[f] + si + 1)
                                s.copy(xbcf[:, (f % 2) * 1027 + lo:(f % 2) * 1027 + hi],
                                       ps[f % 2][:, 0:hi - lo]).then_inc(s_msc, 1)
                            s.wait_ge(s_mv, Vc[f + 1])
                            if f >= 34:
                                s.wait_ge(s_mx[f % 2], 16 * xcum[f % 2][f - 2])
                            s.activation(xbst[:, (f % 2) * 1024:(f % 2) * 1024 + 1024],
                                         accf[:, (f % 2) * 1024:(f % 2) * 1024 + 1024],
                                         AF.Silu).then_inc(s_msc, 1)
                        else:
                            for si, ps in ((0, mpA), (1, mpB)):
                                s.wait_ge(s_mt, Tc[f] + si + 1)
                                s.activation(ef[:, si * 512:si * 512 + 512],
                                             ps[f % 2][0:64, :], AF.Exp,
                                             bias=c_dtb[:]).then_inc(s_msc, 1)
                            s.wait_ge(s_msc, Sc[f] + 2)
                            s.activation(dtf[:], ef[:], AF.Ln,
                                         bias=c_fc[0:64, 2:3]).then_inc(s_msc, 1)

                @block.sync
                def _(sy):
                    for f in range(NF):
                        if f < 32:
                            sy.wait_ge(s_mv, Vc[f + 1])
                            sy.dma_start(out=zsc.ap()[f * 128:(f + 1) * 128, :],
                                         in_=zstb[:, (f % 2) * 1024:(f % 2) * 1024 + 1024]
                                         ).then_inc(s_mz[f % 2], 16)
                        elif f < 80:
                            cf = f - 32
                            sy.wait_ge(s_msc, Sc[f + 1])
                            sy.dma_start(out=xbcsc.ap()[cf * 128:(cf + 1) * 128, :],
                                         in_=xbst[:, (f % 2) * 1024:(f % 2) * 1024 + 1024]
                                         ).then_inc(s_mx[f % 2], 16)

            xctx.close()
            phase_end(sync_wait=[(s_mz[0], 16 * 16), (s_mz[1], 16 * 16),
                                 (s_mx[0], 16 * 24), (s_mx[1], 16 * 24)])
            emit_scan(stage)
            mctx.close()

        if stage >= 3:
            with nc.sbuf_tensor("hid2", [128, KE * 1024], BF16) as hid2:
                with nc.sbuf_tensor("xh3", [128, KD * 1024], BF16) as xh3:
                    emit_ln("l3", xs2.ap(), xh3)
                    phase_end()
                    emit_ffn_u("f2", xh3, hid2)
                emit_ffn_d("f2", hid2, xs2.ap(), out, False, False)

        phase_end()

    return nc


def emit_mamba(*a, **k):
    raise NotImplementedError("stage >= 2 not implemented yet")


def _stage_inputs(inputs):
    u = np.asarray(inputs["L0_attn_u"], dtype=np.float32)
    W_in = np.asarray(inputs["L0_attn_W_in"], dtype=np.float32)
    conv_w = np.asarray(inputs["L0_attn_conv_w"], dtype=np.float32)
    conv_b = np.asarray(inputs["L0_attn_conv_b"], dtype=np.float32)
    dt_bias = np.asarray(inputs["L0_attn_dt_bias"], dtype=np.float32)
    A_log = np.asarray(inputs["L0_attn_A_log"], dtype=np.float32)
    Dv = np.asarray(inputs["L0_attn_D"], dtype=np.float32)
    norm_w = np.asarray(inputs["L0_attn_norm_weight"], dtype=np.float32)
    W_out = np.asarray(inputs["L0_attn_W_out"], dtype=np.float32)
    W_up = np.asarray(inputs["L0_ffn_W_up"], dtype=np.float32)
    W_g = np.asarray(inputs["L0_ffn_W_gate"], dtype=np.float32)
    W_d = np.asarray(inputs["L0_ffn_W_down"], dtype=np.float32)

    WinT = np.ascontiguousarray(W_in.T)
    WinT_pad = np.zeros((D, NF * 128), dtype=np.float32)
    WinT_pad[:, :10304] = WinT
    w_in_t = np.zeros((NF, 16, 128, 128), dtype=np.float32)
    for f in range(NF):
        for k in range(16):
            w_in_t[f, k] = WinT_pad[k * 128:(k + 1) * 128, f * 128:(f + 1) * 128]
    w_in_t = w_in_t.reshape(NF * 16 * 128, 128)

    WoutT = np.ascontiguousarray(W_out.T)
    w_out_t = np.zeros((16, 32, 128, 128), dtype=np.float32)
    for m in range(16):
        for kc in range(32):
            w_out_t[m, kc] = WoutT[kc * 128:(kc + 1) * 128, m * 128:(m + 1) * 128]
    w_out_t = w_out_t.reshape(16 * 32 * 128, 128)

    WupT = np.ascontiguousarray(W_up.T)
    WgT = np.ascontiguousarray(W_g.T)
    WdT = np.ascontiguousarray(W_d.T)

    identb = np.eye(128, dtype=BF)
    identf = np.eye(128, dtype=np.float32)
    trilb = np.triu(np.ones((128, 128), dtype=np.float32)).astype(BF)
    onesf = np.ones((128, 128), dtype=np.float32)
    onesb = np.ones((128, 128), dtype=BF)
    gn_oh = np.zeros((128, 32, 32), dtype=np.float32)
    for g in range(32):
        gn_oh[:, g, g] = 1.0
    gn_oh = gn_oh.reshape(128, 32 * 32).astype(BF)
    d_expv = np.repeat(Dv, 64).reshape(32, 128).T.copy()
    norm_wt = norm_w.reshape(32, 128).T.copy()

    common = dict(
        w_in_t=w_in_t, w_out_t=w_out_t,
        conv_w=conv_w, conv_b=conv_b[:, None].copy(),
        dt_bias=dt_bias[:, None].copy(), a_log=A_log[:, None].copy(),
        d_exp=d_expv, norm_w=norm_wt,
        identb=identb, identf=identf, trilb=trilb,
        onesf=onesf, onesb=onesb, gn_oh=gn_oh,
    )
    in_maps = []
    for c in range(8):
        s, h = c // 2, c % 2
        a = h * 1024
        m = dict(common)
        m["u_fm"] = np.ascontiguousarray(u[s, a:a + 1024, :].T)
        m["wup_sh"] = np.ascontiguousarray(WupT[:, 1024 * c:1024 * (c + 1)])
        m["wg_sh"] = np.ascontiguousarray(WgT[:, 1024 * c:1024 * (c + 1)])
        m["wd_sh"] = np.ascontiguousarray(WdT[:, 256 * c:256 * (c + 1)])
        fc = np.zeros((128, 8), dtype=np.float32)
        fc[:, 0] = 1e-5
        fc[:, 1] = 1.0 / 128.0
        fc[:, 2] = 1.0
        fc[:, 3] = -1.0
        m["fconst"] = fc
        hm = np.zeros((128, 8), dtype=np.float32)
        if h == 1:
            hm[:, c - 1] = 1.0
        m["hmask"] = hm
        in_maps.append(m)
    return in_maps


def kernel(**inputs):
    in_maps = _stage_inputs(inputs)
    nc = build(STAGE)
    res = run_bass_kernel_spmd(nc, in_maps, core_ids=list(range(8)))
    outp = np.empty((4, 2048, 2048), dtype=np.float32)
    for c in range(8):
        s, h = c // 2, c % 2
        a = h * 1024
        outp[s, a:a + 1024, :] = res.results[c]["out"].T
    return outp


# version 07
# speedup vs baseline: 1.3197x; 1.3197x over previous
import contextlib
import numpy as np
import ml_dtypes
import concourse.bass as bass
import concourse.mybir as mybir
from concourse.bass_utils import run_bass_kernel_spmd

F32 = mybir.dt.float32
BF16 = mybir.dt.bfloat16
AF = mybir.ActivationFunctionType
ALU = mybir.AluOpType
AX = mybir.AxisListType
BF = ml_dtypes.bfloat16

D = 2048
NT = 1024
DE = 8192
DI = 4096
CONVD = 6144
NF = 81
NCH = 8
KD = 16
KE = 64
KI = 32
LN_EPS = 1e-5
GN_EPS = 1e-5
TERN_EPS = 1e-8
NW = float(D * DE)

STAGE = 3
LNCUT = 4


def build(stage=STAGE):
    nc = bass.Bass("TRN2", target_bir_lowering=False, debug=False, detect_race_conditions=True)

    u_fm = nc.dram_tensor("u_fm", [D, NT], F32, kind="ExternalInput").ap()
    w_in_t = nc.dram_tensor("w_in_t", [NF * 16 * 128, 128], F32, kind="ExternalInput").ap()
    w_out_t = nc.dram_tensor("w_out_t", [16 * 32 * 128, 128], F32, kind="ExternalInput").ap()
    wup_sh = nc.dram_tensor("wup_sh", [D, 1024], F32, kind="ExternalInput").ap()
    wg_sh = nc.dram_tensor("wg_sh", [D, 1024], F32, kind="ExternalInput").ap()
    wd_sh = nc.dram_tensor("wd_sh", [DE, 256], F32, kind="ExternalInput").ap()
    conv_w = nc.dram_tensor("conv_w", [CONVD, 4], F32, kind="ExternalInput").ap()
    conv_b = nc.dram_tensor("conv_b", [CONVD, 1], F32, kind="ExternalInput").ap()
    dt_bias = nc.dram_tensor("dt_bias", [64, 1], F32, kind="ExternalInput").ap()
    a_log = nc.dram_tensor("a_log", [64, 1], F32, kind="ExternalInput").ap()
    d_exp = nc.dram_tensor("d_exp", [128, 32], F32, kind="ExternalInput").ap()
    norm_w = nc.dram_tensor("norm_w", [128, 32], F32, kind="ExternalInput").ap()
    identb = nc.dram_tensor("identb", [128, 128], BF16, kind="ExternalInput").ap()
    identf = nc.dram_tensor("identf", [128, 128], F32, kind="ExternalInput").ap()
    trilb = nc.dram_tensor("trilb", [128, 128], BF16, kind="ExternalInput").ap()
    onesf_in = nc.dram_tensor("onesf", [128, 128], F32, kind="ExternalInput").ap()
    onesb_in = nc.dram_tensor("onesb", [128, 128], BF16, kind="ExternalInput").ap()
    gn_oh_in = nc.dram_tensor("gn_oh", [128, 32 * 32], BF16, kind="ExternalInput").ap()
    hmask_in = nc.dram_tensor("hmask", [128, 8], F32, kind="ExternalInput").ap()
    fconst_in = nc.dram_tensor("fconst", [128, 8], F32, kind="ExternalInput").ap()
    out = nc.dram_tensor("out", [D, NT], F32, kind="ExternalOutput").ap()
    dbg = nc.dram_tensor("dbg", [128, 16], F32, kind="ExternalOutput").ap()

    ar0_in = nc.dram_tensor("ar0_in", [1, 8], F32)
    ar0_out = nc.dram_tensor("ar0_out", [1, 8], F32)
    ag1_in = nc.dram_tensor("ag1_in", [3 * 16384, 128], BF16)
    ag1_out = nc.dram_tensor("ag1_out", [8 * 3 * 16384, 128], BF16, addr_space="Shared")
    ag2_in = nc.dram_tensor("ag2_in", [D, 4], F32)
    ag2_out = nc.dram_tensor("ag2_out", [8 * D, 4], F32, addr_space="Shared")
    ag3_in = nc.dram_tensor("ag3_in", [128, 4096], BF16)
    ag3_out = nc.dram_tensor("ag3_out", [8 * 128, 4096], BF16, addr_space="Shared")
    xs1 = nc.dram_tensor("xs1", [D, NT], F32)
    xs2 = nc.dram_tensor("xs2", [D, NT], F32)
    zsc = nc.dram_tensor("zsc", [DI, NT], BF16)
    xbcsc = nc.dram_tensor("xbcsc", [CONVD, NT], BF16)

    rg8 = [[0, 1, 2, 3, 4, 5, 6, 7]]

    def ag1_rows(off, n):
        return ag1_out.ap()[off:off + n, :]

    def tup_block(e):
        r, el = e // 8, e % 8
        return ag1_rows(r * 49152 + el * 2048, 2048)

    def tg_block(e):
        r, el = e // 8, e % 8
        return ag1_rows(r * 49152 + 16384 + el * 2048, 2048)

    def td_block(m):
        r, ml = m // 2, m % 2
        return ag1_rows(r * 49152 + 32768 + ml * 8192, 8192)

    def r3(dram_slice, kdim):
        return dram_slice.rearrange("(k p) c -> p k c", p=128)

    def s3(sbuf_ap, kdim):
        return sbuf_ap.rearrange("p (k c) -> p k c", k=kdim)

    ctx = contextlib.ExitStack()
    sems = {}

    def sem(name):
        if name not in sems:
            sems[name] = ctx.enter_context(nc.semaphore(name))
        return sems[name]

    PH = [0]

    with ctx:
        cst = ctx.enter_context(nc.sbuf_tensor("c_identb", [128, 128], BF16))
        c_identb = cst
        c_identf = ctx.enter_context(nc.sbuf_tensor("c_identf", [128, 128], F32))
        c_trilb = ctx.enter_context(nc.sbuf_tensor("c_trilb", [128, 128], BF16))
        c_onesf = ctx.enter_context(nc.sbuf_tensor("c_onesf", [128, 128], F32))
        c_onesb = ctx.enter_context(nc.sbuf_tensor("c_onesb", [128, 128], BF16))
        c_gnoh = ctx.enter_context(nc.sbuf_tensor("c_gnoh", [128, 32 * 32], BF16))
        c_hmask = ctx.enter_context(nc.sbuf_tensor("c_hmask", [128, 8], F32))
        c_convw = ctx.enter_context(nc.sbuf_tensor("c_convw", [128, 48 * 4], F32))
        c_convb = ctx.enter_context(nc.sbuf_tensor("c_convb", [128, 48], F32))
        c_dtb = ctx.enter_context(nc.sbuf_tensor("c_dtb", [64, 1], F32))
        c_av = ctx.enter_context(nc.sbuf_tensor("c_av", [64, 1], F32))
        c_dexp = ctx.enter_context(nc.sbuf_tensor("c_dexp", [128, 32], F32))
        c_normw = ctx.enter_context(nc.sbuf_tensor("c_normw", [128, 32], F32))
        c_diagd = ctx.enter_context(nc.sbuf_tensor("c_diagd", [128, 32 * 128], BF16))
        v_thu = ctx.enter_context(nc.sbuf_tensor("v_thu", [128, 1], F32))
        v_thun = ctx.enter_context(nc.sbuf_tensor("v_thun", [128, 1], F32))
        v_thg = ctx.enter_context(nc.sbuf_tensor("v_thg", [128, 1], F32))
        v_thgn = ctx.enter_context(nc.sbuf_tensor("v_thgn", [128, 1], F32))
        v_thd = ctx.enter_context(nc.sbuf_tensor("v_thd", [128, 1], F32))
        v_thdn = ctx.enter_context(nc.sbuf_tensor("v_thdn", [128, 1], F32))
        v_su = ctx.enter_context(nc.sbuf_tensor("v_su", [128, 1], F32))
        v_sdg = ctx.enter_context(nc.sbuf_tensor("v_sdg", [128, 1], F32))
        s_row = ctx.enter_context(nc.sbuf_tensor("s_row", [1, 16], F32))
        c_fc = ctx.enter_context(nc.sbuf_tensor("c_fc", [128, 8], F32))
        drv = ctx.enter_context(nc.sbuf_tensor("drv", [128, 2], F32))
        drs = ctx.enter_context(nc.sbuf_tensor("drs", [128, 2], F32))
        drg = ctx.enter_context(nc.sbuf_tensor("drg", [128, 2], F32))
        block = ctx.enter_context(nc.Block())

        consts = dict(identb=c_identb, identf=c_identf, trilb=c_trilb, onesf=c_onesf,
                      onesb=c_onesb, gnoh=c_gnoh, hmask=c_hmask, convw=c_convw,
                      convb=c_convb, dtb=c_dtb, av=c_av, dexp=c_dexp, normw=c_normw,
                      diagd=c_diagd)

        ph_sems = {e: sem("ph_" + e) for e in ("sy", "g", "t", "v", "s")}

        def phase_end(sync_wait=None, gp_wait=None, scalar_wait=None):
            tgt = PH[0] + 1

            def _aslist(x):
                if x is None:
                    return []
                if isinstance(x, list):
                    return x
                return [x]

            def wait_all(eng):
                for e in ("sy", "g", "t", "v", "s"):
                    eng.wait_ge(ph_sems[e], tgt)

            @block.sync
            def _(sy):
                for sw in _aslist(sync_wait):
                    sy.wait_ge(sw[0], sw[1])
                sy.sem_inc(ph_sems["sy"], 1)
                wait_all(sy)

            @block.gpsimd
            def _(g):
                for sw in _aslist(gp_wait):
                    g.wait_ge(sw[0], sw[1])
                g.tensor_copy(drg[:, 1:2], drg[:, 0:1]).then_inc(ph_sems["g"], 1)
                wait_all(g)

            @block.tensor
            def _(t):
                t.ldweights(c_onesb[:, 0:128]).then_inc(ph_sems["t"], 1)
                wait_all(t)

            @block.vector
            def _(v):
                v.tensor_copy(drv[:, 1:2], drv[:, 0:1]).then_inc(ph_sems["v"], 1)
                wait_all(v)

            @block.scalar
            def _(s):
                for sw in _aslist(scalar_wait):
                    s.wait_ge(sw[0], sw[1])
                s.copy(drs[:, 1:2], drs[:, 0:1]).then_inc(ph_sems["s"], 1)
                wait_all(s)

            PH[0] = tgt

        s_cld = sem("cld")
        s_e0s = sem("e0s")
        s_e0v = sem("e0v")
        s_e0g = sem("e0g")

        NCLD = [0]

        @block.sync
        def _(sy):
            for dst, srcc in ((c_identb, identb), (c_identf, identf),
                              (c_trilb, trilb), (c_onesf, onesf_in),
                              (c_onesb, onesb_in), (c_gnoh, gn_oh_in),
                              (c_hmask, hmask_in), (c_fc, fconst_in), (c_dtb, dt_bias),
                              (c_av, a_log), (c_dexp, d_exp), (c_normw, norm_w)):
                sy.dma_start(out=dst[:], in_=srcc).then_inc(s_cld, 16)
                NCLD[0] += 1
            for cf in range(48):
                sy.dma_start(out=c_convw[:, cf * 4:cf * 4 + 4],
                             in_=conv_w[cf * 128:cf * 128 + 128, :]).then_inc(s_cld, 16)
                sy.dma_start(out=c_convb[:, cf:cf + 1],
                             in_=conv_b[cf * 128:cf * 128 + 128, :]).then_inc(s_cld, 16)
                NCLD[0] += 2

        @block.scalar
        def _(s):
            s.memzero(drs[:])
            s.wait_ge(s_cld, 16 * NCLD[0])
            s.activation(c_av[:], c_av[:], AF.Exp).then_inc(s_e0s, 1)
            s.wait_ge(s_e0s, 1)
            s.mul(c_av[:], c_av[:], -1.0).then_inc(s_e0s, 1)

        @block.vector
        def _(v):
            v.memset(s_row[:], 0.0)
            v.memset(drv[:], 0.0)
            v.wait_ge(s_cld, 16 * NCLD[0])
            s_dg0 = sem("dg0")
            for j in range(32):
                v.tensor_scalar(c_diagd[:, j * 128:(j + 1) * 128], c_onesb[:],
                                c_dexp[:, j:j + 1], None, ALU.mult).then_inc(s_dg0, 1)
            v.wait_ge(s_dg0, 32)
            v.tensor_copy(drv[:, 1:2], drv[:, 0:1]).then_inc(s_e0v, 1)

        @block.gpsimd
        def _(g):
            g.memset(drg[:], 0.0)
            g.wait_ge(s_e0v, 1)
            for j in range(32):
                g.affine_select(c_diagd[:, j * 128:(j + 1) * 128],
                                c_diagd[:, j * 128:(j + 1) * 128],
                                pattern=[[1, 128]], base=0, channel_multiplier=-1,
                                compare_op=ALU.is_equal, fill=0.0).then_inc(s_e0g, 1)

        shard_plan = [("u", wup_sh, 16, 1024), ("g", wg_sh, 16, 1024),
                      ("d", wd_sh, 64, 256)]
        s_wl = (sem("wl0"), sem("wl1"))
        s_ar = sem("ar")
        s_wred = sem("wred")
        s_p0t = sem("p0t")
        s_p0v = sem("p0v")
        s_ag = sem("ag")
        s_tv = sem("tv")
        s_ts = (sem("ts0"), sem("ts1"))
        win_per = []
        for nm, dram, nl, ncols in shard_plan:
            win_per += [4096 // ncols] * 4
        win_per = win_per * 2
        wcum = [[0] * 24, [0] * 24]
        c0 = c1 = 0
        for w, per in enumerate(win_per):
            if w % 2 == 0:
                c0 += per
            else:
                c1 += per
            wcum[0][w] = c0
            wcum[1][w] = c1

        with (
            nc.sbuf_tensor("tw_win", [128, 2 * 4096], F32) as tw_win,
            nc.sbuf_tensor("tw_acc", [128, 12], F32) as tw_acc,
            nc.sbuf_tensor("tw_red", [128, 3], F32) as tw_red,
            nc.psum_tensor("tw_ps", [1, 4], F32) as tw_ps,
            nc.psum_tensor("tw_pb", [128, 16], F32) as tw_pb,
        ):
            @block.sync
            def _(sy):
                w = 0
                for nm, dram, nl, ncols in shard_plan:
                    per = 4096 // ncols
                    for w0 in range(0, nl, per):
                        if w >= 2:
                            sy.wait_ge(s_wred, w - 1)
                        buf = tw_win[:, (w % 2) * 4096:(w % 2) * 4096 + 4096]
                        for i in range(per):
                            k = w0 + i
                            sy.dma_start(out=buf[:, i * ncols:(i + 1) * ncols],
                                         in_=dram[k * 128:(k + 1) * 128, :]).then_inc(s_wl[w % 2], 16)
                        w += 1

            @block.vector
            def _(v):
                w = 0
                for nm, dram, nl, ncols in shard_plan:
                    per = 4096 // ncols
                    for wi in range(4):
                        v.wait_ge(s_wl[w % 2], 16 * wcum[w % 2][w])
                        buf = tw_win[:, (w % 2) * 4096:(w % 2) * 4096 + 4096]
                        v.tensor_reduce(tw_acc[:, w:w + 1], buf[:], AX.X, ALU.add,
                                        apply_absolute_value=True).then_inc(s_wred, 1)
                        w += 1
                v.wait_ge(s_wred, 12)
                for mi in range(3):
                    v.tensor_reduce(tw_red[:, mi:mi + 1], tw_acc[:, mi * 4:mi * 4 + 4],
                                    AX.X, ALU.add).then_inc(s_wred, 1)

            @block.tensor
            def _(t):
                t.wait_ge(s_wred, 15)
                t.matmul(tw_ps[0:1, 0:3], c_onesf[:, 0:1], tw_red[:, 0:3],
                         start=True, stop=True).then_inc(s_p0t, 1)

            @block.vector
            def _(v):
                v.wait_ge(s_p0t, 1)
                s_ms = sem("ms")
                v.memset(s_row[:, 4:12], 0.0).then_inc(s_ms, 1)
                v.wait_ge(s_ms, 1)
                v.tensor_copy(s_row[:, 4:7], tw_ps[0:1, 0:3]).then_inc(s_p0v, 1)

            @block.sync
            def _(sy):
                sy.wait_ge(s_p0v, 1)
                sy.dma_start(out=ar0_in.ap()[0:1, 0:8], in_=s_row[:, 4:12]).then_inc(s_ar, 16)

            @block.gpsimd
            def _(g):
                g.wait_ge(s_ar, 16)
                g.collective_compute("AllReduce", ALU.add, ins=[ar0_in.ap().opt()],
                                     outs=[ar0_out.ap().opt()],
                                     replica_groups=rg8).then_inc(s_ag, 1)

            @block.sync
            def _(sy):
                sy.wait_ge(s_ag, 1)
                sy.dma_start(out=s_row[:, 8:11], in_=ar0_out.ap()[0:1, 0:3]).then_inc(s_ar, 16)

            s_chn = sem("chn")

            @block.vector
            def _(v):
                v.wait_ge(s_ar, 32)
                v.tensor_scalar(s_row[:, 8:11], s_row[:, 8:11], 1.0 / NW, TERN_EPS,
                                ALU.mult, ALU.add).then_inc(s_chn, 1)
                v.wait_ge(s_chn, 1)
                v.tensor_scalar(s_row[:, 11:14], s_row[:, 8:11], 0.5, None, ALU.mult)
                v.tensor_tensor(s_row[:, 14:15], s_row[:, 9:10], s_row[:, 10:11],
                                ALU.mult).then_inc(s_chn, 1)
                v.wait_ge(s_chn, 2)
                v.tensor_scalar(s_row[:, 14:15], s_row[:, 14:15], 0.5, None,
                                ALU.mult).then_inc(s_chn, 1)
                v.wait_ge(s_chn, 3)
                v.tensor_copy(drv[:, 1:2], drv[:, 0:1]).then_inc(s_p0v, 1)

            @block.tensor
            def _(t):
                t.wait_ge(s_p0v, 2)
                cols5 = (11, 12, 13, 8, 14)
                for i, col in enumerate(cols5):
                    mm = t.matmul(tw_pb[:, i:i + 1], c_onesf[0:1, 0:128],
                                  s_row[:, col:col + 1], start=True, stop=True)
                    if i == len(cols5) - 1:
                        mm.then_inc(s_p0t, 1)

            @block.vector
            def _(v):
                v.wait_ge(s_p0t, 2)
                s_thv = sem("thv")
                for i, dst in enumerate((v_thu, v_thg, v_thd, v_su, v_sdg)):
                    v.tensor_copy(dst[:], tw_pb[:, i:i + 1]).then_inc(s_thv, 1)
                for i, dst in enumerate((v_thun, v_thgn, v_thdn)):
                    v.tensor_scalar(dst[:], tw_pb[:, i:i + 1], -1.0, None,
                                    ALU.mult).then_inc(s_thv, 1)
                v.wait_ge(s_thv, 8)
                v.tensor_copy(drv[:, 1:2], drv[:, 0:1]).then_inc(s_p0v, 1)

            with (
                nc.sbuf_tensor("tn_a", [128, 4096], F32) as tn_a,
                nc.sbuf_tensor("tn_t", [128, 2 * 4096], BF16) as tn_t,
            ):
                thvec = {"u": (v_thu, v_thun), "g": (v_thg, v_thgn), "d": (v_thd, v_thdn)}

                @block.sync
                def _(sy):
                    sy.wait_ge(s_wred, 15)
                    w = 12
                    for nm, dram, nl, ncols in shard_plan:
                        per = 4096 // ncols
                        for w0 in range(0, nl, per):
                            wt = w - 12
                            if wt >= 2:
                                sy.wait_ge(s_tv, 2 * (wt - 1))
                            buf = tw_win[:, (w % 2) * 4096:(w % 2) * 4096 + 4096]
                            for i in range(per):
                                k = w0 + i
                                sy.dma_start(out=buf[:, i * ncols:(i + 1) * ncols],
                                             in_=dram[k * 128:(k + 1) * 128, :]).then_inc(s_wl[w % 2], 16)
                            w += 1

                @block.vector
                def _(v):
                    v.wait_ge(s_p0v, 3)
                    w = 12
                    for nm, dram, nl, ncols in shard_plan:
                        per = 4096 // ncols
                        thp, thn = thvec[nm]
                        for wi in range(4):
                            wt = w - 12
                            v.wait_ge(s_wl[w % 2], 16 * wcum[w % 2][w])
                            if wt >= 2:
                                v.wait_ge(s_ts[wt % 2], 16 * 32 * ((wt - 2) // 2 + 1))
                            buf = tw_win[:, (w % 2) * 4096:(w % 2) * 4096 + 4096]
                            tb = tn_t[:, (w % 2) * 4096:(w % 2) * 4096 + 4096]
                            if wt >= 1:
                                v.wait_ge(s_tv, 2 * wt)
                            v.tensor_scalar(tn_a[:], buf[:], thn[:], None,
                                            ALU.is_lt).then_inc(s_tv, 1)
                            v.wait_ge(s_tv, 2 * wt + 1)
                            v.scalar_tensor_tensor(tb[:], buf[:], thp[:], tn_a[:],
                                                   ALU.is_gt, ALU.subtract).then_inc(s_tv, 1)
                            w += 1

                @block.scalar
                def _(s):
                    w = 0
                    for nm, dram, nl, ncols in shard_plan:
                        per = 4096 // ncols
                        for wi in range(4):
                            s.wait_ge(s_tv, 2 * (w + 1))
                            tb = tn_t[:, (w % 2) * 4096:(w % 2) * 4096 + 4096]
                            for i in range(per):
                                k = wi * per + i
                                for el in range(ncols // 128):
                                    if nm == "u":
                                        roff = el * 2048 + k * 128
                                    elif nm == "g":
                                        roff = 16384 + el * 2048 + k * 128
                                    else:
                                        roff = 32768 + el * 8192 + k * 128
                                    s.dma_start(out=ag1_in.ap()[roff:roff + 128, :],
                                                in_=tb[:, i * ncols + el * 128:i * ncols + el * 128 + 128]
                                                ).then_inc(s_ts[w % 2], 16)
                            w += 1

                @block.gpsimd
                def _(g):
                    g.wait_ge(s_ts[0], 16 * 32 * 6)
                    g.wait_ge(s_ts[1], 16 * 32 * 6)
                    g.collective_compute("AllGather", ALU.bypass, ins=[ag1_in.ap().opt()],
                                         outs=[ag1_out.ap().opt()],
                                         replica_groups=rg8).then_inc(s_ag, 1)

        phase_end(scalar_wait=[(s_ts[0], 16 * 32 * 6), (s_ts[1], 16 * 32 * 6)])

        def emit_dummy_out():
            s_do = sem("dout")

            @block.sync
            def _(sy):
                for k in range(KD):
                    sy.dma_start(out=out[k * 128:(k + 1) * 128, :],
                                 in_=u_fm[k * 128:(k + 1) * 128, :]).then_inc(s_do, 16)
                sy.wait_ge(s_do, 16 * KD)
            phase_end()

        if stage < 0.4:
            emit_dummy_out()
            return nc

        SEMB = {}

        def takebase(name, final):
            b = SEMB.get(name, 0)
            SEMB[name] = b + final
            return b

        def emit_ln(tag, src_dram, xh, ncols=1024, colstride=None, dst_off=0):
            if colstride is None:
                colstride = ncols
            s_lt = sem("lnlt")
            s_lv = sem("lnlv")
            s_ls = sem("lnls")
            s_lw = (sem("lnlw0"), sem("lnlw1"))
            blt = takebase("lnlt", KD + 1)
            blv = takebase("lnlv", 6 * KD + 3)
            bls = takebase("lnls", 1)
            blw = (takebase("lnlw0", 16 * 16), takebase("lnlw1", 16 * 16))
            nsp = 2
            with (
                nc.sbuf_tensor(tag + "win", [128, 2 * 1024], F32) as win,
                nc.sbuf_tensor(tag + "wb", [128, 2 * 1024], BF16) as winb,
                nc.sbuf_tensor(tag + "sq", [128, 2 * 1024], BF16) as sqb,
                nc.sbuf_tensor(tag + "mr", [1, 2 * ncols], F32) as mrow,
                nc.sbuf_tensor(tag + "dd", [128, 1024], F32) as dbuf,
                nc.psum_tensor(tag + "psxA", [1, 512], F32) as psxA,
                nc.psum_tensor(tag + "psxB", [1, 512], F32) as psxB,
                nc.psum_tensor(tag + "psqA", [1, 512], F32) as psqA,
                nc.psum_tensor(tag + "psqB", [1, 512], F32) as psqB,
                nc.psum_tensor(tag + "pmuA", [128, 512], F32) as pmuA,
                nc.psum_tensor(tag + "pmuB", [128, 512], F32) as pmuB,
                nc.psum_tensor(tag + "prA", [128, 512], F32) as prA,
                nc.psum_tensor(tag + "prB", [128, 512], F32) as prB,
            ):
                psx = (psxA, psxB)
                psq = (psqA, psqB)
                pmu = (pmuA, pmuB)
                pr = (prA, prB)
                splits = [(0, 512), (512, 1024)]

                @block.sync
                def _(sy):
                    for k in range(KD):
                        if k >= 2:
                            sy.wait_ge(s_lt, blt + k - 1)
                        sy.dma_start(out=win[:, (k % 2) * 1024:(k % 2) * 1024 + ncols],
                                     in_=src_dram[k * 128:(k + 1) * 128, :]).then_inc(s_lw[k % 2], 16)

                @block.vector
                def _(v):
                    for k in range(KD):
                        v.wait_ge(s_lw[k % 2], blw[k % 2] + 16 * (k // 2 + 1))
                        if k >= 2:
                            v.wait_ge(s_lt, blt + k - 1)
                        v.tensor_copy(winb[:, (k % 2) * 1024:(k % 2) * 1024 + ncols],
                                      win[:, (k % 2) * 1024:(k % 2) * 1024 + ncols]).then_inc(s_lv, 1)
                        v.tensor_tensor(sqb[:, (k % 2) * 1024:(k % 2) * 1024 + ncols],
                                        win[:, (k % 2) * 1024:(k % 2) * 1024 + ncols],
                                        win[:, (k % 2) * 1024:(k % 2) * 1024 + ncols],
                                        ALU.mult).then_inc(s_lv, 1)

                @block.tensor
                def _(t):
                    for k in range(KD):
                        t.wait_ge(s_lv, blv + 2 * (k + 1))
                        for sp, (lo, hi) in enumerate(splits):
                            t.matmul(psx[sp][0:1, :], c_onesb[:, 0:1],
                                     winb[:, (k % 2) * 1024 + lo:(k % 2) * 1024 + hi],
                                     start=(k == 0), stop=(k == KD - 1))
                            mm = t.matmul(psq[sp][0:1, :], c_onesb[:, 0:1],
                                          sqb[:, (k % 2) * 1024 + lo:(k % 2) * 1024 + hi],
                                          start=(k == 0), stop=(k == KD - 1))
                            if sp == len(splits) - 1:
                                mm.then_inc(s_lt, 1)

                if LNCUT < 2:
                    return

                s_lx = sem("lnlx")
                blx = takebase("lnlx", 4)

                @block.vector
                def _(v):
                    v.wait_ge(s_lt, blt + KD)
                    for sp, (lo, hi) in enumerate(splits):
                        v.tensor_scalar(mrow[:, lo:hi], psx[sp][0:1, :], 1.0 / D,
                                        None, ALU.mult).then_inc(s_lx, 1)
                    v.wait_ge(s_lx, blx + 2)
                    for sp, (lo, hi) in enumerate(splits):
                        v.tensor_tensor(mrow[:, ncols + lo:ncols + hi], mrow[:, lo:hi],
                                        mrow[:, lo:hi], ALU.mult).then_inc(s_lx, 1)
                    v.wait_ge(s_lx, blx + 4)
                    for sp, (lo, hi) in enumerate(splits):
                        v.scalar_tensor_tensor(mrow[:, ncols + lo:ncols + hi], psq[sp][0:1, :],
                                               1.0 / D, mrow[:, ncols + lo:ncols + hi],
                                               ALU.mult, ALU.subtract).then_inc(s_lv, 1)

                @block.scalar
                def _(s):
                    s.wait_ge(s_lv, blv + 2 * KD + 2)
                    s.activation(mrow[:, ncols:2 * ncols], mrow[:, ncols:2 * ncols],
                                 AF.Sqrt, bias=c_fc[0:1, 0:1]).then_inc(s_ls, 1)

                @block.vector
                def _(v):
                    v.wait_ge(s_ls, bls + 1)
                    v.reciprocal(mrow[:, ncols:2 * ncols],
                                 mrow[:, ncols:2 * ncols]).then_inc(s_lv, 1)

                if LNCUT < 3:
                    return

                @block.tensor
                def _(t):
                    t.wait_ge(s_lv, blv + 2 * KD + 3)
                    for sp, (lo, hi) in enumerate(splits):
                        t.matmul(pmu[sp][:, :], c_onesf[0:1, 0:128],
                                 mrow[:, lo:hi], start=True, stop=True)
                        mm = t.matmul(pr[sp][:, :], c_onesf[0:1, 0:128],
                                      mrow[:, ncols + lo:ncols + hi], start=True, stop=True)
                        if sp == len(splits) - 1:
                            mm.then_inc(s_lt, 1)

                if LNCUT < 4:
                    return

                @block.sync
                def _(sy):
                    for k in range(KD):
                        if k == 0:
                            sy.wait_ge(s_lt, blt + KD)
                        if k >= 2:
                            sy.wait_ge(s_lv, blv + 2 * KD + 3 + 4 * (k - 1))
                        sy.dma_start(out=win[:, (k % 2) * 1024:(k % 2) * 1024 + ncols],
                                     in_=src_dram[k * 128:(k + 1) * 128, :]).then_inc(s_lw[k % 2], 16)

                @block.vector
                def _(v):
                    v.wait_ge(s_lt, blt + KD + 1)
                    base = blv + 2 * KD + 3
                    for k in range(KD):
                        v.wait_ge(s_lw[k % 2], blw[k % 2] + 16 * (KD // 2 + k // 2 + 1))
                        if k >= 1:
                            v.wait_ge(s_lv, base + 4 * k)
                        for sp, (lo, hi) in enumerate(splits):
                            v.tensor_tensor(dbuf[:, sp * 512:sp * 512 + hi - lo],
                                            win[:, (k % 2) * 1024 + lo:(k % 2) * 1024 + hi],
                                            pmu[sp][:, :], ALU.subtract).then_inc(s_lv, 1)
                        v.wait_ge(s_lv, base + 4 * k + 2)
                        for sp, (lo, hi) in enumerate(splits):
                            v.tensor_tensor(xh[:, k * colstride + dst_off + lo:k * colstride + dst_off + hi],
                                            dbuf[:, sp * 512:sp * 512 + hi - lo],
                                            pr[sp][:, :], ALU.mult).then_inc(s_lv, 1)

        def emit_ffn_u(tag, xh, hid):
            s_fw = (sem("fuw0"), sem("fuw1"))
            s_ft = sem("fut")
            s_fs = sem("fus")
            s_fv = sem("fuv")
            bfw = (takebase("fuw0", 16 * 64), takebase("fuw1", 16 * 64))
            bft = takebase("fut", 4 * KE)
            bfs = takebase("fus", 2 * KE)
            bfv = takebase("fuv", 2 * KE)
            with (
                nc.sbuf_tensor(tag + "tu", [128, 2 * 2048], BF16) as tub,
                nc.sbuf_tensor(tag + "tg", [128, 2 * 2048], BF16) as tgb,
                nc.sbuf_tensor(tag + "sil", [128, 2048], F32) as sil,
                nc.psum_tensor(tag + "pu0a", [128, 512], F32) as pu0a,
                nc.psum_tensor(tag + "pu0b", [128, 512], F32) as pu0b,
                nc.psum_tensor(tag + "pu1a", [128, 512], F32) as pu1a,
                nc.psum_tensor(tag + "pu1b", [128, 512], F32) as pu1b,
                nc.psum_tensor(tag + "pg0a", [128, 512], F32) as pg0a,
                nc.psum_tensor(tag + "pg0b", [128, 512], F32) as pg0b,
                nc.psum_tensor(tag + "pg1a", [128, 512], F32) as pg1a,
                nc.psum_tensor(tag + "pg1b", [128, 512], F32) as pg1b,
            ):
                pu = ((pu0a, pu0b), (pu1a, pu1b))
                pg = ((pg0a, pg0b), (pg1a, pg1b))

                @block.sync
                def _(sy):
                    sy.wait_ge(s_ag, 2)
                    for e in range(KE):
                        if e >= 2:
                            sy.wait_ge(s_ft, bft + 4 * (e - 1))
                        sy.dma_start(out=s3(tub[:, (e % 2) * 2048:(e % 2) * 2048 + 2048], 16),
                                     in_=r3(tup_block(e), 16)).then_inc(s_fw[e % 2], 16)
                        sy.dma_start(out=s3(tgb[:, (e % 2) * 2048:(e % 2) * 2048 + 2048], 16),
                                     in_=r3(tg_block(e), 16)).then_inc(s_fw[e % 2], 16)

                @block.tensor
                def _(t):
                    for e in range(KE):
                        t.wait_ge(s_fw[e % 2], bfw[e % 2] + 32 * (e // 2 + 1))
                        if e >= 2:
                            t.wait_ge(s_fs, bfs + 2 * (e - 1))
                            t.wait_ge(s_fv, bfv + 2 * (e - 1))
                        tu = tub[:, (e % 2) * 2048:(e % 2) * 2048 + 2048]
                        tg = tgb[:, (e % 2) * 2048:(e % 2) * 2048 + 2048]
                        for sp in range(2):
                            for k in range(KD):
                                mm = t.matmul(pu[e % 2][sp][:, :],
                                              tu[:, k * 128:(k + 1) * 128],
                                              xh[:, k * 1024 + sp * 512:k * 1024 + sp * 512 + 512],
                                              start=(k == 0), stop=(k == KD - 1))
                            mm.then_inc(s_ft, 1)
                        for sp in range(2):
                            for k in range(KD):
                                mm = t.matmul(pg[e % 2][sp][:, :],
                                              tg[:, k * 128:(k + 1) * 128],
                                              xh[:, k * 1024 + sp * 512:k * 1024 + sp * 512 + 512],
                                              start=(k == 0), stop=(k == KD - 1))
                            mm.then_inc(s_ft, 1)

                @block.scalar
                def _(s):
                    for e in range(KE):
                        if e >= 2:
                            s.wait_ge(s_fv, bfv + 2 * (e - 1))
                        for sp in range(2):
                            s.wait_ge(s_ft, bft + 4 * e + sp + 1)
                            s.activation(sil[:, (e % 2) * 1024 + sp * 512:(e % 2) * 1024 + sp * 512 + 512],
                                         pu[e % 2][sp][:, :],
                                         AF.Silu, scale=v_su[:]).then_inc(s_fs, 1)

                @block.vector
                def _(v):
                    for e in range(KE):
                        for sp in range(2):
                            v.wait_ge(s_fs, bfs + 2 * e + sp + 1)
                            v.wait_ge(s_ft, bft + 4 * e + 2 + sp + 1)
                            v.tensor_tensor(hid[:, e * 1024 + sp * 512:e * 1024 + sp * 512 + 512],
                                            sil[:, (e % 2) * 1024 + sp * 512:(e % 2) * 1024 + sp * 512 + 512],
                                            pg[e % 2][sp][:, :],
                                            ALU.mult).then_inc(s_fv, 1)

                phase_end()

        def emit_ffn_d(tag, hid, resid_dram, dst_dram, also_out, save_tail):
            s_dw = (sem("fdw0"), sem("fdw1"))
            s_du = (sem("fdu0"), sem("fdu1"))
            s_dt = sem("fdt")
            s_dv = sem("fdv")
            s_dst = (sem("fdst0"), sem("fdst1"))
            s_done = sem("fddone")
            per_m = 1 + (1 if also_out else 0) + (1 if save_tail else 0)
            stc = [[0] * 16, [0] * 16]
            c0 = c1 = 0
            for m in range(16):
                if m % 2 == 0:
                    c0 += per_m
                else:
                    c1 += per_m
                stc[0][m] = c0
                stc[1][m] = c1
            bdw = (takebase("fdw0", 16 * 8), takebase("fdw1", 16 * 8))
            bdu = (takebase("fdu0", 16 * 8), takebase("fdu1", 16 * 8))
            bdt = takebase("fdt", 32)
            bdv = takebase("fdv", 32)
            bdst = (takebase("fdst0", 16 * stc[0][-1]), takebase("fdst1", 16 * stc[1][-1]))
            bdone = takebase("fddone", 1)
            with (
                nc.sbuf_tensor(tag + "td", [128, 2 * 8192], BF16) as tdb,
                nc.sbuf_tensor(tag + "ub", [128, 2 * 1024], F32) as ub,
                nc.sbuf_tensor(tag + "x1b", [128, 2 * 1024], F32) as x1b,
                nc.psum_tensor(tag + "pd0a", [128, 512], F32) as pd0a,
                nc.psum_tensor(tag + "pd0b", [128, 512], F32) as pd0b,
                nc.psum_tensor(tag + "pd1a", [128, 512], F32) as pd1a,
                nc.psum_tensor(tag + "pd1b", [128, 512], F32) as pd1b,
            ):
                pd = ((pd0a, pd0b), (pd1a, pd1b))

                @block.sync
                def _(sy):
                    for m in range(16):
                        if m >= 2:
                            sy.wait_ge(s_dt, bdt + 2 * (m - 1))
                        sy.dma_start(out=s3(tdb[:, (m % 2) * 8192:(m % 2) * 8192 + 8192], 64),
                                     in_=r3(td_block(m), 64)).then_inc(s_dw[m % 2], 16)
                        if m >= 2:
                            sy.wait_ge(s_dv, bdv + 2 * (m - 1))
                        sy.dma_start(out=ub[:, (m % 2) * 1024:(m % 2) * 1024 + 1024],
                                     in_=resid_dram[m * 128:(m + 1) * 128, :]).then_inc(s_du[m % 2], 16)

                @block.tensor
                def _(t):
                    for m in range(16):
                        t.wait_ge(s_dw[m % 2], bdw[m % 2] + 16 * (m // 2 + 1))
                        if m >= 2:
                            t.wait_ge(s_dv, bdv + 2 * (m - 1))
                        td = tdb[:, (m % 2) * 8192:(m % 2) * 8192 + 8192]
                        for sp in range(2):
                            for e in range(KE):
                                mm = t.matmul(pd[m % 2][sp][:, :],
                                              td[:, e * 128:(e + 1) * 128],
                                              hid[:, e * 1024 + sp * 512:e * 1024 + sp * 512 + 512],
                                              start=(e == 0), stop=(e == KE - 1))
                            mm.then_inc(s_dt, 1)

                @block.vector
                def _(v):
                    for m in range(16):
                        v.wait_ge(s_du[m % 2], bdu[m % 2] + 16 * (m // 2 + 1))
                        if m >= 2:
                            v.wait_ge(s_dst[m % 2], bdst[m % 2] + 16 * stc[m % 2][m - 2])
                        for sp in range(2):
                            v.wait_ge(s_dt, bdt + 2 * m + sp + 1)
                            v.scalar_tensor_tensor(
                                x1b[:, (m % 2) * 1024 + sp * 512:(m % 2) * 1024 + sp * 512 + 512],
                                pd[m % 2][sp][:, :],
                                v_sdg[:],
                                ub[:, (m % 2) * 1024 + sp * 512:(m % 2) * 1024 + sp * 512 + 512],
                                ALU.mult, ALU.add).then_inc(s_dv, 1)

                @block.scalar
                def _(s):
                    for m in range(16):
                        s.wait_ge(s_dv, bdv + 2 * (m + 1))
                        xb = x1b[:, (m % 2) * 1024:(m % 2) * 1024 + 1024]
                        s.dma_start(out=dst_dram[m * 128:(m + 1) * 128, :],
                                    in_=xb).then_inc(s_dst[m % 2], 16)
                        if also_out:
                            s.dma_start(out=out[m * 128:(m + 1) * 128, :],
                                        in_=xb).then_inc(s_dst[m % 2], 16)
                        if save_tail:
                            s.dma_start(out=ag2_in.ap()[m * 128:(m + 1) * 128, 0:4],
                                        in_=xb[:, 1020:1024]).then_inc(s_dst[m % 2], 16)
                    s.wait_ge(s_dst[0], bdst[0] + 16 * stc[0][-1])
                    s.wait_ge(s_dst[1], bdst[1] + 16 * stc[1][-1])
                    s.sem_inc(s_done, 1)
            return (s_done, bdone + 1)

        with nc.sbuf_tensor("hid1", [128, KE * 1024], BF16) as hid1:
            with nc.sbuf_tensor("xh1", [128, KD * 1024], BF16) as xh1:
                emit_ln("l1", u_fm, xh1)
                phase_end()
                if stage < 0.7:
                    emit_dummy_out()
                    return nc
                emit_ffn_u("f1", xh1, hid1)
                if stage < 0.9:
                    emit_dummy_out()
                    return nc
            f1done, f1cnt = emit_ffn_d("f1", hid1, u_fm, xs1, stage == 1, True)

        @block.gpsimd
        def _(g):
            g.wait_ge(f1done, f1cnt)
            g.collective_compute("AllGather", ALU.bypass, ins=[ag2_in.ap().opt()],
                                 outs=[ag2_out.ap().opt()],
                                 replica_groups=rg8).then_inc(s_ag, 1)

        phase_end()

        s_dbg = sem("dbg")

        @block.sync
        def _(sy):
            sy.dma_start(out=dbg[0:1, :], in_=s_row[:, :]).then_inc(s_dbg, 16)
            sy.dma_start(out=dbg[1:2, 0:1], in_=v_su[0:1, :]).then_inc(s_dbg, 16)
            sy.dma_start(out=dbg[2:3, 0:1], in_=v_sdg[0:1, :]).then_inc(s_dbg, 16)
            sy.dma_start(out=dbg[3:4, 0:1], in_=v_thu[0:1, :]).then_inc(s_dbg, 16)
            sy.wait_ge(s_dbg, 64)

        def emit_scan(stage):
            import collections
            CNT = collections.defaultdict(int)

            def E(instr, s, key, n=1):
                CNT[key] += n
                instr.then_inc(s, n)
                return CNT[key]

            s2v = sem("s2v")
            s2s = sem("s2s")
            s2t = sem("s2t")
            s2l = sem("s2l")

            @block.vector
            def _(v):
                E(v.tensor_scalar(dAf[:], dtf[:], c_av[:], None, ALU.mult), s2v, "2v")
                v.wait_ge(s2v, CNT["2v"])
                for c in range(NCH):
                    E(v.tensor_tensor_scan(dAcs[:, 128 * c:128 * c + 128],
                                           dAf[:, 128 * c:128 * c + 128],
                                           dAf[:, 128 * c:128 * c + 128],
                                           0.0, ALU.add, ALU.bypass), s2v, "2v")
                v.wait_ge(s2v, CNT["2v"])
                dv3 = dAcs[:].rearrange("h (c l) -> h c l", c=NCH)
                E(v.tensor_copy(lasts[:, 0:8].rearrange("h (c o) -> h c o", o=1),
                                dv3[:, :, 127:128]), s2v, "2v")
                v.wait_ge(s2v, CNT["2v"])
                E(v.tensor_tensor_scan(lasts[:, 8:16], lasts[:, 0:8], lasts[:, 0:8],
                                       0.0, ALU.add, ALU.bypass), s2v, "2v")
                v.wait_ge(s2v, CNT["2v"])
                E(v.tensor_scalar(lasts[:, 16:24], lasts[:, 8:16], -1.0,
                                  lasts[:, 15:16], ALU.mult, ALU.add), s2v, "2v")
            V_DACS = 9
            V_TAIL = CNT["2v"]

            @block.scalar
            def _(s):
                s.wait_ge(s2v, V_DACS)
                for c in range(NCH):
                    E(s.activation(decayf[:, 128 * c:128 * c + 128],
                                   dAcs[:, 128 * c:128 * c + 128], AF.Exp,
                                   bias=dAcs[:, 128 * c + 127:128 * c + 128],
                                   scale=c_fc[0:64, 3:4]), s2s, "2s")
                s.wait_ge(s2v, V_TAIL)
                E(s.activation(tpf[:], lasts[:, 16:24], AF.Exp), s2s, "2s")
                E(s.activation(cdf[:], lasts[:, 0:8], AF.Exp), s2s, "2s")
            S_DEC = 8
            S_CD = CNT["2s"]

            @block.vector
            def _(v):
                v.wait_ge(s2s, S_DEC)
                E(v.tensor_tensor(dtdyf[:], dtf[:], decayf[:], ALU.mult), s2v, "2v")
                v.wait_ge(s2v, CNT["2v"])
                v.wait_ge(s2s, S_DEC + 2)
                for c in range(NCH):
                    E(v.tensor_scalar(dtdtf[:, 128 * c:128 * c + 128],
                                      dtdyf[:, 128 * c:128 * c + 128],
                                      tpf[:, c:c + 1], None, ALU.mult), s2v, "2v")
            V_DTDT = CNT["2v"]

            @block.sync
            def _(sy):
                sy.wait_ge(s2s, S_CD)
                sy.dma_start(out=cdrow[:], in_=cdf[:]).then_inc(s2l, 16)

            with (
                nc.psum_tensor("p2c", [128, 512], F32) as p2c,
                nc.psum_tensor("p2t0", [128, 512], F32) as p2t0,
                nc.psum_tensor("p2t1", [128, 512], F32) as p2t1,
            ):
                p2t = (p2t0, p2t1)

                @block.tensor
                def _(t):
                    t.wait_ge(s2l, 16)
                    E(t.matmul(p2c[:], c_onesf[0:1, 0:128], cdrow[:],
                               start=True, stop=True), s2t, "2t")
                    t.wait_ge(s2v, V_DTDT)
                    for c in range(NCH):
                        if c >= 2:
                            t.wait_ge(s2v, V_DTDT + 1 + 4 * (c - 1))
                        for si, srcm in enumerate((dAcs, dtf, dtdyf, dtdtf)):
                            E(t.matmul(p2t[c % 2][:, 128 * si:128 * si + 64],
                                       srcm[:, 128 * c:128 * c + 128],
                                       c_identf[0:64, 0:64], start=True, stop=True,
                                       is_transpose=True), s2t, "2t")

                @block.vector
                def _(v):
                    v.wait_ge(s2t, 1)
                    E(v.tensor_copy(cdb[:], p2c[:]), s2v, "2v")
                    for c in range(NCH):
                        v.wait_ge(s2t, 1 + 4 * (c + 1))
                        for si, dstm in enumerate((datok, dttok, dctok, ddtok)):
                            E(v.tensor_copy(dstm[:, 64 * c:64 * c + 64],
                                            p2t[c % 2][:, 128 * si:128 * si + 64]),
                              s2v, "2v")
                phase_end()

            s3l = (sem("s3l0"), sem("s3l1"))
            s3t = sem("s3t")
            s3e = sem("s3e")
            s3v2 = sem("s3v2")
            s3t2 = sem("s3t2")
            s3a = sem("s3a")
            with (
                nc.sbuf_tensor("xbw", [128, 2 * 2048], BF16) as xbw,
                nc.sbuf_tensor("xdb1", [128, 2 * 512], BF16) as xdb1,
                nc.sbuf_tensor("scac", [128, 4096], F32) as scac,
                nc.sbuf_tensor("ag3st", [128, 4096], BF16) as ag3st,
                nc.psum_tensor("pT0", [128, 512], BF16) as pT0,
                nc.psum_tensor("pT1", [128, 512], BF16) as pT1,
                nc.psum_tensor("p3a0", [128, 512], F32) as p3a0,
                nc.psum_tensor("p3a1", [128, 512], F32) as p3a1,
            ):
                pT = (pT0, pT1)
                p3a = (p3a0, p3a1)

                @block.sync
                def _(sy):
                    for c in range(NCH):
                        if c >= 2:
                            sy.wait_ge(s3t, 16 * (c - 1))
                        sy.dma_start(
                            out=xbw[:, (c % 2) * 2048:(c % 2) * 2048 + 2048].rearrange(
                                "p (j w) -> p j w", j=16),
                            in_=xbcsc.ap()[0:2048, 128 * c:128 * c + 128].rearrange(
                                "(j p) w -> p j w", p=128)).then_inc(s3l[c % 2], 16)

                @block.tensor
                def _(t):
                    for c in range(NCH):
                        t.wait_ge(s3l[c % 2], 16 * (c // 2 + 1))
                        for f in range(4):
                            Tf = 4 * c + f
                            if Tf >= 2:
                                t.wait_ge(s3e, Tf - 1)
                            for i in range(4):
                                E(t.matmul(pT[Tf % 2][:, 128 * i:128 * i + 128],
                                           xbw[:, (c % 2) * 2048 + 128 * (4 * f + i):
                                               (c % 2) * 2048 + 128 * (4 * f + i) + 128],
                                           c_identb[:], start=True, stop=True,
                                           is_transpose=True), s3t, "3t")

                @block.vector
                def _(v):
                    for c in range(NCH):
                        for f in range(4):
                            Tf = 4 * c + f
                            v.wait_ge(s3t, 4 * (Tf + 1))
                            if f < 2:
                                dst = xtokb[:, 1024 * c + 512 * f:1024 * c + 512 * f + 512]
                            else:
                                dst = btokb[:, 1024 * c + 512 * (f - 2):
                                            1024 * c + 512 * (f - 2) + 512]
                            E(v.tensor_copy(dst, pT[Tf % 2][:]), s3e, "3e")

                def emit_accum(it2):
                    c, hg = it2 // 8, it2 % 8

                    @block.vector
                    def _(v):
                        v.wait_ge(s3t2, it2 + 1)
                        if c == 0:
                            E(v.tensor_copy(scac[:, 512 * hg:512 * hg + 512],
                                            p3a[it2 % 2][:]), s3a, "3a")
                        else:
                            v.wait_ge(s3a, 8 * (c - 1) + hg + 1)
                            E(v.scalar_tensor_tensor(scac[:, 512 * hg:512 * hg + 512],
                                                     p3a[it2 % 2][:], 1.0,
                                                     scac[:, 512 * hg:512 * hg + 512],
                                                     ALU.mult, ALU.add), s3a, "3a")

                for it2 in range(64):
                    c, hg = it2 // 8, it2 % 8

                    @block.vector
                    def _(v):
                        if it2 == 0:
                            v.wait_ge(s3e, 32)
                        if it2 >= 2:
                            v.wait_ge(s3t2, it2 - 1)
                        for hh in range(8):
                            h = 8 * hg + hh
                            v.tensor_scalar(
                                xdb1[:, (it2 % 2) * 512 + 64 * hh:(it2 % 2) * 512 + 64 * hh + 64],
                                xtokb[:, 1024 * c + 128 * hg + 64 * (hh % 2):
                                      1024 * c + 128 * hg + 64 * (hh % 2) + 64],
                                ddtok[:, 64 * c + h:64 * c + h + 1], None,
                                ALU.mult).then_inc(s3v2, 1)

                    @block.tensor
                    def _(t):
                        t.wait_ge(s3v2, 8 * (it2 + 1))
                        if it2 >= 2:
                            t.wait_ge(s3a, it2 - 1)
                        for hh in range(8):
                            mm = t.matmul(p3a[it2 % 2][:, 64 * hh:64 * hh + 64],
                                          btokb[:, 1024 * c + 128 * hg:
                                                1024 * c + 128 * hg + 128],
                                          xdb1[:, (it2 % 2) * 512 + 64 * hh:
                                               (it2 % 2) * 512 + 64 * hh + 64],
                                          start=True, stop=True)
                        mm.then_inc(s3t2, 1)

                    if it2 >= 1:
                        emit_accum(it2 - 1)
                emit_accum(63)

                @block.vector
                def _(v):
                    v.wait_ge(s3a, 64)
                    E(v.tensor_copy(ag3st[:], scac[:]), s3a, "3a")

                @block.sync
                def _(sy):
                    sy.wait_ge(s3a, 65)
                    sy.dma_start(out=ag3_in.ap(), in_=ag3st[:]).then_inc(s3l[0], 16)

                @block.gpsimd
                def _(g):
                    g.wait_ge(s3l[0], 16 * 5)
                    g.collective_compute("AllGather", ALU.bypass, ins=[ag3_in.ap().opt()],
                                         outs=[ag3_out.ap().opt()],
                                         replica_groups=rg8).then_inc(s_ag, 1)
                phase_end()

            s4v = sem("s4v")
            s3l_base = (16 * 5, 16 * 4)
            with nc.sbuf_tensor("agw", [128, 2 * 4096], BF16) as agw:
                @block.sync
                def _(sy):
                    sy.wait_ge(s_ag, 4)
                    for r in range(8):
                        if r >= 2:
                            sy.wait_ge(s4v, r - 1)
                        sy.dma_start(out=agw[:, (r % 2) * 4096:(r % 2) * 4096 + 4096],
                                     in_=ag3_out.ap()[128 * r:128 * r + 128, :]
                                     ).then_inc(s3l[r % 2], 16)

                @block.vector
                def _(v):
                    for r in range(8):
                        v.wait_ge(s3l[r % 2], s3l_base[r % 2] + 16 * (r // 2 + 1))
                        if r == 0:
                            E(v.tensor_scalar(carry[:], agw[:, 0:4096],
                                              c_hmask[:, 0:1], None, ALU.mult),
                              s4v, "4v")
                        else:
                            v.wait_ge(s4v, r)
                            E(v.scalar_tensor_tensor(carry[:],
                                                     agw[:, (r % 2) * 4096:(r % 2) * 4096 + 4096],
                                                     c_hmask[:, r:r + 1], carry[:],
                                                     ALU.mult, ALU.add), s4v, "4v")
                    v.wait_ge(s4v, 8)
                    E(v.tensor_copy(carryb[:], carry[:]), s4v, "4v")
                phase_end()

            s5l = sem("s5l")
            s5g = (sem("s5g0"), sem("s5g1"))
            s5t = sem("s5t")
            s5v = sem("s5v")
            s5s = sem("s5s")
            s5p = sem("s5p")
            s5r = sem("s5r")
            s5st = sem("s5st")
            s5w = (sem("s5w0"), sem("s5w1"))
            m5ctx = contextlib.ExitStack()
            with m5ctx:
                xw = m5ctx.enter_context(nc.sbuf_tensor("xw", [128, 48 * 128], BF16))
                szb = m5ctx.enter_context(nc.sbuf_tensor("szb", [128, 4096], BF16))
                x1w = m5ctx.enter_context(nc.sbuf_tensor("x1w", [128, 16 * 128], F32))
                dar = m5ctx.enter_context(nc.sbuf_tensor("dar", [1, 2 * 1024], F32))
                negf = m5ctx.enter_context(nc.sbuf_tensor("negf", [1, 128], F32))
                rcb = m5ctx.enter_context(nc.sbuf_tensor("rcb", [128, 2 * 512], F32))
                ebb = m5ctx.enter_context(nc.sbuf_tensor("ebb", [128, 2 * 512], BF16))
                e2b = m5ctx.enter_context(nc.sbuf_tensor("e2b", [128, 2 * 512], BF16))
                gmb = m5ctx.enter_context(nc.sbuf_tensor("gmb", [128, 2 * 256], BF16))
                glb = m5ctx.enter_context(nc.sbuf_tensor("glb", [128, 2 * 512], BF16))
                cvb = m5ctx.enter_context(nc.sbuf_tensor("cvb", [128, 2 * 512], BF16))
                xdbv = m5ctx.enter_context(nc.sbuf_tensor("xdbv", [128, 2 * 256], BF16))
                sqb2 = m5ctx.enter_context(nc.sbuf_tensor("sqb2", [128, 2 * 256], BF16))
                ybb = m5ctx.enter_context(nc.sbuf_tensor("ybb", [128, 4096], BF16))
                yhb = m5ctx.enter_context(nc.sbuf_tensor("yhb", [128, 4096], BF16))
                sdb = m5ctx.enter_context(nc.sbuf_tensor("sdb", [32, 128], F32))
                sdbb = m5ctx.enter_context(nc.sbuf_tensor("sdbb", [32, 128], BF16))
                rrow = m5ctx.enter_context(nc.sbuf_tensor("rrow", [1, 4096], BF16))
                wob = m5ctx.enter_context(nc.sbuf_tensor("wob", [128, 2 * 4096], BF16))
                x2w = m5ctx.enter_context(nc.sbuf_tensor("x2w", [128, 16 * 128], F32))
                pR = (m5ctx.enter_context(nc.psum_tensor("pR0", [128, 512], F32)),
                      m5ctx.enter_context(nc.psum_tensor("pR1", [128, 512], F32)))
                pEB = (m5ctx.enter_context(nc.psum_tensor("pEB0", [128, 512], F32)),
                       m5ctx.enter_context(nc.psum_tensor("pEB1", [128, 512], F32)))
                pG2 = m5ctx.enter_context(nc.psum_tensor("pG2", [128, 512], F32))
                pY2 = m5ctx.enter_context(nc.psum_tensor("pY2", [128, 512], F32))
                pS2 = m5ctx.enter_context(nc.psum_tensor("pS2", [128, 512], F32))
                pGN = m5ctx.enter_context(nc.psum_tensor("pGN", [32, 512], F32))

                @block.vector
                def _(v):
                    E(v.tensor_scalar(negf[:], c_onesf[0:1, 0:128], -1.0, None,
                                      ALU.mult), s5v, "5v")
                NEG_V = CNT["5v"]

                for c in range(NCH):
                    @block.sync
                    def _(sy):
                        sy.dma_start(
                            out=xw[:].rearrange("p (j w) -> p j w", j=48),
                            in_=xbcsc.ap()[:, 128 * c:128 * c + 128].rearrange(
                                "(j p) w -> p j w", p=128)).then_inc(s5l, 16)
                        sy.dma_start(
                            out=szb[:].rearrange("p (j w) -> p j w", j=32),
                            in_=zsc.ap()[:, 128 * c:128 * c + 128].rearrange(
                                "(j p) w -> p j w", p=128)).then_inc(s5l, 16)
                        sy.dma_start(
                            out=x1w[:].rearrange("p (j w) -> p j w", j=16),
                            in_=xs1.ap()[:, 128 * c:128 * c + 128].rearrange(
                                "(j p) w -> p j w", p=128)).then_inc(s5l, 16)
                    CNT["5l"] += 3
                    lw_all = CNT["5l"]

                    @block.scalar
                    def _(s):
                        s.wait_ge(s5l, 16 * lw_all)
                        E(s.activation(szb[:], szb[:], AF.Silu), s5s, "5s")
                    sz_cnt = CNT["5s"]

                    t1r = {}
                    t1x = {}
                    v1c = {}
                    v2a = {}
                    s1a = {}
                    s1c = {}
                    gpc = {}
                    t2c = {}
                    v3c = {}
                    v3y = {}
                    v2b = {}
                    gp2 = {}
                    darn = {}

                    def emit_T1(g):
                        s_ = g % 2
                        hg = g // 2
                        if g % 2 == 0:
                            gate = t1x.get(2 * (hg - 2) + 1)

                            @block.sync
                            def _(sy):
                                if gate is not None:
                                    sy.wait_ge(s5t, gate)
                                sy.dma_start(
                                    out=dar[0:1, (hg % 2) * 1024:(hg % 2) * 1024 + 1024],
                                    in_=dAcs[8 * hg:8 * hg + 8, 128 * c:128 * c + 128]
                                    ).then_inc(s5g[hg % 2], 16)
                            CNT["5g%d" % (hg % 2)] += 1
                            darn[hg] = CNT["5g%d" % (hg % 2)]

                        @block.tensor
                        def _(t):
                            t.wait_ge(s5g[hg % 2], 16 * darn[hg])
                            t.wait_ge(s5l, 16 * lw_all)
                            t.wait_ge(s5v, v2b.get(g - 2, NEG_V))
                            ws = s1c.get(g - 2)
                            if ws is not None:
                                t.wait_ge(s5s, ws)
                            mm = None
                            for i in range(4):
                                hh = 4 * s_ + i
                                dsl = dar[0:1, (hg % 2) * 1024 + 128 * hh:
                                          (hg % 2) * 1024 + 128 * hh + 128]
                                t.matmul(pR[s_][:, 128 * i:128 * i + 128],
                                         c_onesf[0:1, 0:128], dsl,
                                         start=True, stop=False)
                                mm = t.matmul(pR[s_][:, 128 * i:128 * i + 128],
                                              dsl, negf[0:1, 0:128],
                                              start=False, stop=True)
                            E(mm, s5t, "5t")
                            t1r[g] = CNT["5t"]
                            for i in range(4):
                                hh = 4 * s_ + i
                                dsl = dar[0:1, (hg % 2) * 1024 + 128 * hh:
                                          (hg % 2) * 1024 + 128 * hh + 128]
                                mm = t.matmul(pEB[s_][:, 128 * i:128 * i + 128],
                                              c_onesf[0:1, 0:128], dsl,
                                              start=True, stop=True)
                            for jj in range(2):
                                j = 2 * g + jj
                                mm = t.matmul(pG2[:, 256 * s_ + 128 * jj:
                                                  256 * s_ + 128 * jj + 128],
                                              xw[:, 128 * (8 + hg):128 * (8 + hg) + 128],
                                              xw[:, 128 * (16 + j):128 * (16 + j) + 128],
                                              start=True, stop=True)
                            E(mm, s5t, "5t")
                            t1x[g] = CNT["5t"]

                    def emit_V1(g):
                        s_ = g % 2
                        hg = g // 2

                        @block.vector
                        def _(v):
                            v.wait_ge(s5t, t1r[g])
                            E(v.tensor_scalar(rcb[:, 512 * s_:512 * s_ + 512],
                                              pR[s_][:], 0.0, None, ALU.min),
                              s5v, "5v")
                            v1c[g] = CNT["5v"]
                            wg = t2c.get(g - 2)
                            if wg is not None:
                                v.wait_ge(s5t, wg)
                            for i in range(4):
                                h = 4 * g + i
                                ro = 64 * (i % 2)
                                E(v.tensor_scalar(xdbv[:, 256 * s_ + 64 * i:
                                                       256 * s_ + 64 * i + 64],
                                                  xtokb[:, 1024 * c + 128 * hg + ro:
                                                        1024 * c + 128 * hg + ro + 64],
                                                  dctok[:, 64 * c + h:64 * c + h + 1],
                                                  None, ALU.mult), s5v, "5v")
                            v2a[g] = CNT["5v"]

                    def emit_S1(g):
                        s_ = g % 2

                        @block.scalar
                        def _(s):
                            s.wait_ge(s5v, v1c[g])
                            s.wait_ge(s5t, t1x[g])
                            E(s.activation(ebb[:, 512 * s_:512 * s_ + 512],
                                           rcb[:, 512 * s_:512 * s_ + 512], AF.Exp),
                              s5s, "5s")
                            s1a[g] = CNT["5s"]
                            E(s.activation(e2b[:, 512 * s_:512 * s_ + 512],
                                           pEB[s_][:], AF.Exp), s5s, "5s")
                            s1c[g] = CNT["5s"]

                    def emit_GPc(g):
                        s_ = g % 2

                        @block.gpsimd
                        def _(gp):
                            gp.wait_ge(s5s, s1c[g])
                            wg = t2c.get(g - 2)
                            if wg is not None:
                                gp.wait_ge(s5t, wg)
                            for i in range(4):
                                j = 2 * g + i // 2
                                E(gp.tensor_tensor(cvb[:, 512 * s_ + 128 * i:
                                                       512 * s_ + 128 * i + 128],
                                                   xw[:, 128 * (16 + j):128 * (16 + j) + 128],
                                                   e2b[:, 512 * s_ + 128 * i:
                                                       512 * s_ + 128 * i + 128],
                                                   ALU.mult), s5p, "5p")
                            gpc[g] = CNT["5p"]

                    def emit_V3(g):
                        s_ = g % 2

                        @block.vector
                        def _(v):
                            v.wait_ge(s5t, t2c[g])
                            for i in range(4):
                                h = 4 * g + i
                                E(v.tensor_scalar(carry[:, 64 * h:64 * h + 64],
                                                  carry[:, 64 * h:64 * h + 64],
                                                  cdb[:, 8 * h + c:8 * h + c + 1],
                                                  None, ALU.mult), s5v, "5v")
                            v.wait_ge(s5v, CNT["5v"])
                            E(v.scalar_tensor_tensor(carry[:, 256 * g:256 * g + 256],
                                                     pS2[:, 256 * s_:256 * s_ + 256], 1.0,
                                                     carry[:, 256 * g:256 * g + 256],
                                                     ALU.mult, ALU.add), s5v, "5v")
                            v3c[g] = CNT["5v"]
                            v.wait_ge(s5s, sz_cnt)
                            E(v.tensor_tensor(ybb[:, 256 * g:256 * g + 256],
                                              pY2[:, 256 * s_:256 * s_ + 256],
                                              szb[:, 256 * g:256 * g + 256],
                                              ALU.mult), s5v, "5v")
                            v3y[g] = CNT["5v"]

                    def emit_V2b(g):
                        s_ = g % 2

                        @block.vector
                        def _(v):
                            v.wait_ge(s5s, s1a[g])
                            v.wait_ge(s5t, t1x[g])
                            for jj in range(2):
                                E(v.tensor_tensor(gmb[:, 256 * s_ + 128 * jj:
                                                      256 * s_ + 128 * jj + 128],
                                                  pG2[:, 256 * s_ + 128 * jj:
                                                      256 * s_ + 128 * jj + 128],
                                                  c_trilb[:], ALU.mult), s5v, "5v")
                            v.wait_ge(s5v, CNT["5v"])
                            for i in range(4):
                                h = 4 * g + i
                                jj = i // 2
                                E(v.scalar_tensor_tensor(
                                    glb[:, 512 * s_ + 128 * i:512 * s_ + 128 * i + 128],
                                    ebb[:, 512 * s_ + 128 * i:512 * s_ + 128 * i + 128],
                                    dttok[:, 64 * c + h:64 * c + h + 1],
                                    gmb[:, 256 * s_ + 128 * jj:256 * s_ + 128 * jj + 128],
                                    ALU.mult, ALU.mult), s5v, "5v")
                            v2b[g] = CNT["5v"]

                    def emit_T2(g):
                        s_ = g % 2
                        hg = g // 2

                        @block.tensor
                        def _(t):
                            t.wait_ge(s5v, v2b[g])
                            t.wait_ge(s5p, gpc[g])
                            wg = gp2.get(g - 1)
                            if wg is not None:
                                t.wait_ge(s5p, wg)
                            if g >= 1:
                                for jj in range(2):
                                    jp = 2 * (g - 1) + jj
                                    t.matmul(pGN[:, 0:128], c_gnoh[:, 32 * jp:32 * jp + 32],
                                             sqb2[:, 256 * ((g - 1) % 2) + 128 * jj:
                                                  256 * ((g - 1) % 2) + 128 * jj + 128],
                                             start=(jp == 0), stop=(jp == 31))
                            mm = None
                            for i in range(4):
                                h = 4 * g + i
                                jj = i // 2
                                j = 2 * g + jj
                                ro = 64 * (i % 2)
                                yt = pY2[ro:ro + 64, 256 * s_ + 128 * jj:
                                         256 * s_ + 128 * jj + 128]
                                t.matmul(yt,
                                         xtokb[:, 1024 * c + 128 * hg + ro:
                                               1024 * c + 128 * hg + ro + 64],
                                         glb[:, 512 * s_ + 128 * i:512 * s_ + 128 * i + 128],
                                         start=True, stop=False)
                                t.matmul(yt, carryb[:, 64 * h:64 * h + 64],
                                         cvb[:, 512 * s_ + 128 * i:512 * s_ + 128 * i + 128],
                                         start=False, stop=False)
                                t.matmul(yt,
                                         c_diagd[ro:ro + 64, 128 * j + ro:128 * j + ro + 64],
                                         xw[ro:ro + 64, 128 * hg:128 * hg + 128],
                                         start=False, stop=True)
                                mm = t.matmul(pS2[:, 256 * s_ + 64 * i:256 * s_ + 64 * i + 64],
                                              btokb[:, 1024 * c + 128 * hg:
                                                    1024 * c + 128 * hg + 128],
                                              xdbv[:, 256 * s_ + 64 * i:256 * s_ + 64 * i + 64],
                                              start=True, stop=True)
                            E(mm, s5t, "5t")
                            t2c[g] = CNT["5t"]

                    def emit_GP2(g):
                        s_ = g % 2

                        @block.gpsimd
                        def _(gp):
                            gp.wait_ge(s5v, v3y[g])
                            E(gp.tensor_copy(carryb[:, 256 * g:256 * g + 256],
                                             carry[:, 256 * g:256 * g + 256]), s5p, "5p")
                            E(gp.tensor_tensor(sqb2[:, 256 * s_:256 * s_ + 256],
                                               ybb[:, 256 * g:256 * g + 256],
                                               ybb[:, 256 * g:256 * g + 256],
                                               ALU.mult), s5p, "5p")
                            gp2[g] = CNT["5p"]

                    for tk in range(17):
                        gA = tk
                        gB = tk - 1
                        if gA <= 15:
                            emit_T1(gA)
                            emit_V1(gA)
                            emit_S1(gA)
                            emit_GPc(gA)
                        if gB >= 0:
                            emit_T2(gB)
                            emit_V3(gB)
                        if gA <= 15:
                            emit_V2b(gA)
                        if gB >= 0:
                            emit_GP2(gB)

                    @block.tensor
                    def _(t):
                        t.wait_ge(s5p, gp2[15])
                        mm = None
                        for jj in range(2):
                            jp = 30 + jj
                            mm = t.matmul(pGN[:, 0:128], c_gnoh[:, 32 * jp:32 * jp + 32],
                                          sqb2[:, 256 * (15 % 2) + 128 * jj:
                                               256 * (15 % 2) + 128 * jj + 128],
                                          start=False, stop=(jp == 31))
                        E(mm, s5t, "5t")
                    tgn = CNT["5t"]

                    @block.scalar
                    def _(s):
                        s.wait_ge(s5t, tgn)
                        E(s.activation(sdb[:], pGN[:, 0:128], AF.Sqrt,
                                       bias=c_fc[0:32, 0:1], scale=c_fc[0:32, 1:2]),
                          s5s, "5s")
                    srs = CNT["5s"]

                    @block.vector
                    def _(v):
                        v.wait_ge(s5s, srs)
                        E(v.reciprocal(sdb[:], sdb[:]), s5v, "5v")
                        v.wait_ge(s5v, CNT["5v"])
                        E(v.tensor_copy(sdbb[:], sdb[:]), s5v, "5v")
                    vrs = CNT["5v"]

                    @block.sync
                    def _(sy):
                        sy.wait_ge(s5v, vrs)
                        sy.dma_start(out=rrow[:], in_=sdbb[:]).then_inc(s5r, 16)
                    CNT["5r"] += 1

                    yh_after = {}
                    for q in range(8):
                        pv = yh_after.get(q - 2)

                        @block.tensor
                        def _(t):
                            t.wait_ge(s5r, 16 * CNT["5r"])
                            if pv is not None:
                                t.wait_ge(s5v, pv)
                            mm = None
                            for qq in range(4):
                                g2 = 4 * q + qq
                                mm = t.matmul(pEB[q % 2][:, 128 * qq:128 * qq + 128],
                                              c_onesb[0:1, 0:128],
                                              rrow[0:1, 128 * g2:128 * g2 + 128],
                                              start=True, stop=True)
                            E(mm, s5t, "5t")
                        trb = CNT["5t"]

                        @block.vector
                        def _(v):
                            v.wait_ge(s5t, trb)
                            for qq in range(4):
                                g2 = 4 * q + qq
                                E(v.scalar_tensor_tensor(yhb[:, 128 * g2:128 * g2 + 128],
                                                         ybb[:, 128 * g2:128 * g2 + 128],
                                                         c_normw[:, g2:g2 + 1],
                                                         pEB[q % 2][:, 128 * qq:128 * qq + 128],
                                                         ALU.mult, ALU.mult), s5v, "5v")
                        yh_after[q] = CNT["5v"]

                    yv_all = CNT["5v"]
                    tom_after = {}
                    oev_after = {}
                    for m in range(16):
                        tp = tom_after.get(m - 2)

                        @block.gpsimd
                        def _(gp):
                            if tp is not None:
                                gp.wait_ge(s5t, tp)
                            gp.dma_start(
                                out=s3(wob[:, (m % 2) * 4096:(m % 2) * 4096 + 4096], 32),
                                in_=r3(w_out_t[m * 4096:(m + 1) * 4096, :], 32)
                                ).then_inc(s5w[m % 2], 16)
                        CNT["5w%d" % (m % 2)] += 1
                        wn = CNT["5w%d" % (m % 2)]
                        ev = oev_after.get(m - 2)

                        @block.tensor
                        def _(t):
                            t.wait_ge(s5w[m % 2], 16 * wn)
                            t.wait_ge(s5v, yv_all)
                            if ev is not None:
                                t.wait_ge(s5v, ev)
                            for kc in range(32):
                                mm = t.matmul(pR[m % 2][:, 0:128],
                                              wob[:, (m % 2) * 4096 + 128 * kc:
                                                  (m % 2) * 4096 + 128 * (kc + 1)],
                                              yhb[:, 128 * kc:128 * kc + 128],
                                              start=(kc == 0), stop=(kc == 31))
                            E(mm, s5t, "5t")
                        tom_after[m] = CNT["5t"]

                        @block.vector
                        def _(v):
                            v.wait_ge(s5t, tom_after[m])
                            E(v.tensor_tensor(x2w[:, 128 * m:128 * m + 128],
                                              pR[m % 2][:, 0:128], x1w[:, 128 * m:128 * m + 128],
                                              ALU.add), s5v, "5v")
                        oev_after[m] = CNT["5v"]

                    vall = CNT["5v"]

                    @block.sync
                    def _(sy):
                        sy.wait_ge(s5v, vall)
                        sy.dma_start(out=xs2.ap()[:, 128 * c:128 * c + 128].rearrange(
                                         "(j p) w -> p j w", p=128),
                                     in_=x2w[:].rearrange("p (j w) -> p j w", j=16)
                                     ).then_inc(s5st, 16)
                        if stage == 2:
                            sy.dma_start(out=out[:, 128 * c:128 * c + 128].rearrange(
                                             "(j p) w -> p j w", p=128),
                                         in_=x2w[:].rearrange("p (j w) -> p j w", j=16)
                                         ).then_inc(s5st, 16)
                    CNT["5st"] += 2 if stage == 2 else 1

                    phase_end(sync_wait=(s5st, 16 * CNT["5st"]))

        if stage >= 2:
            mctx = contextlib.ExitStack()
            xctx = contextlib.ExitStack()
            dtf = mctx.enter_context(nc.sbuf_tensor("dtf", [64, 1024], F32))
            dAf = mctx.enter_context(nc.sbuf_tensor("dAf", [64, 1024], F32))
            dAcs = mctx.enter_context(nc.sbuf_tensor("dAcs", [64, 1024], F32))
            decayf = mctx.enter_context(nc.sbuf_tensor("decayf", [64, 1024], F32))
            dtdyf = mctx.enter_context(nc.sbuf_tensor("dtdyf", [64, 1024], F32))
            dtdtf = mctx.enter_context(nc.sbuf_tensor("dtdtf", [64, 1024], F32))
            lasts = mctx.enter_context(nc.sbuf_tensor("lasts", [64, 24], F32))
            tpf = mctx.enter_context(nc.sbuf_tensor("tpf", [64, 8], F32))
            cdf = mctx.enter_context(nc.sbuf_tensor("cdf", [64, 8], F32))
            cdb = mctx.enter_context(nc.sbuf_tensor("cdb", [128, 512], F32))
            cdrow = mctx.enter_context(nc.sbuf_tensor("cdrow", [1, 512], F32))
            datok = mctx.enter_context(nc.sbuf_tensor("datok", [128, 512], F32))
            dttok = mctx.enter_context(nc.sbuf_tensor("dttok", [128, 512], F32))
            dctok = mctx.enter_context(nc.sbuf_tensor("dctok", [128, 512], F32))
            ddtok = mctx.enter_context(nc.sbuf_tensor("ddtok", [128, 512], F32))
            xtokb = mctx.enter_context(nc.sbuf_tensor("xtokb", [128, 8 * 1024], BF16))
            btokb = mctx.enter_context(nc.sbuf_tensor("btokb", [128, 8 * 1024], BF16))
            carry = mctx.enter_context(nc.sbuf_tensor("carry", [128, 4096], F32))
            carryb = mctx.enter_context(nc.sbuf_tensor("carryb", [128, 4096], BF16))
            xh2 = xctx.enter_context(nc.sbuf_tensor("xh2", [128, KD * 1027], BF16))

            s_h = sem("mh")
            s_hs = sem("mhs")
            s_ht = sem("mht")
            s_hl = sem("mhl")
            with (
                nc.sbuf_tensor("hg", [128, 512], F32) as hg,
                nc.sbuf_tensor("h64", [128, 64], F32) as h64,
                nc.sbuf_tensor("hb", [128, 64], BF16) as hb,
                nc.sbuf_tensor("hsq", [128, 64], BF16) as hsq,
                nc.sbuf_tensor("hrow", [1, 16], F32) as hrow,
                nc.sbuf_tensor("hd", [128, 4], F32) as hd,
                nc.psum_tensor("phSx", [1, 512], F32) as phSx,
                nc.psum_tensor("phSq", [1, 512], F32) as phSq,
                nc.psum_tensor("phM", [128, 8], F32) as phM,
            ):
                @block.sync
                def _(sy):
                    sy.wait_ge(s_ag, 3)
                    sy.dma_start(out=hg[:].rearrange("p (r k c) -> p r k c", r=8, k=16),
                                 in_=ag2_out.ap().rearrange("(r k p) c -> p r k c", p=128, k=16)
                                 ).then_inc(s_hl, 16)

                @block.vector
                def _(v):
                    v.wait_ge(s_hl, 16)
                    v.tensor_scalar(h64[:], hg[:, 0:64], c_hmask[:, 0:1], None,
                                    ALU.mult).then_inc(s_h, 1)
                    for r in range(1, 8):
                        v.wait_ge(s_h, r)
                        v.scalar_tensor_tensor(h64[:], hg[:, 64 * r:64 * r + 64],
                                               c_hmask[:, r:r + 1], h64[:],
                                               ALU.mult, ALU.add).then_inc(s_h, 1)
                    v.wait_ge(s_h, 8)
                    v.tensor_copy(hb[:], h64[:]).then_inc(s_h, 1)
                    v.tensor_tensor(hsq[:], h64[:], h64[:], ALU.mult).then_inc(s_h, 1)

                @block.tensor
                def _(t):
                    t.wait_ge(s_h, 10)
                    for k in range(KD):
                        t.matmul(phSx[0:1, 0:4], c_onesb[:, 0:1], hb[:, 4 * k:4 * k + 4],
                                 start=(k == 0), stop=(k == KD - 1))
                        mm = t.matmul(phSq[0:1, 0:4], c_onesb[:, 0:1], hsq[:, 4 * k:4 * k + 4],
                                      start=(k == 0), stop=(k == KD - 1))
                    mm.then_inc(s_ht, 1)

                @block.vector
                def _(v):
                    v.wait_ge(s_ht, 1)
                    v.tensor_scalar(hrow[:, 0:4], phSx[0:1, 0:4], 1.0 / D, None,
                                    ALU.mult).then_inc(s_h, 1)
                    v.wait_ge(s_h, 11)
                    v.tensor_tensor(hrow[:, 8:12], hrow[:, 0:4], hrow[:, 0:4],
                                    ALU.mult).then_inc(s_h, 1)
                    v.wait_ge(s_h, 12)
                    v.scalar_tensor_tensor(hrow[:, 4:8], phSq[0:1, 0:4], 1.0 / D,
                                           hrow[:, 8:12], ALU.mult,
                                           ALU.subtract).then_inc(s_h, 1)

                @block.scalar
                def _(s):
                    s.wait_ge(s_h, 13)
                    s.activation(hrow[:, 4:8], hrow[:, 4:8], AF.Sqrt,
                                 bias=c_fc[0:1, 0:1]).then_inc(s_hs, 1)

                @block.vector
                def _(v):
                    v.wait_ge(s_hs, 1)
                    v.reciprocal(hrow[:, 4:8], hrow[:, 4:8]).then_inc(s_h, 1)

                @block.tensor
                def _(t):
                    t.wait_ge(s_h, 14)
                    t.matmul(phM[:, 0:4], c_onesf[0:1, 0:128], hrow[:, 0:4],
                             start=True, stop=True)
                    t.matmul(phM[:, 4:8], c_onesf[0:1, 0:128], hrow[:, 4:8],
                             start=True, stop=True).then_inc(s_ht, 1)

                @block.vector
                def _(v):
                    v.wait_ge(s_ht, 2)
                    for k in range(KD):
                        if k >= 1:
                            v.wait_ge(s_h, 14 + 2 * k)
                        v.tensor_tensor(hd[:], h64[:, 4 * k:4 * k + 4],
                                        phM[:, 0:4], ALU.subtract).then_inc(s_h, 1)
                        v.wait_ge(s_h, 14 + 2 * k + 1)
                        v.tensor_tensor(xh2[:, k * 1027:k * 1027 + 3],
                                        hd[:, 1:4], phM[:, 5:8], ALU.mult).then_inc(s_h, 1)

            phase_end()
            emit_ln("l2", xs1.ap(), xh2, ncols=1024, colstride=1027, dst_off=3)
            phase_end()

            s_mw = (sem("mw0"), sem("mw1"))
            s_mt = sem("mt")
            s_mv = sem("mv")
            s_msc = sem("msc")
            s_mz = (sem("mz0"), sem("mz1"))
            s_mx = (sem("mx0"), sem("mx1"))
            Tc = [0] * (NF + 1)
            Vc = [0] * (NF + 1)
            Sc = [0] * (NF + 1)
            for f in range(NF):
                Tc[f + 1] = Tc[f] + (2 if (f < 32 or f == 80) else 3)
                Vc[f + 1] = Vc[f] + (2 if f < 32 else (4 if f < 80 else 0))
                Sc[f + 1] = Sc[f] + (0 if f < 32 else (4 if f < 80 else 3))
            zcum = [[0] * NF, [0] * NF]
            xcum = [[0] * NF, [0] * NF]
            a = b = c2 = d2 = 0
            for f in range(NF):
                if f < 32:
                    if f % 2 == 0:
                        a += 1
                    else:
                        b += 1
                elif f < 80:
                    if f % 2 == 0:
                        c2 += 1
                    else:
                        d2 += 1
                zcum[0][f] = a
                zcum[1][f] = b
                xcum[0][f] = c2
                xcum[1][f] = d2

            with (
                nc.sbuf_tensor("wib", [128, 2 * 2048], BF16) as wib,
                nc.sbuf_tensor("zstb", [128, 2 * 1024], BF16) as zstb,
                nc.sbuf_tensor("xbcf", [128, 2 * 1027], F32) as xbcf,
                nc.sbuf_tensor("accf", [128, 2 * 1024], F32) as accf,
                nc.sbuf_tensor("xbst", [128, 2 * 1024], BF16) as xbst,
                nc.sbuf_tensor("ef", [64, 1024], F32) as ef,
                nc.psum_tensor("mpA0", [128, 512], F32) as mpA0,
                nc.psum_tensor("mpA1", [128, 512], F32) as mpA1,
                nc.psum_tensor("mpB0", [128, 512], F32) as mpB0,
                nc.psum_tensor("mpB1", [128, 512], F32) as mpB1,
                nc.psum_tensor("mpC0", [128, 4], F32) as mpC0,
                nc.psum_tensor("mpC1", [128, 4], F32) as mpC1,
            ):
                mpA = (mpA0, mpA1)
                mpB = (mpB0, mpB1)
                mpC = (mpC0, mpC1)

                @block.gpsimd
                def _(g):
                    for f in range(NF):
                        if f >= 2:
                            g.wait_ge(s_mt, Tc[f - 1])
                        g.dma_start(out=s3(wib[:, (f % 2) * 2048:(f % 2) * 2048 + 2048], 16),
                                    in_=r3(w_in_t[f * 2048:(f + 1) * 2048, :], 16)
                                    ).then_inc(s_mw[f % 2], 16)

                @block.tensor
                def _(t):
                    for f in range(NF):
                        t.wait_ge(s_mw[f % 2], 16 * (f // 2 + 1))
                        if f >= 2:
                            t.wait_ge(s_mv, Vc[f - 1])
                            t.wait_ge(s_msc, Sc[f - 1])
                        wt = wib[:, (f % 2) * 2048:(f % 2) * 2048 + 2048]
                        if f < 32 or f == 80:
                            spl = [(3, 515, mpA), (515, 1027, mpB)]
                        else:
                            spl = [(0, 512, mpA), (512, 1024, mpB), (1024, 1027, mpC)]
                        for si, (lo, hi, ps) in enumerate(spl):
                            for k in range(KD):
                                mm = t.matmul(ps[f % 2][:, 0:hi - lo],
                                              wt[:, k * 128:(k + 1) * 128],
                                              xh2[:, k * 1027 + lo:k * 1027 + hi],
                                              start=(k == 0), stop=(k == KD - 1))
                            mm.then_inc(s_mt, 1)

                @block.vector
                def _(v):
                    for f in range(NF):
                        if f < 32:
                            if f >= 2:
                                v.wait_ge(s_mz[f % 2], 16 * zcum[f % 2][f - 2])
                            for si, ps in ((0, mpA), (1, mpB)):
                                v.wait_ge(s_mt, Tc[f] + si + 1)
                                v.tensor_copy(zstb[:, (f % 2) * 1024 + si * 512:(f % 2) * 1024 + si * 512 + 512],
                                              ps[f % 2][:, :]).then_inc(s_mv, 1)
                        elif f < 80:
                            cf = f - 32
                            base = Vc[f]
                            v.wait_ge(s_msc, Sc[f] + 3)
                            if f >= 34:
                                v.wait_ge(s_msc, Sc[f - 1])
                            xb = xbcf[:, (f % 2) * 1027:(f % 2) * 1027 + 1027]
                            ac = accf[:, (f % 2) * 1024:(f % 2) * 1024 + 1024]
                            v.tensor_scalar(ac[:, :], xb[:, 0:1024],
                                            c_convw[:, cf * 4:cf * 4 + 1],
                                            c_convb[:, cf:cf + 1], ALU.mult,
                                            ALU.add).then_inc(s_mv, 1)
                            for kk in range(1, 4):
                                v.wait_ge(s_mv, base + kk)
                                v.scalar_tensor_tensor(ac[:, :], xb[:, kk:kk + 1024],
                                                       c_convw[:, cf * 4 + kk:cf * 4 + kk + 1],
                                                       ac[:, :], ALU.mult,
                                                       ALU.add).then_inc(s_mv, 1)

                @block.scalar
                def _(s):
                    for f in range(32, NF):
                        if f < 80:
                            if f >= 34:
                                s.wait_ge(s_mv, Vc[f - 1])
                            for si, (lo, hi, ps) in enumerate(((0, 512, mpA), (512, 1024, mpB),
                                                               (1024, 1027, mpC))):
                                s.wait_ge(s_mt, Tc[f] + si + 1)
                                s.copy(xbcf[:, (f % 2) * 1027 + lo:(f % 2) * 1027 + hi],
                                       ps[f % 2][:, 0:hi - lo]).then_inc(s_msc, 1)
                            s.wait_ge(s_mv, Vc[f + 1])
                            if f >= 34:
                                s.wait_ge(s_mx[f % 2], 16 * xcum[f % 2][f - 2])
                            s.activation(xbst[:, (f % 2) * 1024:(f % 2) * 1024 + 1024],
                                         accf[:, (f % 2) * 1024:(f % 2) * 1024 + 1024],
                                         AF.Silu).then_inc(s_msc, 1)
                        else:
                            for si, ps in ((0, mpA), (1, mpB)):
                                s.wait_ge(s_mt, Tc[f] + si + 1)
                                s.activation(ef[:, si * 512:si * 512 + 512],
                                             ps[f % 2][0:64, :], AF.Exp,
                                             bias=c_dtb[:]).then_inc(s_msc, 1)
                            s.wait_ge(s_msc, Sc[f] + 2)
                            s.activation(dtf[:], ef[:], AF.Ln,
                                         bias=c_fc[0:64, 2:3]).then_inc(s_msc, 1)

                @block.sync
                def _(sy):
                    for f in range(NF):
                        if f < 32:
                            sy.wait_ge(s_mv, Vc[f + 1])
                            sy.dma_start(out=zsc.ap()[f * 128:(f + 1) * 128, :],
                                         in_=zstb[:, (f % 2) * 1024:(f % 2) * 1024 + 1024]
                                         ).then_inc(s_mz[f % 2], 16)
                        elif f < 80:
                            cf = f - 32
                            sy.wait_ge(s_msc, Sc[f + 1])
                            sy.dma_start(out=xbcsc.ap()[cf * 128:(cf + 1) * 128, :],
                                         in_=xbst[:, (f % 2) * 1024:(f % 2) * 1024 + 1024]
                                         ).then_inc(s_mx[f % 2], 16)

            xctx.close()
            phase_end(sync_wait=[(s_mz[0], 16 * 16), (s_mz[1], 16 * 16),
                                 (s_mx[0], 16 * 24), (s_mx[1], 16 * 24)])
            emit_scan(stage)
            mctx.close()

        if stage >= 3:
            with nc.sbuf_tensor("hid2", [128, KE * 1024], BF16) as hid2:
                with nc.sbuf_tensor("xh3", [128, KD * 1024], BF16) as xh3:
                    emit_ln("l3", xs2.ap(), xh3)
                    phase_end()
                    emit_ffn_u("f2", xh3, hid2)
                emit_ffn_d("f2", hid2, xs2.ap(), out, False, False)

        phase_end()

    return nc


def emit_mamba(*a, **k):
    raise NotImplementedError("stage >= 2 not implemented yet")


def _stage_inputs(inputs):
    u = np.asarray(inputs["L0_attn_u"], dtype=np.float32)
    W_in = np.asarray(inputs["L0_attn_W_in"], dtype=np.float32)
    conv_w = np.asarray(inputs["L0_attn_conv_w"], dtype=np.float32)
    conv_b = np.asarray(inputs["L0_attn_conv_b"], dtype=np.float32)
    dt_bias = np.asarray(inputs["L0_attn_dt_bias"], dtype=np.float32)
    A_log = np.asarray(inputs["L0_attn_A_log"], dtype=np.float32)
    Dv = np.asarray(inputs["L0_attn_D"], dtype=np.float32)
    norm_w = np.asarray(inputs["L0_attn_norm_weight"], dtype=np.float32)
    W_out = np.asarray(inputs["L0_attn_W_out"], dtype=np.float32)
    W_up = np.asarray(inputs["L0_ffn_W_up"], dtype=np.float32)
    W_g = np.asarray(inputs["L0_ffn_W_gate"], dtype=np.float32)
    W_d = np.asarray(inputs["L0_ffn_W_down"], dtype=np.float32)

    WinT = np.ascontiguousarray(W_in.T)
    WinT_pad = np.zeros((D, NF * 128), dtype=np.float32)
    WinT_pad[:, :10304] = WinT
    w_in_t = np.zeros((NF, 16, 128, 128), dtype=np.float32)
    for f in range(NF):
        for k in range(16):
            w_in_t[f, k] = WinT_pad[k * 128:(k + 1) * 128, f * 128:(f + 1) * 128]
    w_in_t = w_in_t.reshape(NF * 16 * 128, 128)

    WoutT = np.ascontiguousarray(W_out.T)
    w_out_t = np.zeros((16, 32, 128, 128), dtype=np.float32)
    for m in range(16):
        for kc in range(32):
            w_out_t[m, kc] = WoutT[kc * 128:(kc + 1) * 128, m * 128:(m + 1) * 128]
    w_out_t = w_out_t.reshape(16 * 32 * 128, 128)

    WupT = np.ascontiguousarray(W_up.T)
    WgT = np.ascontiguousarray(W_g.T)
    WdT = np.ascontiguousarray(W_d.T)

    identb = np.eye(128, dtype=BF)
    identf = np.eye(128, dtype=np.float32)
    trilb = np.triu(np.ones((128, 128), dtype=np.float32)).astype(BF)
    onesf = np.ones((128, 128), dtype=np.float32)
    onesb = np.ones((128, 128), dtype=BF)
    gn_oh = np.zeros((128, 32, 32), dtype=np.float32)
    for g in range(32):
        gn_oh[:, g, g] = 1.0
    gn_oh = gn_oh.reshape(128, 32 * 32).astype(BF)
    d_expv = np.repeat(Dv, 64).reshape(32, 128).T.copy()
    norm_wt = norm_w.reshape(32, 128).T.copy()

    common = dict(
        w_in_t=w_in_t, w_out_t=w_out_t,
        conv_w=conv_w, conv_b=conv_b[:, None].copy(),
        dt_bias=dt_bias[:, None].copy(), a_log=A_log[:, None].copy(),
        d_exp=d_expv, norm_w=norm_wt,
        identb=identb, identf=identf, trilb=trilb,
        onesf=onesf, onesb=onesb, gn_oh=gn_oh,
    )
    in_maps = []
    for c in range(8):
        s, h = c // 2, c % 2
        a = h * 1024
        m = dict(common)
        m["u_fm"] = np.ascontiguousarray(u[s, a:a + 1024, :].T)
        m["wup_sh"] = np.ascontiguousarray(WupT[:, 1024 * c:1024 * (c + 1)])
        m["wg_sh"] = np.ascontiguousarray(WgT[:, 1024 * c:1024 * (c + 1)])
        m["wd_sh"] = np.ascontiguousarray(WdT[:, 256 * c:256 * (c + 1)])
        fc = np.zeros((128, 8), dtype=np.float32)
        fc[:, 0] = 1e-5
        fc[:, 1] = 1.0 / 128.0
        fc[:, 2] = 1.0
        fc[:, 3] = -1.0
        m["fconst"] = fc
        hm = np.zeros((128, 8), dtype=np.float32)
        if h == 1:
            hm[:, c - 1] = 1.0
        m["hmask"] = hm
        in_maps.append(m)
    return in_maps


def kernel(**inputs):
    in_maps = _stage_inputs(inputs)
    nc = build(STAGE)
    res = run_bass_kernel_spmd(nc, in_maps, core_ids=list(range(8)))
    outp = np.empty((4, 2048, 2048), dtype=np.float32)
    for c in range(8):
        s, h = c // 2, c % 2
        a = h * 1024
        outp[s, a:a + 1024, :] = res.results[c]["out"].T
    return outp


# version 09
# speedup vs baseline: 1.3258x; 1.0046x over previous
import contextlib
import numpy as np
import ml_dtypes
import concourse.bass as bass
import concourse.mybir as mybir
from concourse.bass_utils import run_bass_kernel_spmd

F32 = mybir.dt.float32
BF16 = mybir.dt.bfloat16
AF = mybir.ActivationFunctionType
ALU = mybir.AluOpType
AX = mybir.AxisListType
BF = ml_dtypes.bfloat16

D = 2048
NT = 1024
DE = 8192
DI = 4096
CONVD = 6144
NF = 81
NCH = 8
KD = 16
KE = 64
KI = 32
LN_EPS = 1e-5
GN_EPS = 1e-5
TERN_EPS = 1e-8
NW = float(D * DE)

STAGE = 3
LNCUT = 4


def build(stage=STAGE):
    nc = bass.Bass("TRN2", target_bir_lowering=False, debug=False, detect_race_conditions=True)

    u_fm = nc.dram_tensor("u_fm", [D, NT], F32, kind="ExternalInput").ap()
    w_in_t = nc.dram_tensor("w_in_t", [NF * 16 * 128, 128], F32, kind="ExternalInput").ap()
    w_out_t = nc.dram_tensor("w_out_t", [16 * 32 * 128, 128], F32, kind="ExternalInput").ap()
    wup_sh = nc.dram_tensor("wup_sh", [D, 1024], F32, kind="ExternalInput").ap()
    wg_sh = nc.dram_tensor("wg_sh", [D, 1024], F32, kind="ExternalInput").ap()
    wd_sh = nc.dram_tensor("wd_sh", [DE, 256], F32, kind="ExternalInput").ap()
    conv_w = nc.dram_tensor("conv_w", [CONVD, 4], F32, kind="ExternalInput").ap()
    conv_b = nc.dram_tensor("conv_b", [CONVD, 1], F32, kind="ExternalInput").ap()
    dt_bias = nc.dram_tensor("dt_bias", [64, 1], F32, kind="ExternalInput").ap()
    a_log = nc.dram_tensor("a_log", [64, 1], F32, kind="ExternalInput").ap()
    d_exp = nc.dram_tensor("d_exp", [128, 32], F32, kind="ExternalInput").ap()
    norm_w = nc.dram_tensor("norm_w", [128, 32], F32, kind="ExternalInput").ap()
    identb = nc.dram_tensor("identb", [128, 128], BF16, kind="ExternalInput").ap()
    identf = nc.dram_tensor("identf", [128, 128], F32, kind="ExternalInput").ap()
    trilb = nc.dram_tensor("trilb", [128, 128], BF16, kind="ExternalInput").ap()
    onesf_in = nc.dram_tensor("onesf", [128, 128], F32, kind="ExternalInput").ap()
    onesb_in = nc.dram_tensor("onesb", [128, 128], BF16, kind="ExternalInput").ap()
    gn_oh_in = nc.dram_tensor("gn_oh", [128, 32 * 32], BF16, kind="ExternalInput").ap()
    hmask_in = nc.dram_tensor("hmask", [128, 8], F32, kind="ExternalInput").ap()
    fconst_in = nc.dram_tensor("fconst", [128, 8], F32, kind="ExternalInput").ap()
    out = nc.dram_tensor("out", [D, NT], F32, kind="ExternalOutput").ap()
    dbg = nc.dram_tensor("dbg", [128, 16], F32, kind="ExternalOutput").ap()

    ar0_in = nc.dram_tensor("ar0_in", [1, 8], F32)
    ar0_out = nc.dram_tensor("ar0_out", [1, 8], F32)
    ag1_in = nc.dram_tensor("ag1_in", [3 * 16384, 128], BF16)
    ag1a_out = nc.dram_tensor("ag1a_out", [8 * 2 * 16384, 128], BF16, addr_space="Shared")
    ag1b_out = nc.dram_tensor("ag1b_out", [8 * 16384, 128], BF16, addr_space="Shared")
    ag2_in = nc.dram_tensor("ag2_in", [D, 4], F32)
    ag2_out = nc.dram_tensor("ag2_out", [8 * D, 4], F32, addr_space="Shared")
    ag3_in = nc.dram_tensor("ag3_in", [128, 4096], BF16)
    ag3_out = nc.dram_tensor("ag3_out", [8 * 128, 4096], BF16, addr_space="Shared")
    xs1 = nc.dram_tensor("xs1", [D, NT], F32)
    xs2 = nc.dram_tensor("xs2", [D, NT], F32)
    zsc = nc.dram_tensor("zsc", [DI, NT], BF16)
    xbcsc = nc.dram_tensor("xbcsc", [CONVD, NT], BF16)

    rg8 = [[0, 1, 2, 3, 4, 5, 6, 7]]

    def tup_block(e):
        r, el = e // 8, e % 8
        off = r * 32768 + el * 2048
        return ag1a_out.ap()[off:off + 2048, :]

    def tg_block(e):
        r, el = e // 8, e % 8
        off = r * 32768 + 16384 + el * 2048
        return ag1a_out.ap()[off:off + 2048, :]

    def td_block(m):
        r, ml = m // 2, m % 2
        off = r * 16384 + ml * 8192
        return ag1b_out.ap()[off:off + 8192, :]

    def r3(dram_slice, kdim):
        return dram_slice.rearrange("(k p) c -> p k c", p=128)

    def s3(sbuf_ap, kdim):
        return sbuf_ap.rearrange("p (k c) -> p k c", k=kdim)

    ctx = contextlib.ExitStack()
    sems = {}

    def sem(name):
        if name not in sems:
            sems[name] = ctx.enter_context(nc.semaphore(name))
        return sems[name]

    PH = [0]

    with ctx:
        cst = ctx.enter_context(nc.sbuf_tensor("c_identb", [128, 128], BF16))
        c_identb = cst
        c_identf = ctx.enter_context(nc.sbuf_tensor("c_identf", [128, 128], F32))
        c_trilb = ctx.enter_context(nc.sbuf_tensor("c_trilb", [128, 128], BF16))
        c_onesf = ctx.enter_context(nc.sbuf_tensor("c_onesf", [128, 128], F32))
        c_onesb = ctx.enter_context(nc.sbuf_tensor("c_onesb", [128, 128], BF16))
        c_gnoh = ctx.enter_context(nc.sbuf_tensor("c_gnoh", [128, 32 * 32], BF16))
        c_hmask = ctx.enter_context(nc.sbuf_tensor("c_hmask", [128, 8], F32))
        c_convw = ctx.enter_context(nc.sbuf_tensor("c_convw", [128, 48 * 4], F32))
        c_convb = ctx.enter_context(nc.sbuf_tensor("c_convb", [128, 48], F32))
        c_dtb = ctx.enter_context(nc.sbuf_tensor("c_dtb", [64, 1], F32))
        c_av = ctx.enter_context(nc.sbuf_tensor("c_av", [64, 1], F32))
        c_dexp = ctx.enter_context(nc.sbuf_tensor("c_dexp", [128, 32], F32))
        c_normw = ctx.enter_context(nc.sbuf_tensor("c_normw", [128, 32], F32))
        c_diagd = ctx.enter_context(nc.sbuf_tensor("c_diagd", [128, 32 * 128], BF16))
        v_thu = ctx.enter_context(nc.sbuf_tensor("v_thu", [128, 1], F32))
        v_thun = ctx.enter_context(nc.sbuf_tensor("v_thun", [128, 1], F32))
        v_thg = ctx.enter_context(nc.sbuf_tensor("v_thg", [128, 1], F32))
        v_thgn = ctx.enter_context(nc.sbuf_tensor("v_thgn", [128, 1], F32))
        v_thd = ctx.enter_context(nc.sbuf_tensor("v_thd", [128, 1], F32))
        v_thdn = ctx.enter_context(nc.sbuf_tensor("v_thdn", [128, 1], F32))
        v_su = ctx.enter_context(nc.sbuf_tensor("v_su", [128, 1], F32))
        v_sdg = ctx.enter_context(nc.sbuf_tensor("v_sdg", [128, 1], F32))
        s_row = ctx.enter_context(nc.sbuf_tensor("s_row", [1, 16], F32))
        c_fc = ctx.enter_context(nc.sbuf_tensor("c_fc", [128, 8], F32))
        drv = ctx.enter_context(nc.sbuf_tensor("drv", [128, 2], F32))
        drs = ctx.enter_context(nc.sbuf_tensor("drs", [128, 2], F32))
        drg = ctx.enter_context(nc.sbuf_tensor("drg", [128, 2], F32))
        block = ctx.enter_context(nc.Block())

        consts = dict(identb=c_identb, identf=c_identf, trilb=c_trilb, onesf=c_onesf,
                      onesb=c_onesb, gnoh=c_gnoh, hmask=c_hmask, convw=c_convw,
                      convb=c_convb, dtb=c_dtb, av=c_av, dexp=c_dexp, normw=c_normw,
                      diagd=c_diagd)

        ph_sems = {e: sem("ph_" + e) for e in ("sy", "g", "t", "v", "s")}

        def phase_end(sync_wait=None, gp_wait=None, scalar_wait=None):
            tgt = PH[0] + 1

            def _aslist(x):
                if x is None:
                    return []
                if isinstance(x, list):
                    return x
                return [x]

            def wait_all(eng):
                for e in ("sy", "g", "t", "v", "s"):
                    eng.wait_ge(ph_sems[e], tgt)

            @block.sync
            def _(sy):
                for sw in _aslist(sync_wait):
                    sy.wait_ge(sw[0], sw[1])
                sy.sem_inc(ph_sems["sy"], 1)
                wait_all(sy)

            @block.gpsimd
            def _(g):
                for sw in _aslist(gp_wait):
                    g.wait_ge(sw[0], sw[1])
                g.tensor_copy(drg[:, 1:2], drg[:, 0:1]).then_inc(ph_sems["g"], 1)
                wait_all(g)

            @block.tensor
            def _(t):
                t.ldweights(c_onesb[:, 0:128]).then_inc(ph_sems["t"], 1)
                wait_all(t)

            @block.vector
            def _(v):
                v.tensor_copy(drv[:, 1:2], drv[:, 0:1]).then_inc(ph_sems["v"], 1)
                wait_all(v)

            @block.scalar
            def _(s):
                for sw in _aslist(scalar_wait):
                    s.wait_ge(sw[0], sw[1])
                s.copy(drs[:, 1:2], drs[:, 0:1]).then_inc(ph_sems["s"], 1)
                wait_all(s)

            PH[0] = tgt

        s_cld = sem("cld")
        s_e0s = sem("e0s")
        s_e0v = sem("e0v")
        s_e0g = sem("e0g")

        NCLD = [0]

        @block.sync
        def _(sy):
            for dst, srcc in ((c_identb, identb), (c_identf, identf),
                              (c_trilb, trilb), (c_onesf, onesf_in),
                              (c_onesb, onesb_in), (c_gnoh, gn_oh_in),
                              (c_hmask, hmask_in), (c_fc, fconst_in), (c_dtb, dt_bias),
                              (c_av, a_log), (c_dexp, d_exp), (c_normw, norm_w)):
                sy.dma_start(out=dst[:], in_=srcc).then_inc(s_cld, 16)
                NCLD[0] += 1
            for cf in range(48):
                sy.dma_start(out=c_convw[:, cf * 4:cf * 4 + 4],
                             in_=conv_w[cf * 128:cf * 128 + 128, :]).then_inc(s_cld, 16)
                sy.dma_start(out=c_convb[:, cf:cf + 1],
                             in_=conv_b[cf * 128:cf * 128 + 128, :]).then_inc(s_cld, 16)
                NCLD[0] += 2

        @block.scalar
        def _(s):
            s.memzero(drs[:])
            s.wait_ge(s_cld, 16 * NCLD[0])
            s.activation(c_av[:], c_av[:], AF.Exp).then_inc(s_e0s, 1)
            s.wait_ge(s_e0s, 1)
            s.mul(c_av[:], c_av[:], -1.0).then_inc(s_e0s, 1)

        @block.vector
        def _(v):
            v.memset(s_row[:], 0.0)
            v.memset(drv[:], 0.0)
            v.wait_ge(s_cld, 16 * NCLD[0])
            s_dg0 = sem("dg0")
            for j in range(32):
                v.tensor_scalar(c_diagd[:, j * 128:(j + 1) * 128], c_onesb[:],
                                c_dexp[:, j:j + 1], None, ALU.mult).then_inc(s_dg0, 1)
            v.wait_ge(s_dg0, 32)
            v.tensor_copy(drv[:, 1:2], drv[:, 0:1]).then_inc(s_e0v, 1)

        @block.gpsimd
        def _(g):
            g.memset(drg[:], 0.0)
            g.wait_ge(s_e0v, 1)
            for j in range(32):
                g.affine_select(c_diagd[:, j * 128:(j + 1) * 128],
                                c_diagd[:, j * 128:(j + 1) * 128],
                                pattern=[[1, 128]], base=0, channel_multiplier=-1,
                                compare_op=ALU.is_equal, fill=0.0).then_inc(s_e0g, 1)

        shard_plan = [("u", wup_sh, 16, 1024), ("g", wg_sh, 16, 1024),
                      ("d", wd_sh, 64, 256)]
        s_wl = (sem("wl0"), sem("wl1"))
        s_ar = sem("ar")
        s_wred = sem("wred")
        s_p0t = sem("p0t")
        s_p0v = sem("p0v")
        s_ag = sem("ag")
        s_tv = sem("tv")
        s_ts = (sem("ts0"), sem("ts1"))
        win_per = []
        for nm, dram, nl, ncols in shard_plan:
            win_per += [4096 // ncols] * 4
        win_per = win_per * 2
        wcum = [[0] * 24, [0] * 24]
        c0 = c1 = 0
        for w, per in enumerate(win_per):
            if w % 2 == 0:
                c0 += per
            else:
                c1 += per
            wcum[0][w] = c0
            wcum[1][w] = c1

        with (
            nc.sbuf_tensor("tw_win", [128, 2 * 4096], F32) as tw_win,
            nc.sbuf_tensor("tw_acc", [128, 12], F32) as tw_acc,
            nc.sbuf_tensor("tw_red", [128, 3], F32) as tw_red,
            nc.psum_tensor("tw_ps", [1, 4], F32) as tw_ps,
            nc.psum_tensor("tw_pb", [128, 16], F32) as tw_pb,
        ):
            @block.sync
            def _(sy):
                w = 0
                for nm, dram, nl, ncols in shard_plan:
                    per = 4096 // ncols
                    for w0 in range(0, nl, per):
                        if w >= 2:
                            sy.wait_ge(s_wred, w - 1)
                        buf = tw_win[:, (w % 2) * 4096:(w % 2) * 4096 + 4096]
                        for i in range(per):
                            k = w0 + i
                            sy.dma_start(out=buf[:, i * ncols:(i + 1) * ncols],
                                         in_=dram[k * 128:(k + 1) * 128, :]).then_inc(s_wl[w % 2], 16)
                        w += 1

            @block.vector
            def _(v):
                w = 0
                for nm, dram, nl, ncols in shard_plan:
                    per = 4096 // ncols
                    for wi in range(4):
                        v.wait_ge(s_wl[w % 2], 16 * wcum[w % 2][w])
                        buf = tw_win[:, (w % 2) * 4096:(w % 2) * 4096 + 4096]
                        v.tensor_reduce(tw_acc[:, w:w + 1], buf[:], AX.X, ALU.add,
                                        apply_absolute_value=True).then_inc(s_wred, 1)
                        w += 1
                v.wait_ge(s_wred, 12)
                for mi in range(3):
                    v.tensor_reduce(tw_red[:, mi:mi + 1], tw_acc[:, mi * 4:mi * 4 + 4],
                                    AX.X, ALU.add).then_inc(s_wred, 1)

            @block.tensor
            def _(t):
                t.wait_ge(s_wred, 15)
                t.matmul(tw_ps[0:1, 0:3], c_onesf[:, 0:1], tw_red[:, 0:3],
                         start=True, stop=True).then_inc(s_p0t, 1)

            @block.vector
            def _(v):
                v.wait_ge(s_p0t, 1)
                s_ms = sem("ms")
                v.memset(s_row[:, 4:12], 0.0).then_inc(s_ms, 1)
                v.wait_ge(s_ms, 1)
                v.tensor_copy(s_row[:, 4:7], tw_ps[0:1, 0:3]).then_inc(s_p0v, 1)

            @block.sync
            def _(sy):
                sy.wait_ge(s_p0v, 1)
                sy.dma_start(out=ar0_in.ap()[0:1, 0:8], in_=s_row[:, 4:12]).then_inc(s_ar, 16)

            @block.gpsimd
            def _(g):
                g.wait_ge(s_ar, 16)
                g.collective_compute("AllReduce", ALU.add, ins=[ar0_in.ap().opt()],
                                     outs=[ar0_out.ap().opt()],
                                     replica_groups=rg8).then_inc(s_ag, 1)

            @block.sync
            def _(sy):
                sy.wait_ge(s_ag, 1)
                sy.dma_start(out=s_row[:, 8:11], in_=ar0_out.ap()[0:1, 0:3]).then_inc(s_ar, 16)

            s_chn = sem("chn")

            @block.vector
            def _(v):
                v.wait_ge(s_ar, 32)
                v.tensor_scalar(s_row[:, 8:11], s_row[:, 8:11], 1.0 / NW, TERN_EPS,
                                ALU.mult, ALU.add).then_inc(s_chn, 1)
                v.wait_ge(s_chn, 1)
                v.tensor_scalar(s_row[:, 11:14], s_row[:, 8:11], 0.5, None, ALU.mult)
                v.tensor_tensor(s_row[:, 14:15], s_row[:, 9:10], s_row[:, 10:11],
                                ALU.mult).then_inc(s_chn, 1)
                v.wait_ge(s_chn, 2)
                v.tensor_scalar(s_row[:, 14:15], s_row[:, 14:15], 0.5, None,
                                ALU.mult).then_inc(s_chn, 1)
                v.wait_ge(s_chn, 3)
                v.tensor_copy(drv[:, 1:2], drv[:, 0:1]).then_inc(s_p0v, 1)

            @block.tensor
            def _(t):
                t.wait_ge(s_p0v, 2)
                cols5 = (11, 12, 13, 8, 14)
                for i, col in enumerate(cols5):
                    mm = t.matmul(tw_pb[:, i:i + 1], c_onesf[0:1, 0:128],
                                  s_row[:, col:col + 1], start=True, stop=True)
                    if i == len(cols5) - 1:
                        mm.then_inc(s_p0t, 1)

            @block.vector
            def _(v):
                v.wait_ge(s_p0t, 2)
                s_thv = sem("thv")
                for i, dst in enumerate((v_thu, v_thg, v_thd, v_su, v_sdg)):
                    v.tensor_copy(dst[:], tw_pb[:, i:i + 1]).then_inc(s_thv, 1)
                for i, dst in enumerate((v_thun, v_thgn, v_thdn)):
                    v.tensor_scalar(dst[:], tw_pb[:, i:i + 1], -1.0, None,
                                    ALU.mult).then_inc(s_thv, 1)
                v.wait_ge(s_thv, 8)
                v.tensor_copy(drv[:, 1:2], drv[:, 0:1]).then_inc(s_p0v, 1)

            with (
                nc.sbuf_tensor("tn_a", [128, 4096], F32) as tn_a,
                nc.sbuf_tensor("tn_t", [128, 2 * 4096], BF16) as tn_t,
            ):
                thvec = {"u": (v_thu, v_thun), "g": (v_thg, v_thgn), "d": (v_thd, v_thdn)}
                tern_inc = [16 * 8] * 8 + [16 * 2] * 4
                TS_CUM = [[0] * 12, [0] * 12]
                _c = [0, 0]
                for _wt in range(12):
                    _c[_wt % 2] += tern_inc[_wt]
                    TS_CUM[_wt % 2][_wt] = _c[_wt % 2]

                @block.sync
                def _(sy):
                    sy.wait_ge(s_wred, 15)
                    w = 12
                    for nm, dram, nl, ncols in shard_plan:
                        per = 4096 // ncols
                        for w0 in range(0, nl, per):
                            wt = w - 12
                            if wt >= 2:
                                sy.wait_ge(s_tv, 2 * (wt - 1))
                            buf = tw_win[:, (w % 2) * 4096:(w % 2) * 4096 + 4096]
                            for i in range(per):
                                k = w0 + i
                                sy.dma_start(out=buf[:, i * ncols:(i + 1) * ncols],
                                             in_=dram[k * 128:(k + 1) * 128, :]).then_inc(s_wl[w % 2], 16)
                            w += 1

                @block.vector
                def _(v):
                    v.wait_ge(s_p0v, 3)
                    w = 12
                    for nm, dram, nl, ncols in shard_plan:
                        per = 4096 // ncols
                        thp, thn = thvec[nm]
                        for wi in range(4):
                            wt = w - 12
                            v.wait_ge(s_wl[w % 2], 16 * wcum[w % 2][w])
                            if wt >= 2:
                                v.wait_ge(s_ts[wt % 2], TS_CUM[wt % 2][wt - 2])
                            buf = tw_win[:, (w % 2) * 4096:(w % 2) * 4096 + 4096]
                            tb = tn_t[:, (w % 2) * 4096:(w % 2) * 4096 + 4096]
                            if wt >= 1:
                                v.wait_ge(s_tv, 2 * wt)
                            v.tensor_scalar(tn_a[:], buf[:], thn[:], None,
                                            ALU.is_lt).then_inc(s_tv, 1)
                            v.wait_ge(s_tv, 2 * wt + 1)
                            v.scalar_tensor_tensor(tb[:], buf[:], thp[:], tn_a[:],
                                                   ALU.is_gt, ALU.subtract).then_inc(s_tv, 1)
                            w += 1

                reg_off = {"u": 0, "g": 16384, "d": 32768}
                reg_elstride = {"u": 2048, "g": 2048, "d": 8192}

                @block.scalar
                def _(s):
                    w = 0
                    for nm, dram, nl, ncols in shard_plan:
                        per = 4096 // ncols
                        nel = ncols // 128
                        for wi in range(4):
                            s.wait_ge(s_tv, 2 * (w + 1))
                            tb3 = tn_t[:, (w % 2) * 4096:(w % 2) * 4096 + 4096].rearrange(
                                "p (i ec) -> p i ec", i=per)
                            for el in range(nel):
                                roff = reg_off[nm] + el * reg_elstride[nm] + wi * per * 128
                                s.dma_start(
                                    out=ag1_in.ap()[roff:roff + per * 128, :].rearrange(
                                        "(i p) c -> p i c", p=128),
                                    in_=tb3[:, :, el * 128:(el + 1) * 128]
                                    ).then_inc(s_ts[w % 2], 16)
                            w += 1

                @block.gpsimd
                def _(g):
                    g.wait_ge(s_ts[0], 512)
                    g.wait_ge(s_ts[1], 512)
                    g.collective_compute("AllGather", ALU.bypass,
                                         ins=[ag1_in.ap()[0:32768, :].opt()],
                                         outs=[ag1a_out.ap().opt()],
                                         replica_groups=rg8).then_inc(s_ag, 1)

        phase_end(scalar_wait=[(s_ts[0], 576), (s_ts[1], 576)])

        def emit_dummy_out():
            s_do = sem("dout")

            @block.sync
            def _(sy):
                for k in range(KD):
                    sy.dma_start(out=out[k * 128:(k + 1) * 128, :],
                                 in_=u_fm[k * 128:(k + 1) * 128, :]).then_inc(s_do, 16)
                sy.wait_ge(s_do, 16 * KD)
            phase_end()

        if stage < 0.4:
            emit_dummy_out()
            return nc

        SEMB = {}

        def takebase(name, final):
            b = SEMB.get(name, 0)
            SEMB[name] = b + final
            return b

        def emit_ln(tag, src_dram, xh, ncols=1024, colstride=None, dst_off=0):
            if colstride is None:
                colstride = ncols
            s_lt = sem("lnlt")
            s_lv = sem("lnlv")
            s_ls = sem("lnls")
            s_lw = (sem("lnlw0"), sem("lnlw1"))
            blt = takebase("lnlt", KD + 1)
            blv = takebase("lnlv", 6 * KD + 3)
            bls = takebase("lnls", 1)
            blw = (takebase("lnlw0", 16 * 16), takebase("lnlw1", 16 * 16))
            nsp = 2
            with (
                nc.sbuf_tensor(tag + "win", [128, 2 * 1024], F32) as win,
                nc.sbuf_tensor(tag + "wb", [128, 2 * 1024], BF16) as winb,
                nc.sbuf_tensor(tag + "sq", [128, 2 * 1024], BF16) as sqb,
                nc.sbuf_tensor(tag + "mr", [1, 2 * ncols], F32) as mrow,
                nc.sbuf_tensor(tag + "dd", [128, 1024], F32) as dbuf,
                nc.psum_tensor(tag + "psxA", [1, 512], F32) as psxA,
                nc.psum_tensor(tag + "psxB", [1, 512], F32) as psxB,
                nc.psum_tensor(tag + "psqA", [1, 512], F32) as psqA,
                nc.psum_tensor(tag + "psqB", [1, 512], F32) as psqB,
                nc.psum_tensor(tag + "pmuA", [128, 512], F32) as pmuA,
                nc.psum_tensor(tag + "pmuB", [128, 512], F32) as pmuB,
                nc.psum_tensor(tag + "prA", [128, 512], F32) as prA,
                nc.psum_tensor(tag + "prB", [128, 512], F32) as prB,
            ):
                psx = (psxA, psxB)
                psq = (psqA, psqB)
                pmu = (pmuA, pmuB)
                pr = (prA, prB)
                splits = [(0, 512), (512, 1024)]

                @block.sync
                def _(sy):
                    for k in range(KD):
                        if k >= 2:
                            sy.wait_ge(s_lt, blt + k - 1)
                        sy.dma_start(out=win[:, (k % 2) * 1024:(k % 2) * 1024 + ncols],
                                     in_=src_dram[k * 128:(k + 1) * 128, :]).then_inc(s_lw[k % 2], 16)

                @block.vector
                def _(v):
                    for k in range(KD):
                        v.wait_ge(s_lw[k % 2], blw[k % 2] + 16 * (k // 2 + 1))
                        if k >= 2:
                            v.wait_ge(s_lt, blt + k - 1)
                        v.tensor_copy(winb[:, (k % 2) * 1024:(k % 2) * 1024 + ncols],
                                      win[:, (k % 2) * 1024:(k % 2) * 1024 + ncols]).then_inc(s_lv, 1)
                        v.tensor_tensor(sqb[:, (k % 2) * 1024:(k % 2) * 1024 + ncols],
                                        win[:, (k % 2) * 1024:(k % 2) * 1024 + ncols],
                                        win[:, (k % 2) * 1024:(k % 2) * 1024 + ncols],
                                        ALU.mult).then_inc(s_lv, 1)

                @block.tensor
                def _(t):
                    for k in range(KD):
                        t.wait_ge(s_lv, blv + 2 * (k + 1))
                        for sp, (lo, hi) in enumerate(splits):
                            t.matmul(psx[sp][0:1, :], c_onesb[:, 0:1],
                                     winb[:, (k % 2) * 1024 + lo:(k % 2) * 1024 + hi],
                                     start=(k == 0), stop=(k == KD - 1))
                            mm = t.matmul(psq[sp][0:1, :], c_onesb[:, 0:1],
                                          sqb[:, (k % 2) * 1024 + lo:(k % 2) * 1024 + hi],
                                          start=(k == 0), stop=(k == KD - 1))
                            if sp == len(splits) - 1:
                                mm.then_inc(s_lt, 1)

                if LNCUT < 2:
                    return

                s_lx = sem("lnlx")
                blx = takebase("lnlx", 4)

                @block.vector
                def _(v):
                    v.wait_ge(s_lt, blt + KD)
                    for sp, (lo, hi) in enumerate(splits):
                        v.tensor_scalar(mrow[:, lo:hi], psx[sp][0:1, :], 1.0 / D,
                                        None, ALU.mult).then_inc(s_lx, 1)
                    v.wait_ge(s_lx, blx + 2)
                    for sp, (lo, hi) in enumerate(splits):
                        v.tensor_tensor(mrow[:, ncols + lo:ncols + hi], mrow[:, lo:hi],
                                        mrow[:, lo:hi], ALU.mult).then_inc(s_lx, 1)
                    v.wait_ge(s_lx, blx + 4)
                    for sp, (lo, hi) in enumerate(splits):
                        v.scalar_tensor_tensor(mrow[:, ncols + lo:ncols + hi], psq[sp][0:1, :],
                                               1.0 / D, mrow[:, ncols + lo:ncols + hi],
                                               ALU.mult, ALU.subtract).then_inc(s_lv, 1)

                @block.scalar
                def _(s):
                    s.wait_ge(s_lv, blv + 2 * KD + 2)
                    s.activation(mrow[:, ncols:2 * ncols], mrow[:, ncols:2 * ncols],
                                 AF.Sqrt, bias=c_fc[0:1, 0:1]).then_inc(s_ls, 1)

                @block.vector
                def _(v):
                    v.wait_ge(s_ls, bls + 1)
                    v.reciprocal(mrow[:, ncols:2 * ncols],
                                 mrow[:, ncols:2 * ncols]).then_inc(s_lv, 1)

                if LNCUT < 3:
                    return

                @block.tensor
                def _(t):
                    t.wait_ge(s_lv, blv + 2 * KD + 3)
                    for sp, (lo, hi) in enumerate(splits):
                        t.matmul(pmu[sp][:, :], c_onesf[0:1, 0:128],
                                 mrow[:, lo:hi], start=True, stop=True)
                        mm = t.matmul(pr[sp][:, :], c_onesf[0:1, 0:128],
                                      mrow[:, ncols + lo:ncols + hi], start=True, stop=True)
                        if sp == len(splits) - 1:
                            mm.then_inc(s_lt, 1)

                if LNCUT < 4:
                    return

                @block.sync
                def _(sy):
                    for k in range(KD):
                        if k == 0:
                            sy.wait_ge(s_lt, blt + KD)
                        if k >= 2:
                            sy.wait_ge(s_lv, blv + 2 * KD + 3 + 4 * (k - 1))
                        sy.dma_start(out=win[:, (k % 2) * 1024:(k % 2) * 1024 + ncols],
                                     in_=src_dram[k * 128:(k + 1) * 128, :]).then_inc(s_lw[k % 2], 16)

                @block.vector
                def _(v):
                    v.wait_ge(s_lt, blt + KD + 1)
                    base = blv + 2 * KD + 3
                    for k in range(KD):
                        v.wait_ge(s_lw[k % 2], blw[k % 2] + 16 * (KD // 2 + k // 2 + 1))
                        if k >= 1:
                            v.wait_ge(s_lv, base + 4 * k)
                        for sp, (lo, hi) in enumerate(splits):
                            v.tensor_tensor(dbuf[:, sp * 512:sp * 512 + hi - lo],
                                            win[:, (k % 2) * 1024 + lo:(k % 2) * 1024 + hi],
                                            pmu[sp][:, :], ALU.subtract).then_inc(s_lv, 1)
                        v.wait_ge(s_lv, base + 4 * k + 2)
                        for sp, (lo, hi) in enumerate(splits):
                            v.tensor_tensor(xh[:, k * colstride + dst_off + lo:k * colstride + dst_off + hi],
                                            dbuf[:, sp * 512:sp * 512 + hi - lo],
                                            pr[sp][:, :], ALU.mult).then_inc(s_lv, 1)

        def emit_ffn_u(tag, xh, hid):
            s_fw = (sem("fuw0"), sem("fuw1"))
            s_ft = sem("fut")
            s_fs = sem("fus")
            s_fv = sem("fuv")
            bfw = (takebase("fuw0", 16 * 64), takebase("fuw1", 16 * 64))
            bft = takebase("fut", 4 * KE)
            bfs = takebase("fus", 2 * KE)
            bfv = takebase("fuv", 2 * KE)
            with (
                nc.sbuf_tensor(tag + "tu", [128, 2 * 2048], BF16) as tub,
                nc.sbuf_tensor(tag + "tg", [128, 2 * 2048], BF16) as tgb,
                nc.sbuf_tensor(tag + "sil", [128, 2048], F32) as sil,
                nc.psum_tensor(tag + "pu0a", [128, 512], F32) as pu0a,
                nc.psum_tensor(tag + "pu0b", [128, 512], F32) as pu0b,
                nc.psum_tensor(tag + "pu1a", [128, 512], F32) as pu1a,
                nc.psum_tensor(tag + "pu1b", [128, 512], F32) as pu1b,
                nc.psum_tensor(tag + "pg0a", [128, 512], F32) as pg0a,
                nc.psum_tensor(tag + "pg0b", [128, 512], F32) as pg0b,
                nc.psum_tensor(tag + "pg1a", [128, 512], F32) as pg1a,
                nc.psum_tensor(tag + "pg1b", [128, 512], F32) as pg1b,
            ):
                pu = ((pu0a, pu0b), (pu1a, pu1b))
                pg = ((pg0a, pg0b), (pg1a, pg1b))

                @block.sync
                def _(sy):
                    sy.wait_ge(s_ag, 2)
                    for e in range(KE):
                        if e >= 2:
                            sy.wait_ge(s_ft, bft + 4 * (e - 1))
                        sy.dma_start(out=s3(tub[:, (e % 2) * 2048:(e % 2) * 2048 + 2048], 16),
                                     in_=r3(tup_block(e), 16)).then_inc(s_fw[e % 2], 16)
                        sy.dma_start(out=s3(tgb[:, (e % 2) * 2048:(e % 2) * 2048 + 2048], 16),
                                     in_=r3(tg_block(e), 16)).then_inc(s_fw[e % 2], 16)

                @block.tensor
                def _(t):
                    for e in range(KE):
                        t.wait_ge(s_fw[e % 2], bfw[e % 2] + 32 * (e // 2 + 1))
                        if e >= 2:
                            t.wait_ge(s_fs, bfs + 2 * (e - 1))
                            t.wait_ge(s_fv, bfv + 2 * (e - 1))
                        tu = tub[:, (e % 2) * 2048:(e % 2) * 2048 + 2048]
                        tg = tgb[:, (e % 2) * 2048:(e % 2) * 2048 + 2048]
                        for sp in range(2):
                            for k in range(KD):
                                mm = t.matmul(pu[e % 2][sp][:, :],
                                              tu[:, k * 128:(k + 1) * 128],
                                              xh[:, k * 1024 + sp * 512:k * 1024 + sp * 512 + 512],
                                              start=(k == 0), stop=(k == KD - 1))
                            mm.then_inc(s_ft, 1)
                        for sp in range(2):
                            for k in range(KD):
                                mm = t.matmul(pg[e % 2][sp][:, :],
                                              tg[:, k * 128:(k + 1) * 128],
                                              xh[:, k * 1024 + sp * 512:k * 1024 + sp * 512 + 512],
                                              start=(k == 0), stop=(k == KD - 1))
                            mm.then_inc(s_ft, 1)

                @block.scalar
                def _(s):
                    for e in range(KE):
                        if e >= 2:
                            s.wait_ge(s_fv, bfv + 2 * (e - 1))
                        for sp in range(2):
                            s.wait_ge(s_ft, bft + 4 * e + sp + 1)
                            s.activation(sil[:, (e % 2) * 1024 + sp * 512:(e % 2) * 1024 + sp * 512 + 512],
                                         pu[e % 2][sp][:, :],
                                         AF.Silu, scale=v_su[:]).then_inc(s_fs, 1)

                @block.vector
                def _(v):
                    for e in range(KE):
                        for sp in range(2):
                            v.wait_ge(s_fs, bfs + 2 * e + sp + 1)
                            v.wait_ge(s_ft, bft + 4 * e + 2 + sp + 1)
                            v.tensor_tensor(hid[:, e * 1024 + sp * 512:e * 1024 + sp * 512 + 512],
                                            sil[:, (e % 2) * 1024 + sp * 512:(e % 2) * 1024 + sp * 512 + 512],
                                            pg[e % 2][sp][:, :],
                                            ALU.mult).then_inc(s_fv, 1)

                phase_end()

        def emit_ffn_d(tag, hid, resid_dram, dst_dram, also_out, save_tail):
            s_dw = (sem("fdw0"), sem("fdw1"))
            s_du = (sem("fdu0"), sem("fdu1"))
            s_dt = sem("fdt")
            s_dv = sem("fdv")
            s_dst = (sem("fdst0"), sem("fdst1"))
            s_done = sem("fddone")
            per_m = 1 + (1 if also_out else 0) + (1 if save_tail else 0)
            stc = [[0] * 16, [0] * 16]
            c0 = c1 = 0
            for m in range(16):
                if m % 2 == 0:
                    c0 += per_m
                else:
                    c1 += per_m
                stc[0][m] = c0
                stc[1][m] = c1
            bdw = (takebase("fdw0", 16 * 8), takebase("fdw1", 16 * 8))
            bdu = (takebase("fdu0", 16 * 8), takebase("fdu1", 16 * 8))
            bdt = takebase("fdt", 32)
            bdv = takebase("fdv", 32)
            bdst = (takebase("fdst0", 16 * stc[0][-1]), takebase("fdst1", 16 * stc[1][-1]))
            bdone = takebase("fddone", 1)
            with (
                nc.sbuf_tensor(tag + "td", [128, 2 * 8192], BF16) as tdb,
                nc.sbuf_tensor(tag + "ub", [128, 2 * 1024], F32) as ub,
                nc.sbuf_tensor(tag + "x1b", [128, 2 * 1024], F32) as x1b,
                nc.psum_tensor(tag + "pd0a", [128, 512], F32) as pd0a,
                nc.psum_tensor(tag + "pd0b", [128, 512], F32) as pd0b,
                nc.psum_tensor(tag + "pd1a", [128, 512], F32) as pd1a,
                nc.psum_tensor(tag + "pd1b", [128, 512], F32) as pd1b,
            ):
                pd = ((pd0a, pd0b), (pd1a, pd1b))

                @block.sync
                def _(sy):
                    sy.wait_ge(s_ag, 3)
                    for m in range(16):
                        if m >= 2:
                            sy.wait_ge(s_dt, bdt + 2 * (m - 1))
                        sy.dma_start(out=s3(tdb[:, (m % 2) * 8192:(m % 2) * 8192 + 8192], 64),
                                     in_=r3(td_block(m), 64)).then_inc(s_dw[m % 2], 16)
                        if m >= 2:
                            sy.wait_ge(s_dv, bdv + 2 * (m - 1))
                        sy.dma_start(out=ub[:, (m % 2) * 1024:(m % 2) * 1024 + 1024],
                                     in_=resid_dram[m * 128:(m + 1) * 128, :]).then_inc(s_du[m % 2], 16)

                @block.tensor
                def _(t):
                    for m in range(16):
                        t.wait_ge(s_dw[m % 2], bdw[m % 2] + 16 * (m // 2 + 1))
                        if m >= 2:
                            t.wait_ge(s_dv, bdv + 2 * (m - 1))
                        td = tdb[:, (m % 2) * 8192:(m % 2) * 8192 + 8192]
                        for sp in range(2):
                            for e in range(KE):
                                mm = t.matmul(pd[m % 2][sp][:, :],
                                              td[:, e * 128:(e + 1) * 128],
                                              hid[:, e * 1024 + sp * 512:e * 1024 + sp * 512 + 512],
                                              start=(e == 0), stop=(e == KE - 1))
                            mm.then_inc(s_dt, 1)

                @block.vector
                def _(v):
                    for m in range(16):
                        v.wait_ge(s_du[m % 2], bdu[m % 2] + 16 * (m // 2 + 1))
                        if m >= 2:
                            v.wait_ge(s_dst[m % 2], bdst[m % 2] + 16 * stc[m % 2][m - 2])
                        for sp in range(2):
                            v.wait_ge(s_dt, bdt + 2 * m + sp + 1)
                            v.scalar_tensor_tensor(
                                x1b[:, (m % 2) * 1024 + sp * 512:(m % 2) * 1024 + sp * 512 + 512],
                                pd[m % 2][sp][:, :],
                                v_sdg[:],
                                ub[:, (m % 2) * 1024 + sp * 512:(m % 2) * 1024 + sp * 512 + 512],
                                ALU.mult, ALU.add).then_inc(s_dv, 1)

                @block.scalar
                def _(s):
                    for m in range(16):
                        s.wait_ge(s_dv, bdv + 2 * (m + 1))
                        xb = x1b[:, (m % 2) * 1024:(m % 2) * 1024 + 1024]
                        s.dma_start(out=dst_dram[m * 128:(m + 1) * 128, :],
                                    in_=xb).then_inc(s_dst[m % 2], 16)
                        if also_out:
                            s.dma_start(out=out[m * 128:(m + 1) * 128, :],
                                        in_=xb).then_inc(s_dst[m % 2], 16)
                        if save_tail:
                            s.dma_start(out=ag2_in.ap()[m * 128:(m + 1) * 128, 0:4],
                                        in_=xb[:, 1020:1024]).then_inc(s_dst[m % 2], 16)
                    s.wait_ge(s_dst[0], bdst[0] + 16 * stc[0][-1])
                    s.wait_ge(s_dst[1], bdst[1] + 16 * stc[1][-1])
                    s.sem_inc(s_done, 1)
            return (s_done, bdone + 1)

        with nc.sbuf_tensor("hid1", [128, KE * 1024], BF16) as hid1:
            with nc.sbuf_tensor("xh1", [128, KD * 1024], BF16) as xh1:
                emit_ln("l1", u_fm, xh1)
                phase_end()
                if stage < 0.7:
                    emit_dummy_out()
                    return nc
                emit_ffn_u("f1", xh1, hid1)

                @block.gpsimd
                def _(g):
                    g.wait_ge(s_ts[0], 576)
                    g.wait_ge(s_ts[1], 576)
                    g.collective_compute("AllGather", ALU.bypass,
                                         ins=[ag1_in.ap()[32768:49152, :].opt()],
                                         outs=[ag1b_out.ap().opt()],
                                         replica_groups=rg8).then_inc(s_ag, 1)

                if stage < 0.9:
                    emit_dummy_out()
                    return nc
            f1done, f1cnt = emit_ffn_d("f1", hid1, u_fm, xs1, stage == 1, True)

        @block.gpsimd
        def _(g):
            g.wait_ge(f1done, f1cnt)
            g.collective_compute("AllGather", ALU.bypass, ins=[ag2_in.ap().opt()],
                                 outs=[ag2_out.ap().opt()],
                                 replica_groups=rg8).then_inc(s_ag, 1)

        phase_end()

        s_dbg = sem("dbg")

        @block.sync
        def _(sy):
            sy.dma_start(out=dbg[0:1, :], in_=s_row[:, :]).then_inc(s_dbg, 16)
            sy.dma_start(out=dbg[1:2, 0:1], in_=v_su[0:1, :]).then_inc(s_dbg, 16)
            sy.dma_start(out=dbg[2:3, 0:1], in_=v_sdg[0:1, :]).then_inc(s_dbg, 16)
            sy.dma_start(out=dbg[3:4, 0:1], in_=v_thu[0:1, :]).then_inc(s_dbg, 16)
            sy.wait_ge(s_dbg, 64)

        def emit_scan(stage):
            import collections
            CNT = collections.defaultdict(int)

            def E(instr, s, key, n=1):
                CNT[key] += n
                instr.then_inc(s, n)
                return CNT[key]

            s2v = sem("s2v")
            s2s = sem("s2s")
            s2t = sem("s2t")
            s2l = sem("s2l")

            @block.vector
            def _(v):
                E(v.tensor_scalar(dAf[:], dtf[:], c_av[:], None, ALU.mult), s2v, "2v")
                v.wait_ge(s2v, CNT["2v"])
                for c in range(NCH):
                    E(v.tensor_tensor_scan(dAcs[:, 128 * c:128 * c + 128],
                                           dAf[:, 128 * c:128 * c + 128],
                                           dAf[:, 128 * c:128 * c + 128],
                                           0.0, ALU.add, ALU.bypass), s2v, "2v")
                v.wait_ge(s2v, CNT["2v"])
                dv3 = dAcs[:].rearrange("h (c l) -> h c l", c=NCH)
                E(v.tensor_copy(lasts[:, 0:8].rearrange("h (c o) -> h c o", o=1),
                                dv3[:, :, 127:128]), s2v, "2v")
                v.wait_ge(s2v, CNT["2v"])
                E(v.tensor_tensor_scan(lasts[:, 8:16], lasts[:, 0:8], lasts[:, 0:8],
                                       0.0, ALU.add, ALU.bypass), s2v, "2v")
                v.wait_ge(s2v, CNT["2v"])
                E(v.tensor_scalar(lasts[:, 16:24], lasts[:, 8:16], -1.0,
                                  lasts[:, 15:16], ALU.mult, ALU.add), s2v, "2v")
            V_DACS = 9
            V_TAIL = CNT["2v"]

            @block.scalar
            def _(s):
                s.wait_ge(s2v, V_DACS)
                for c in range(NCH):
                    E(s.activation(decayf[:, 128 * c:128 * c + 128],
                                   dAcs[:, 128 * c:128 * c + 128], AF.Exp,
                                   bias=dAcs[:, 128 * c + 127:128 * c + 128],
                                   scale=c_fc[0:64, 3:4]), s2s, "2s")
                s.wait_ge(s2v, V_TAIL)
                E(s.activation(tpf[:], lasts[:, 16:24], AF.Exp), s2s, "2s")
                E(s.activation(cdf[:], lasts[:, 0:8], AF.Exp), s2s, "2s")
            S_DEC = 8
            S_CD = CNT["2s"]

            @block.vector
            def _(v):
                v.wait_ge(s2s, S_DEC)
                E(v.tensor_tensor(dtdyf[:], dtf[:], decayf[:], ALU.mult), s2v, "2v")
                v.wait_ge(s2v, CNT["2v"])
                v.wait_ge(s2s, S_DEC + 2)
                for c in range(NCH):
                    E(v.tensor_scalar(dtdtf[:, 128 * c:128 * c + 128],
                                      dtdyf[:, 128 * c:128 * c + 128],
                                      tpf[:, c:c + 1], None, ALU.mult), s2v, "2v")
            V_DTDT = CNT["2v"]

            @block.sync
            def _(sy):
                sy.wait_ge(s2s, S_CD)
                sy.dma_start(out=cdrow[:], in_=cdf[:]).then_inc(s2l, 16)

            with (
                nc.psum_tensor("p2c", [128, 512], F32) as p2c,
                nc.psum_tensor("p2t0", [128, 512], F32) as p2t0,
                nc.psum_tensor("p2t1", [128, 512], F32) as p2t1,
            ):
                p2t = (p2t0, p2t1)

                @block.tensor
                def _(t):
                    t.wait_ge(s2l, 16)
                    E(t.matmul(p2c[:], c_onesf[0:1, 0:128], cdrow[:],
                               start=True, stop=True), s2t, "2t")
                    t.wait_ge(s2v, V_DTDT)
                    for c in range(NCH):
                        if c >= 2:
                            t.wait_ge(s2v, V_DTDT + 1 + 4 * (c - 1))
                        for si, srcm in enumerate((dAcs, dtf, dtdyf, dtdtf)):
                            E(t.matmul(p2t[c % 2][:, 128 * si:128 * si + 64],
                                       srcm[:, 128 * c:128 * c + 128],
                                       c_identf[0:64, 0:64], start=True, stop=True,
                                       is_transpose=True), s2t, "2t")

                @block.vector
                def _(v):
                    v.wait_ge(s2t, 1)
                    E(v.tensor_copy(cdb[:], p2c[:]), s2v, "2v")
                    for c in range(NCH):
                        v.wait_ge(s2t, 1 + 4 * (c + 1))
                        for si, dstm in enumerate((datok, dttok, dctok, ddtok)):
                            E(v.tensor_copy(dstm[:, 64 * c:64 * c + 64],
                                            p2t[c % 2][:, 128 * si:128 * si + 64]),
                              s2v, "2v")
                phase_end()

            s3l = (sem("s3l0"), sem("s3l1"))
            s3t = sem("s3t")
            s3e = sem("s3e")
            s3v2 = sem("s3v2")
            s3t2 = sem("s3t2")
            s3a = sem("s3a")
            with (
                nc.sbuf_tensor("xbw", [128, 2 * 2048], BF16) as xbw,
                nc.sbuf_tensor("xdb1", [128, 2 * 512], BF16) as xdb1,
                nc.sbuf_tensor("scac", [128, 4096], F32) as scac,
                nc.sbuf_tensor("ag3st", [128, 4096], BF16) as ag3st,
                nc.psum_tensor("pT0", [128, 512], BF16) as pT0,
                nc.psum_tensor("pT1", [128, 512], BF16) as pT1,
                nc.psum_tensor("p3a0", [128, 512], F32) as p3a0,
                nc.psum_tensor("p3a1", [128, 512], F32) as p3a1,
            ):
                pT = (pT0, pT1)
                p3a = (p3a0, p3a1)

                @block.sync
                def _(sy):
                    for c in range(NCH):
                        if c >= 2:
                            sy.wait_ge(s3t, 16 * (c - 1))
                        sy.dma_start(
                            out=xbw[:, (c % 2) * 2048:(c % 2) * 2048 + 2048].rearrange(
                                "p (j w) -> p j w", j=16),
                            in_=xbcsc.ap()[0:2048, 128 * c:128 * c + 128].rearrange(
                                "(j p) w -> p j w", p=128)).then_inc(s3l[c % 2], 16)

                @block.tensor
                def _(t):
                    for c in range(NCH):
                        t.wait_ge(s3l[c % 2], 16 * (c // 2 + 1))
                        for f in range(4):
                            Tf = 4 * c + f
                            if Tf >= 2:
                                t.wait_ge(s3e, Tf - 1)
                            for i in range(4):
                                E(t.matmul(pT[Tf % 2][:, 128 * i:128 * i + 128],
                                           xbw[:, (c % 2) * 2048 + 128 * (4 * f + i):
                                               (c % 2) * 2048 + 128 * (4 * f + i) + 128],
                                           c_identb[:], start=True, stop=True,
                                           is_transpose=True), s3t, "3t")

                @block.vector
                def _(v):
                    for c in range(NCH):
                        for f in range(4):
                            Tf = 4 * c + f
                            v.wait_ge(s3t, 4 * (Tf + 1))
                            if f < 2:
                                dst = xtokb[:, 1024 * c + 512 * f:1024 * c + 512 * f + 512]
                            else:
                                dst = btokb[:, 1024 * c + 512 * (f - 2):
                                            1024 * c + 512 * (f - 2) + 512]
                            E(v.tensor_copy(dst, pT[Tf % 2][:]), s3e, "3e")

                def emit_accum(it2):
                    c, hg = it2 // 8, it2 % 8

                    @block.vector
                    def _(v):
                        v.wait_ge(s3t2, it2 + 1)
                        if c == 0:
                            E(v.tensor_copy(scac[:, 512 * hg:512 * hg + 512],
                                            p3a[it2 % 2][:]), s3a, "3a")
                        else:
                            v.wait_ge(s3a, 8 * (c - 1) + hg + 1)
                            E(v.scalar_tensor_tensor(scac[:, 512 * hg:512 * hg + 512],
                                                     p3a[it2 % 2][:], 1.0,
                                                     scac[:, 512 * hg:512 * hg + 512],
                                                     ALU.mult, ALU.add), s3a, "3a")

                for it2 in range(64):
                    c, hg = it2 // 8, it2 % 8

                    @block.vector
                    def _(v):
                        if it2 == 0:
                            v.wait_ge(s3e, 32)
                        if it2 >= 2:
                            v.wait_ge(s3t2, it2 - 1)
                        for hh in range(8):
                            h = 8 * hg + hh
                            v.tensor_scalar(
                                xdb1[:, (it2 % 2) * 512 + 64 * hh:(it2 % 2) * 512 + 64 * hh + 64],
                                xtokb[:, 1024 * c + 128 * hg + 64 * (hh % 2):
                                      1024 * c + 128 * hg + 64 * (hh % 2) + 64],
                                ddtok[:, 64 * c + h:64 * c + h + 1], None,
                                ALU.mult).then_inc(s3v2, 1)

                    @block.tensor
                    def _(t):
                        t.wait_ge(s3v2, 8 * (it2 + 1))
                        if it2 >= 2:
                            t.wait_ge(s3a, it2 - 1)
                        for hh in range(8):
                            mm = t.matmul(p3a[it2 % 2][:, 64 * hh:64 * hh + 64],
                                          btokb[:, 1024 * c + 128 * hg:
                                                1024 * c + 128 * hg + 128],
                                          xdb1[:, (it2 % 2) * 512 + 64 * hh:
                                               (it2 % 2) * 512 + 64 * hh + 64],
                                          start=True, stop=True)
                        mm.then_inc(s3t2, 1)

                    if it2 >= 1:
                        emit_accum(it2 - 1)
                emit_accum(63)

                @block.vector
                def _(v):
                    v.wait_ge(s3a, 64)
                    E(v.tensor_copy(ag3st[:], scac[:]), s3a, "3a")

                @block.sync
                def _(sy):
                    sy.wait_ge(s3a, 65)
                    sy.dma_start(out=ag3_in.ap(), in_=ag3st[:]).then_inc(s3l[0], 16)

                @block.gpsimd
                def _(g):
                    g.wait_ge(s3l[0], 16 * 5)
                    g.collective_compute("AllGather", ALU.bypass, ins=[ag3_in.ap().opt()],
                                         outs=[ag3_out.ap().opt()],
                                         replica_groups=rg8).then_inc(s_ag, 1)
                phase_end()

            s4v = sem("s4v")
            s3l_base = (16 * 5, 16 * 4)
            with nc.sbuf_tensor("agw", [128, 2 * 4096], BF16) as agw:
                @block.sync
                def _(sy):
                    sy.wait_ge(s_ag, 5)
                    for r in range(8):
                        if r >= 2:
                            sy.wait_ge(s4v, r - 1)
                        sy.dma_start(out=agw[:, (r % 2) * 4096:(r % 2) * 4096 + 4096],
                                     in_=ag3_out.ap()[128 * r:128 * r + 128, :]
                                     ).then_inc(s3l[r % 2], 16)

                @block.vector
                def _(v):
                    for r in range(8):
                        v.wait_ge(s3l[r % 2], s3l_base[r % 2] + 16 * (r // 2 + 1))
                        if r == 0:
                            E(v.tensor_scalar(carry[:], agw[:, 0:4096],
                                              c_hmask[:, 0:1], None, ALU.mult),
                              s4v, "4v")
                        else:
                            v.wait_ge(s4v, r)
                            E(v.scalar_tensor_tensor(carry[:],
                                                     agw[:, (r % 2) * 4096:(r % 2) * 4096 + 4096],
                                                     c_hmask[:, r:r + 1], carry[:],
                                                     ALU.mult, ALU.add), s4v, "4v")
                    v.wait_ge(s4v, 8)
                    E(v.tensor_copy(carryb[:], carry[:]), s4v, "4v")
                phase_end()

            s5l = sem("s5l")
            s5g = (sem("s5g0"), sem("s5g1"))
            s5t = sem("s5t")
            s5v = sem("s5v")
            s5s = sem("s5s")
            s5p = sem("s5p")
            s5r = sem("s5r")
            s5st = sem("s5st")
            s5w = (sem("s5w0"), sem("s5w1"))
            m5ctx = contextlib.ExitStack()
            with m5ctx:
                xw = m5ctx.enter_context(nc.sbuf_tensor("xw", [128, 48 * 128], BF16))
                szb = m5ctx.enter_context(nc.sbuf_tensor("szb", [128, 4096], BF16))
                x1w = m5ctx.enter_context(nc.sbuf_tensor("x1w", [128, 16 * 128], F32))
                dar = m5ctx.enter_context(nc.sbuf_tensor("dar", [1, 2 * 1024], F32))
                negf = m5ctx.enter_context(nc.sbuf_tensor("negf", [1, 128], F32))
                rcb = m5ctx.enter_context(nc.sbuf_tensor("rcb", [128, 2 * 512], F32))
                ebb = m5ctx.enter_context(nc.sbuf_tensor("ebb", [128, 2 * 512], BF16))
                e2b = m5ctx.enter_context(nc.sbuf_tensor("e2b", [128, 2 * 512], BF16))
                gmb = m5ctx.enter_context(nc.sbuf_tensor("gmb", [128, 2 * 256], BF16))
                glb = m5ctx.enter_context(nc.sbuf_tensor("glb", [128, 2 * 512], BF16))
                cvb = m5ctx.enter_context(nc.sbuf_tensor("cvb", [128, 2 * 512], BF16))
                xdbv = m5ctx.enter_context(nc.sbuf_tensor("xdbv", [128, 2 * 256], BF16))
                sqb2 = m5ctx.enter_context(nc.sbuf_tensor("sqb2", [128, 2 * 256], BF16))
                ybb = m5ctx.enter_context(nc.sbuf_tensor("ybb", [128, 4096], BF16))
                yhb = m5ctx.enter_context(nc.sbuf_tensor("yhb", [128, 4096], BF16))
                sdb = m5ctx.enter_context(nc.sbuf_tensor("sdb", [32, 128], F32))
                sdbb = m5ctx.enter_context(nc.sbuf_tensor("sdbb", [32, 128], BF16))
                rrow = m5ctx.enter_context(nc.sbuf_tensor("rrow", [1, 4096], BF16))
                wob = m5ctx.enter_context(nc.sbuf_tensor("wob", [128, 2 * 4096], BF16))
                x2w = m5ctx.enter_context(nc.sbuf_tensor("x2w", [128, 16 * 128], F32))
                pR = (m5ctx.enter_context(nc.psum_tensor("pR0", [128, 512], F32)),
                      m5ctx.enter_context(nc.psum_tensor("pR1", [128, 512], F32)))
                pEB = (m5ctx.enter_context(nc.psum_tensor("pEB0", [128, 512], F32)),
                       m5ctx.enter_context(nc.psum_tensor("pEB1", [128, 512], F32)))
                pG2 = m5ctx.enter_context(nc.psum_tensor("pG2", [128, 512], F32))
                pY2 = m5ctx.enter_context(nc.psum_tensor("pY2", [128, 512], F32))
                pS2 = m5ctx.enter_context(nc.psum_tensor("pS2", [128, 512], F32))
                pGN = m5ctx.enter_context(nc.psum_tensor("pGN", [32, 512], F32))

                @block.vector
                def _(v):
                    E(v.tensor_scalar(negf[:], c_onesf[0:1, 0:128], -1.0, None,
                                      ALU.mult), s5v, "5v")
                NEG_V = CNT["5v"]

                for c in range(NCH):
                    @block.sync
                    def _(sy):
                        sy.dma_start(
                            out=xw[:].rearrange("p (j w) -> p j w", j=48),
                            in_=xbcsc.ap()[:, 128 * c:128 * c + 128].rearrange(
                                "(j p) w -> p j w", p=128)).then_inc(s5l, 16)
                        sy.dma_start(
                            out=szb[:].rearrange("p (j w) -> p j w", j=32),
                            in_=zsc.ap()[:, 128 * c:128 * c + 128].rearrange(
                                "(j p) w -> p j w", p=128)).then_inc(s5l, 16)
                        sy.dma_start(
                            out=x1w[:].rearrange("p (j w) -> p j w", j=16),
                            in_=xs1.ap()[:, 128 * c:128 * c + 128].rearrange(
                                "(j p) w -> p j w", p=128)).then_inc(s5l, 16)
                    CNT["5l"] += 3
                    lw_all = CNT["5l"]

                    @block.scalar
                    def _(s):
                        s.wait_ge(s5l, 16 * lw_all)
                        E(s.activation(szb[:], szb[:], AF.Silu), s5s, "5s")
                    sz_cnt = CNT["5s"]

                    t1r = {}
                    t1x = {}
                    v1c = {}
                    v2a = {}
                    s1a = {}
                    s1c = {}
                    gpc = {}
                    t2c = {}
                    v3c = {}
                    v3y = {}
                    v2b = {}
                    gp2 = {}
                    darn = {}

                    def emit_T1(g):
                        s_ = g % 2
                        hg = g // 2
                        if g % 2 == 0:
                            gate = t1x.get(2 * (hg - 2) + 1)

                            @block.sync
                            def _(sy):
                                if gate is not None:
                                    sy.wait_ge(s5t, gate)
                                sy.dma_start(
                                    out=dar[0:1, (hg % 2) * 1024:(hg % 2) * 1024 + 1024],
                                    in_=dAcs[8 * hg:8 * hg + 8, 128 * c:128 * c + 128]
                                    ).then_inc(s5g[hg % 2], 16)
                            CNT["5g%d" % (hg % 2)] += 1
                            darn[hg] = CNT["5g%d" % (hg % 2)]

                        @block.tensor
                        def _(t):
                            t.wait_ge(s5g[hg % 2], 16 * darn[hg])
                            t.wait_ge(s5l, 16 * lw_all)
                            t.wait_ge(s5v, v2b.get(g - 2, NEG_V))
                            ws = s1c.get(g - 2)
                            if ws is not None:
                                t.wait_ge(s5s, ws)
                            mm = None
                            for i in range(4):
                                hh = 4 * s_ + i
                                dsl = dar[0:1, (hg % 2) * 1024 + 128 * hh:
                                          (hg % 2) * 1024 + 128 * hh + 128]
                                t.matmul(pR[s_][:, 128 * i:128 * i + 128],
                                         c_onesf[0:1, 0:128], dsl,
                                         start=True, stop=False)
                                mm = t.matmul(pR[s_][:, 128 * i:128 * i + 128],
                                              dsl, negf[0:1, 0:128],
                                              start=False, stop=True)
                            E(mm, s5t, "5t")
                            t1r[g] = CNT["5t"]
                            for i in range(4):
                                hh = 4 * s_ + i
                                dsl = dar[0:1, (hg % 2) * 1024 + 128 * hh:
                                          (hg % 2) * 1024 + 128 * hh + 128]
                                mm = t.matmul(pEB[s_][:, 128 * i:128 * i + 128],
                                              c_onesf[0:1, 0:128], dsl,
                                              start=True, stop=True)
                            for jj in range(2):
                                j = 2 * g + jj
                                mm = t.matmul(pG2[:, 256 * s_ + 128 * jj:
                                                  256 * s_ + 128 * jj + 128],
                                              xw[:, 128 * (8 + hg):128 * (8 + hg) + 128],
                                              xw[:, 128 * (16 + j):128 * (16 + j) + 128],
                                              start=True, stop=True)
                            E(mm, s5t, "5t")
                            t1x[g] = CNT["5t"]

                    def emit_V1(g):
                        s_ = g % 2
                        hg = g // 2

                        @block.vector
                        def _(v):
                            v.wait_ge(s5t, t1r[g])
                            E(v.tensor_scalar(rcb[:, 512 * s_:512 * s_ + 512],
                                              pR[s_][:], 0.0, None, ALU.min),
                              s5v, "5v")
                            v1c[g] = CNT["5v"]
                            wg = t2c.get(g - 2)
                            if wg is not None:
                                v.wait_ge(s5t, wg)
                            for i in range(4):
                                h = 4 * g + i
                                ro = 64 * (i % 2)
                                E(v.tensor_scalar(xdbv[:, 256 * s_ + 64 * i:
                                                       256 * s_ + 64 * i + 64],
                                                  xtokb[:, 1024 * c + 128 * hg + ro:
                                                        1024 * c + 128 * hg + ro + 64],
                                                  dctok[:, 64 * c + h:64 * c + h + 1],
                                                  None, ALU.mult), s5v, "5v")
                            v2a[g] = CNT["5v"]

                    def emit_S1(g):
                        s_ = g % 2

                        @block.scalar
                        def _(s):
                            s.wait_ge(s5v, v1c[g])
                            s.wait_ge(s5t, t1x[g])
                            E(s.activation(ebb[:, 512 * s_:512 * s_ + 512],
                                           rcb[:, 512 * s_:512 * s_ + 512], AF.Exp),
                              s5s, "5s")
                            s1a[g] = CNT["5s"]
                            E(s.activation(e2b[:, 512 * s_:512 * s_ + 512],
                                           pEB[s_][:], AF.Exp), s5s, "5s")
                            s1c[g] = CNT["5s"]

                    def emit_GPc(g):
                        s_ = g % 2

                        @block.gpsimd
                        def _(gp):
                            gp.wait_ge(s5s, s1c[g])
                            wg = t2c.get(g - 2)
                            if wg is not None:
                                gp.wait_ge(s5t, wg)
                            for i in range(4):
                                j = 2 * g + i // 2
                                E(gp.tensor_tensor(cvb[:, 512 * s_ + 128 * i:
                                                       512 * s_ + 128 * i + 128],
                                                   xw[:, 128 * (16 + j):128 * (16 + j) + 128],
                                                   e2b[:, 512 * s_ + 128 * i:
                                                       512 * s_ + 128 * i + 128],
                                                   ALU.mult), s5p, "5p")
                            gpc[g] = CNT["5p"]

                    def emit_V3(g):
                        s_ = g % 2

                        @block.vector
                        def _(v):
                            v.wait_ge(s5t, t2c[g])
                            for i in range(4):
                                h = 4 * g + i
                                E(v.tensor_scalar(carry[:, 64 * h:64 * h + 64],
                                                  carry[:, 64 * h:64 * h + 64],
                                                  cdb[:, 8 * h + c:8 * h + c + 1],
                                                  None, ALU.mult), s5v, "5v")
                            v.wait_ge(s5v, CNT["5v"])
                            E(v.scalar_tensor_tensor(carry[:, 256 * g:256 * g + 256],
                                                     pS2[:, 256 * s_:256 * s_ + 256], 1.0,
                                                     carry[:, 256 * g:256 * g + 256],
                                                     ALU.mult, ALU.add), s5v, "5v")
                            v3c[g] = CNT["5v"]
                            v.wait_ge(s5s, sz_cnt)
                            E(v.tensor_tensor(ybb[:, 256 * g:256 * g + 256],
                                              pY2[:, 256 * s_:256 * s_ + 256],
                                              szb[:, 256 * g:256 * g + 256],
                                              ALU.mult), s5v, "5v")
                            v3y[g] = CNT["5v"]

                    def emit_V2b(g):
                        s_ = g % 2

                        @block.vector
                        def _(v):
                            v.wait_ge(s5s, s1a[g])
                            v.wait_ge(s5t, t1x[g])
                            for jj in range(2):
                                E(v.tensor_tensor(gmb[:, 256 * s_ + 128 * jj:
                                                      256 * s_ + 128 * jj + 128],
                                                  pG2[:, 256 * s_ + 128 * jj:
                                                      256 * s_ + 128 * jj + 128],
                                                  c_trilb[:], ALU.mult), s5v, "5v")
                            v.wait_ge(s5v, CNT["5v"])
                            for i in range(4):
                                h = 4 * g + i
                                jj = i // 2
                                E(v.scalar_tensor_tensor(
                                    glb[:, 512 * s_ + 128 * i:512 * s_ + 128 * i + 128],
                                    ebb[:, 512 * s_ + 128 * i:512 * s_ + 128 * i + 128],
                                    dttok[:, 64 * c + h:64 * c + h + 1],
                                    gmb[:, 256 * s_ + 128 * jj:256 * s_ + 128 * jj + 128],
                                    ALU.mult, ALU.mult), s5v, "5v")
                            v2b[g] = CNT["5v"]

                    def emit_T2(g):
                        s_ = g % 2
                        hg = g // 2

                        @block.tensor
                        def _(t):
                            t.wait_ge(s5v, v2b[g])
                            t.wait_ge(s5p, gpc[g])
                            wg = gp2.get(g - 1)
                            if wg is not None:
                                t.wait_ge(s5p, wg)
                            if g >= 1:
                                for jj in range(2):
                                    jp = 2 * (g - 1) + jj
                                    t.matmul(pGN[:, 0:128], c_gnoh[:, 32 * jp:32 * jp + 32],
                                             sqb2[:, 256 * ((g - 1) % 2) + 128 * jj:
                                                  256 * ((g - 1) % 2) + 128 * jj + 128],
                                             start=(jp == 0), stop=(jp == 31))
                            mm = None
                            for i in range(4):
                                h = 4 * g + i
                                jj = i // 2
                                j = 2 * g + jj
                                ro = 64 * (i % 2)
                                yt = pY2[ro:ro + 64, 256 * s_ + 128 * jj:
                                         256 * s_ + 128 * jj + 128]
                                t.matmul(yt,
                                         xtokb[:, 1024 * c + 128 * hg + ro:
                                               1024 * c + 128 * hg + ro + 64],
                                         glb[:, 512 * s_ + 128 * i:512 * s_ + 128 * i + 128],
                                         start=True, stop=False)
                                t.matmul(yt, carryb[:, 64 * h:64 * h + 64],
                                         cvb[:, 512 * s_ + 128 * i:512 * s_ + 128 * i + 128],
                                         start=False, stop=False)
                                t.matmul(yt,
                                         c_diagd[ro:ro + 64, 128 * j + ro:128 * j + ro + 64],
                                         xw[ro:ro + 64, 128 * hg:128 * hg + 128],
                                         start=False, stop=True)
                                mm = t.matmul(pS2[:, 256 * s_ + 64 * i:256 * s_ + 64 * i + 64],
                                              btokb[:, 1024 * c + 128 * hg:
                                                    1024 * c + 128 * hg + 128],
                                              xdbv[:, 256 * s_ + 64 * i:256 * s_ + 64 * i + 64],
                                              start=True, stop=True)
                            E(mm, s5t, "5t")
                            t2c[g] = CNT["5t"]

                    def emit_GP2(g):
                        s_ = g % 2

                        @block.gpsimd
                        def _(gp):
                            gp.wait_ge(s5v, v3y[g])
                            E(gp.tensor_copy(carryb[:, 256 * g:256 * g + 256],
                                             carry[:, 256 * g:256 * g + 256]), s5p, "5p")
                            E(gp.tensor_tensor(sqb2[:, 256 * s_:256 * s_ + 256],
                                               ybb[:, 256 * g:256 * g + 256],
                                               ybb[:, 256 * g:256 * g + 256],
                                               ALU.mult), s5p, "5p")
                            gp2[g] = CNT["5p"]

                    for tk in range(17):
                        gA = tk
                        gB = tk - 1
                        if gA <= 15:
                            emit_T1(gA)
                            emit_V1(gA)
                            emit_S1(gA)
                            emit_GPc(gA)
                        if gB >= 0:
                            emit_T2(gB)
                            emit_V3(gB)
                        if gA <= 15:
                            emit_V2b(gA)
                        if gB >= 0:
                            emit_GP2(gB)

                    @block.tensor
                    def _(t):
                        t.wait_ge(s5p, gp2[15])
                        mm = None
                        for jj in range(2):
                            jp = 30 + jj
                            mm = t.matmul(pGN[:, 0:128], c_gnoh[:, 32 * jp:32 * jp + 32],
                                          sqb2[:, 256 * (15 % 2) + 128 * jj:
                                               256 * (15 % 2) + 128 * jj + 128],
                                          start=False, stop=(jp == 31))
                        E(mm, s5t, "5t")
                    tgn = CNT["5t"]

                    @block.scalar
                    def _(s):
                        s.wait_ge(s5t, tgn)
                        E(s.activation(sdb[:], pGN[:, 0:128], AF.Sqrt,
                                       bias=c_fc[0:32, 0:1], scale=c_fc[0:32, 1:2]),
                          s5s, "5s")
                    srs = CNT["5s"]

                    @block.vector
                    def _(v):
                        v.wait_ge(s5s, srs)
                        E(v.reciprocal(sdb[:], sdb[:]), s5v, "5v")
                        v.wait_ge(s5v, CNT["5v"])
                        E(v.tensor_copy(sdbb[:], sdb[:]), s5v, "5v")
                    vrs = CNT["5v"]

                    @block.sync
                    def _(sy):
                        sy.wait_ge(s5v, vrs)
                        sy.dma_start(out=rrow[:], in_=sdbb[:]).then_inc(s5r, 16)
                    CNT["5r"] += 1

                    yh_after = {}
                    for q in range(8):
                        pv = yh_after.get(q - 2)

                        @block.tensor
                        def _(t):
                            t.wait_ge(s5r, 16 * CNT["5r"])
                            if pv is not None:
                                t.wait_ge(s5v, pv)
                            mm = None
                            for qq in range(4):
                                g2 = 4 * q + qq
                                mm = t.matmul(pEB[q % 2][:, 128 * qq:128 * qq + 128],
                                              c_onesb[0:1, 0:128],
                                              rrow[0:1, 128 * g2:128 * g2 + 128],
                                              start=True, stop=True)
                            E(mm, s5t, "5t")
                        trb = CNT["5t"]

                        @block.vector
                        def _(v):
                            v.wait_ge(s5t, trb)
                            for qq in range(4):
                                g2 = 4 * q + qq
                                E(v.scalar_tensor_tensor(yhb[:, 128 * g2:128 * g2 + 128],
                                                         ybb[:, 128 * g2:128 * g2 + 128],
                                                         c_normw[:, g2:g2 + 1],
                                                         pEB[q % 2][:, 128 * qq:128 * qq + 128],
                                                         ALU.mult, ALU.mult), s5v, "5v")
                        yh_after[q] = CNT["5v"]

                    yv_all = CNT["5v"]
                    tom_after = {}
                    oev_after = {}
                    for m in range(16):
                        tp = tom_after.get(m - 2)

                        @block.gpsimd
                        def _(gp):
                            if tp is not None:
                                gp.wait_ge(s5t, tp)
                            gp.dma_start(
                                out=s3(wob[:, (m % 2) * 4096:(m % 2) * 4096 + 4096], 32),
                                in_=r3(w_out_t[m * 4096:(m + 1) * 4096, :], 32)
                                ).then_inc(s5w[m % 2], 16)
                        CNT["5w%d" % (m % 2)] += 1
                        wn = CNT["5w%d" % (m % 2)]
                        ev = oev_after.get(m - 2)

                        @block.tensor
                        def _(t):
                            t.wait_ge(s5w[m % 2], 16 * wn)
                            t.wait_ge(s5v, yv_all)
                            if ev is not None:
                                t.wait_ge(s5v, ev)
                            for kc in range(32):
                                mm = t.matmul(pR[m % 2][:, 0:128],
                                              wob[:, (m % 2) * 4096 + 128 * kc:
                                                  (m % 2) * 4096 + 128 * (kc + 1)],
                                              yhb[:, 128 * kc:128 * kc + 128],
                                              start=(kc == 0), stop=(kc == 31))
                            E(mm, s5t, "5t")
                        tom_after[m] = CNT["5t"]

                        @block.vector
                        def _(v):
                            v.wait_ge(s5t, tom_after[m])
                            E(v.tensor_tensor(x2w[:, 128 * m:128 * m + 128],
                                              pR[m % 2][:, 0:128], x1w[:, 128 * m:128 * m + 128],
                                              ALU.add), s5v, "5v")
                        oev_after[m] = CNT["5v"]

                    vall = CNT["5v"]

                    @block.sync
                    def _(sy):
                        sy.wait_ge(s5v, vall)
                        sy.dma_start(out=xs2.ap()[:, 128 * c:128 * c + 128].rearrange(
                                         "(j p) w -> p j w", p=128),
                                     in_=x2w[:].rearrange("p (j w) -> p j w", j=16)
                                     ).then_inc(s5st, 16)
                        if stage == 2:
                            sy.dma_start(out=out[:, 128 * c:128 * c + 128].rearrange(
                                             "(j p) w -> p j w", p=128),
                                         in_=x2w[:].rearrange("p (j w) -> p j w", j=16)
                                         ).then_inc(s5st, 16)
                    CNT["5st"] += 2 if stage == 2 else 1

                    phase_end(sync_wait=(s5st, 16 * CNT["5st"]))

        if stage >= 2:
            mctx = contextlib.ExitStack()
            xctx = contextlib.ExitStack()
            dtf = mctx.enter_context(nc.sbuf_tensor("dtf", [64, 1024], F32))
            dAf = mctx.enter_context(nc.sbuf_tensor("dAf", [64, 1024], F32))
            dAcs = mctx.enter_context(nc.sbuf_tensor("dAcs", [64, 1024], F32))
            decayf = mctx.enter_context(nc.sbuf_tensor("decayf", [64, 1024], F32))
            dtdyf = mctx.enter_context(nc.sbuf_tensor("dtdyf", [64, 1024], F32))
            dtdtf = mctx.enter_context(nc.sbuf_tensor("dtdtf", [64, 1024], F32))
            lasts = mctx.enter_context(nc.sbuf_tensor("lasts", [64, 24], F32))
            tpf = mctx.enter_context(nc.sbuf_tensor("tpf", [64, 8], F32))
            cdf = mctx.enter_context(nc.sbuf_tensor("cdf", [64, 8], F32))
            cdb = mctx.enter_context(nc.sbuf_tensor("cdb", [128, 512], F32))
            cdrow = mctx.enter_context(nc.sbuf_tensor("cdrow", [1, 512], F32))
            datok = mctx.enter_context(nc.sbuf_tensor("datok", [128, 512], F32))
            dttok = mctx.enter_context(nc.sbuf_tensor("dttok", [128, 512], F32))
            dctok = mctx.enter_context(nc.sbuf_tensor("dctok", [128, 512], F32))
            ddtok = mctx.enter_context(nc.sbuf_tensor("ddtok", [128, 512], F32))
            xtokb = mctx.enter_context(nc.sbuf_tensor("xtokb", [128, 8 * 1024], BF16))
            btokb = mctx.enter_context(nc.sbuf_tensor("btokb", [128, 8 * 1024], BF16))
            carry = mctx.enter_context(nc.sbuf_tensor("carry", [128, 4096], F32))
            carryb = mctx.enter_context(nc.sbuf_tensor("carryb", [128, 4096], BF16))
            xh2 = xctx.enter_context(nc.sbuf_tensor("xh2", [128, KD * 1027], BF16))

            s_h = sem("mh")
            s_hs = sem("mhs")
            s_ht = sem("mht")
            s_hl = sem("mhl")
            with (
                nc.sbuf_tensor("hg", [128, 512], F32) as hg,
                nc.sbuf_tensor("h64", [128, 64], F32) as h64,
                nc.sbuf_tensor("hb", [128, 64], BF16) as hb,
                nc.sbuf_tensor("hsq", [128, 64], BF16) as hsq,
                nc.sbuf_tensor("hrow", [1, 16], F32) as hrow,
                nc.sbuf_tensor("hd", [128, 4], F32) as hd,
                nc.psum_tensor("phSx", [1, 512], F32) as phSx,
                nc.psum_tensor("phSq", [1, 512], F32) as phSq,
                nc.psum_tensor("phM", [128, 8], F32) as phM,
            ):
                @block.sync
                def _(sy):
                    sy.wait_ge(s_ag, 4)
                    sy.dma_start(out=hg[:].rearrange("p (r k c) -> p r k c", r=8, k=16),
                                 in_=ag2_out.ap().rearrange("(r k p) c -> p r k c", p=128, k=16)
                                 ).then_inc(s_hl, 16)

                @block.vector
                def _(v):
                    v.wait_ge(s_hl, 16)
                    v.tensor_scalar(h64[:], hg[:, 0:64], c_hmask[:, 0:1], None,
                                    ALU.mult).then_inc(s_h, 1)
                    for r in range(1, 8):
                        v.wait_ge(s_h, r)
                        v.scalar_tensor_tensor(h64[:], hg[:, 64 * r:64 * r + 64],
                                               c_hmask[:, r:r + 1], h64[:],
                                               ALU.mult, ALU.add).then_inc(s_h, 1)
                    v.wait_ge(s_h, 8)
                    v.tensor_copy(hb[:], h64[:]).then_inc(s_h, 1)
                    v.tensor_tensor(hsq[:], h64[:], h64[:], ALU.mult).then_inc(s_h, 1)

                @block.tensor
                def _(t):
                    t.wait_ge(s_h, 10)
                    for k in range(KD):
                        t.matmul(phSx[0:1, 0:4], c_onesb[:, 0:1], hb[:, 4 * k:4 * k + 4],
                                 start=(k == 0), stop=(k == KD - 1))
                        mm = t.matmul(phSq[0:1, 0:4], c_onesb[:, 0:1], hsq[:, 4 * k:4 * k + 4],
                                      start=(k == 0), stop=(k == KD - 1))
                    mm.then_inc(s_ht, 1)

                @block.vector
                def _(v):
                    v.wait_ge(s_ht, 1)
                    v.tensor_scalar(hrow[:, 0:4], phSx[0:1, 0:4], 1.0 / D, None,
                                    ALU.mult).then_inc(s_h, 1)
                    v.wait_ge(s_h, 11)
                    v.tensor_tensor(hrow[:, 8:12], hrow[:, 0:4], hrow[:, 0:4],
                                    ALU.mult).then_inc(s_h, 1)
                    v.wait_ge(s_h, 12)
                    v.scalar_tensor_tensor(hrow[:, 4:8], phSq[0:1, 0:4], 1.0 / D,
                                           hrow[:, 8:12], ALU.mult,
                                           ALU.subtract).then_inc(s_h, 1)

                @block.scalar
                def _(s):
                    s.wait_ge(s_h, 13)
                    s.activation(hrow[:, 4:8], hrow[:, 4:8], AF.Sqrt,
                                 bias=c_fc[0:1, 0:1]).then_inc(s_hs, 1)

                @block.vector
                def _(v):
                    v.wait_ge(s_hs, 1)
                    v.reciprocal(hrow[:, 4:8], hrow[:, 4:8]).then_inc(s_h, 1)

                @block.tensor
                def _(t):
                    t.wait_ge(s_h, 14)
                    t.matmul(phM[:, 0:4], c_onesf[0:1, 0:128], hrow[:, 0:4],
                             start=True, stop=True)
                    t.matmul(phM[:, 4:8], c_onesf[0:1, 0:128], hrow[:, 4:8],
                             start=True, stop=True).then_inc(s_ht, 1)

                @block.vector
                def _(v):
                    v.wait_ge(s_ht, 2)
                    for k in range(KD):
                        if k >= 1:
                            v.wait_ge(s_h, 14 + 2 * k)
                        v.tensor_tensor(hd[:], h64[:, 4 * k:4 * k + 4],
                                        phM[:, 0:4], ALU.subtract).then_inc(s_h, 1)
                        v.wait_ge(s_h, 14 + 2 * k + 1)
                        v.tensor_tensor(xh2[:, k * 1027:k * 1027 + 3],
                                        hd[:, 1:4], phM[:, 5:8], ALU.mult).then_inc(s_h, 1)

            phase_end()
            emit_ln("l2", xs1.ap(), xh2, ncols=1024, colstride=1027, dst_off=3)
            phase_end()

            s_mw = (sem("mw0"), sem("mw1"))
            s_mt = sem("mt")
            s_mv = sem("mv")
            s_msc = sem("msc")
            s_mz = (sem("mz0"), sem("mz1"))
            s_mx = (sem("mx0"), sem("mx1"))
            Tc = [0] * (NF + 1)
            Vc = [0] * (NF + 1)
            Sc = [0] * (NF + 1)
            for f in range(NF):
                Tc[f + 1] = Tc[f] + (2 if (f < 32 or f == 80) else 3)
                Vc[f + 1] = Vc[f] + (2 if f < 32 else (4 if f < 80 else 0))
                Sc[f + 1] = Sc[f] + (0 if f < 32 else (4 if f < 80 else 3))
            zcum = [[0] * NF, [0] * NF]
            xcum = [[0] * NF, [0] * NF]
            a = b = c2 = d2 = 0
            for f in range(NF):
                if f < 32:
                    if f % 2 == 0:
                        a += 1
                    else:
                        b += 1
                elif f < 80:
                    if f % 2 == 0:
                        c2 += 1
                    else:
                        d2 += 1
                zcum[0][f] = a
                zcum[1][f] = b
                xcum[0][f] = c2
                xcum[1][f] = d2

            with (
                nc.sbuf_tensor("wib", [128, 2 * 2048], BF16) as wib,
                nc.sbuf_tensor("zstb", [128, 2 * 1024], BF16) as zstb,
                nc.sbuf_tensor("xbcf", [128, 2 * 1027], F32) as xbcf,
                nc.sbuf_tensor("accf", [128, 2 * 1024], F32) as accf,
                nc.sbuf_tensor("xbst", [128, 2 * 1024], BF16) as xbst,
                nc.sbuf_tensor("ef", [64, 1024], F32) as ef,
                nc.psum_tensor("mpA0", [128, 512], F32) as mpA0,
                nc.psum_tensor("mpA1", [128, 512], F32) as mpA1,
                nc.psum_tensor("mpB0", [128, 512], F32) as mpB0,
                nc.psum_tensor("mpB1", [128, 512], F32) as mpB1,
                nc.psum_tensor("mpC0", [128, 4], F32) as mpC0,
                nc.psum_tensor("mpC1", [128, 4], F32) as mpC1,
            ):
                mpA = (mpA0, mpA1)
                mpB = (mpB0, mpB1)
                mpC = (mpC0, mpC1)

                @block.gpsimd
                def _(g):
                    for f in range(NF):
                        if f >= 2:
                            g.wait_ge(s_mt, Tc[f - 1])
                        g.dma_start(out=s3(wib[:, (f % 2) * 2048:(f % 2) * 2048 + 2048], 16),
                                    in_=r3(w_in_t[f * 2048:(f + 1) * 2048, :], 16)
                                    ).then_inc(s_mw[f % 2], 16)

                @block.tensor
                def _(t):
                    for f in range(NF):
                        t.wait_ge(s_mw[f % 2], 16 * (f // 2 + 1))
                        if f >= 2:
                            t.wait_ge(s_mv, Vc[f - 1])
                            t.wait_ge(s_msc, Sc[f - 1])
                        wt = wib[:, (f % 2) * 2048:(f % 2) * 2048 + 2048]
                        if f < 32 or f == 80:
                            spl = [(3, 515, mpA), (515, 1027, mpB)]
                        else:
                            spl = [(0, 512, mpA), (512, 1024, mpB), (1024, 1027, mpC)]
                        for si, (lo, hi, ps) in enumerate(spl):
                            for k in range(KD):
                                mm = t.matmul(ps[f % 2][:, 0:hi - lo],
                                              wt[:, k * 128:(k + 1) * 128],
                                              xh2[:, k * 1027 + lo:k * 1027 + hi],
                                              start=(k == 0), stop=(k == KD - 1))
                            mm.then_inc(s_mt, 1)

                @block.vector
                def _(v):
                    for f in range(NF):
                        if f < 32:
                            if f >= 2:
                                v.wait_ge(s_mz[f % 2], 16 * zcum[f % 2][f - 2])
                            for si, ps in ((0, mpA), (1, mpB)):
                                v.wait_ge(s_mt, Tc[f] + si + 1)
                                v.tensor_copy(zstb[:, (f % 2) * 1024 + si * 512:(f % 2) * 1024 + si * 512 + 512],
                                              ps[f % 2][:, :]).then_inc(s_mv, 1)
                        elif f < 80:
                            cf = f - 32
                            base = Vc[f]
                            v.wait_ge(s_msc, Sc[f] + 3)
                            if f >= 34:
                                v.wait_ge(s_msc, Sc[f - 1])
                            xb = xbcf[:, (f % 2) * 1027:(f % 2) * 1027 + 1027]
                            ac = accf[:, (f % 2) * 1024:(f % 2) * 1024 + 1024]
                            v.tensor_scalar(ac[:, :], xb[:, 0:1024],
                                            c_convw[:, cf * 4:cf * 4 + 1],
                                            c_convb[:, cf:cf + 1], ALU.mult,
                                            ALU.add).then_inc(s_mv, 1)
                            for kk in range(1, 4):
                                v.wait_ge(s_mv, base + kk)
                                v.scalar_tensor_tensor(ac[:, :], xb[:, kk:kk + 1024],
                                                       c_convw[:, cf * 4 + kk:cf * 4 + kk + 1],
                                                       ac[:, :], ALU.mult,
                                                       ALU.add).then_inc(s_mv, 1)

                @block.scalar
                def _(s):
                    for f in range(32, NF):
                        if f < 80:
                            if f >= 34:
                                s.wait_ge(s_mv, Vc[f - 1])
                            for si, (lo, hi, ps) in enumerate(((0, 512, mpA), (512, 1024, mpB),
                                                               (1024, 1027, mpC))):
                                s.wait_ge(s_mt, Tc[f] + si + 1)
                                s.copy(xbcf[:, (f % 2) * 1027 + lo:(f % 2) * 1027 + hi],
                                       ps[f % 2][:, 0:hi - lo]).then_inc(s_msc, 1)
                            s.wait_ge(s_mv, Vc[f + 1])
                            if f >= 34:
                                s.wait_ge(s_mx[f % 2], 16 * xcum[f % 2][f - 2])
                            s.activation(xbst[:, (f % 2) * 1024:(f % 2) * 1024 + 1024],
                                         accf[:, (f % 2) * 1024:(f % 2) * 1024 + 1024],
                                         AF.Silu).then_inc(s_msc, 1)
                        else:
                            for si, ps in ((0, mpA), (1, mpB)):
                                s.wait_ge(s_mt, Tc[f] + si + 1)
                                s.activation(ef[:, si * 512:si * 512 + 512],
                                             ps[f % 2][0:64, :], AF.Exp,
                                             bias=c_dtb[:]).then_inc(s_msc, 1)
                            s.wait_ge(s_msc, Sc[f] + 2)
                            s.activation(dtf[:], ef[:], AF.Ln,
                                         bias=c_fc[0:64, 2:3]).then_inc(s_msc, 1)

                @block.sync
                def _(sy):
                    for f in range(NF):
                        if f < 32:
                            sy.wait_ge(s_mv, Vc[f + 1])
                            sy.dma_start(out=zsc.ap()[f * 128:(f + 1) * 128, :],
                                         in_=zstb[:, (f % 2) * 1024:(f % 2) * 1024 + 1024]
                                         ).then_inc(s_mz[f % 2], 16)
                        elif f < 80:
                            cf = f - 32
                            sy.wait_ge(s_msc, Sc[f + 1])
                            sy.dma_start(out=xbcsc.ap()[cf * 128:(cf + 1) * 128, :],
                                         in_=xbst[:, (f % 2) * 1024:(f % 2) * 1024 + 1024]
                                         ).then_inc(s_mx[f % 2], 16)

            xctx.close()
            phase_end(sync_wait=[(s_mz[0], 16 * 16), (s_mz[1], 16 * 16),
                                 (s_mx[0], 16 * 24), (s_mx[1], 16 * 24)])
            emit_scan(stage)
            mctx.close()

        if stage >= 3:
            with nc.sbuf_tensor("hid2", [128, KE * 1024], BF16) as hid2:
                with nc.sbuf_tensor("xh3", [128, KD * 1024], BF16) as xh3:
                    emit_ln("l3", xs2.ap(), xh3)
                    phase_end()
                    emit_ffn_u("f2", xh3, hid2)
                emit_ffn_d("f2", hid2, xs2.ap(), out, False, False)

        phase_end()

    return nc


def emit_mamba(*a, **k):
    raise NotImplementedError("stage >= 2 not implemented yet")


def _stage_inputs(inputs):
    u = np.asarray(inputs["L0_attn_u"], dtype=np.float32)
    W_in = np.asarray(inputs["L0_attn_W_in"], dtype=np.float32)
    conv_w = np.asarray(inputs["L0_attn_conv_w"], dtype=np.float32)
    conv_b = np.asarray(inputs["L0_attn_conv_b"], dtype=np.float32)
    dt_bias = np.asarray(inputs["L0_attn_dt_bias"], dtype=np.float32)
    A_log = np.asarray(inputs["L0_attn_A_log"], dtype=np.float32)
    Dv = np.asarray(inputs["L0_attn_D"], dtype=np.float32)
    norm_w = np.asarray(inputs["L0_attn_norm_weight"], dtype=np.float32)
    W_out = np.asarray(inputs["L0_attn_W_out"], dtype=np.float32)
    W_up = np.asarray(inputs["L0_ffn_W_up"], dtype=np.float32)
    W_g = np.asarray(inputs["L0_ffn_W_gate"], dtype=np.float32)
    W_d = np.asarray(inputs["L0_ffn_W_down"], dtype=np.float32)

    WinT = np.ascontiguousarray(W_in.T)
    WinT_pad = np.zeros((D, NF * 128), dtype=np.float32)
    WinT_pad[:, :10304] = WinT
    w_in_t = np.zeros((NF, 16, 128, 128), dtype=np.float32)
    for f in range(NF):
        for k in range(16):
            w_in_t[f, k] = WinT_pad[k * 128:(k + 1) * 128, f * 128:(f + 1) * 128]
    w_in_t = w_in_t.reshape(NF * 16 * 128, 128)

    WoutT = np.ascontiguousarray(W_out.T)
    w_out_t = np.zeros((16, 32, 128, 128), dtype=np.float32)
    for m in range(16):
        for kc in range(32):
            w_out_t[m, kc] = WoutT[kc * 128:(kc + 1) * 128, m * 128:(m + 1) * 128]
    w_out_t = w_out_t.reshape(16 * 32 * 128, 128)

    WupT = np.ascontiguousarray(W_up.T)
    WgT = np.ascontiguousarray(W_g.T)
    WdT = np.ascontiguousarray(W_d.T)

    identb = np.eye(128, dtype=BF)
    identf = np.eye(128, dtype=np.float32)
    trilb = np.triu(np.ones((128, 128), dtype=np.float32)).astype(BF)
    onesf = np.ones((128, 128), dtype=np.float32)
    onesb = np.ones((128, 128), dtype=BF)
    gn_oh = np.zeros((128, 32, 32), dtype=np.float32)
    for g in range(32):
        gn_oh[:, g, g] = 1.0
    gn_oh = gn_oh.reshape(128, 32 * 32).astype(BF)
    d_expv = np.repeat(Dv, 64).reshape(32, 128).T.copy()
    norm_wt = norm_w.reshape(32, 128).T.copy()

    common = dict(
        w_in_t=w_in_t, w_out_t=w_out_t,
        conv_w=conv_w, conv_b=conv_b[:, None].copy(),
        dt_bias=dt_bias[:, None].copy(), a_log=A_log[:, None].copy(),
        d_exp=d_expv, norm_w=norm_wt,
        identb=identb, identf=identf, trilb=trilb,
        onesf=onesf, onesb=onesb, gn_oh=gn_oh,
    )
    in_maps = []
    for c in range(8):
        s, h = c // 2, c % 2
        a = h * 1024
        m = dict(common)
        m["u_fm"] = np.ascontiguousarray(u[s, a:a + 1024, :].T)
        m["wup_sh"] = np.ascontiguousarray(WupT[:, 1024 * c:1024 * (c + 1)])
        m["wg_sh"] = np.ascontiguousarray(WgT[:, 1024 * c:1024 * (c + 1)])
        m["wd_sh"] = np.ascontiguousarray(WdT[:, 256 * c:256 * (c + 1)])
        fc = np.zeros((128, 8), dtype=np.float32)
        fc[:, 0] = 1e-5
        fc[:, 1] = 1.0 / 128.0
        fc[:, 2] = 1.0
        fc[:, 3] = -1.0
        m["fconst"] = fc
        hm = np.zeros((128, 8), dtype=np.float32)
        if h == 1:
            hm[:, c - 1] = 1.0
        m["hmask"] = hm
        in_maps.append(m)
    return in_maps


def kernel(**inputs):
    in_maps = _stage_inputs(inputs)
    nc = build(STAGE)
    res = run_bass_kernel_spmd(nc, in_maps, core_ids=list(range(8)))
    outp = np.empty((4, 2048, 2048), dtype=np.float32)
    for c in range(8):
        s, h = c // 2, c % 2
        a = h * 1024
        outp[s, a:a + 1024, :] = res.results[c]["out"].T
    return outp


# version 10
# speedup vs baseline: 1.3362x; 1.0079x over previous
import contextlib
import numpy as np
import ml_dtypes
import concourse.bass as bass
import concourse.mybir as mybir
from concourse.bass_utils import run_bass_kernel_spmd

F32 = mybir.dt.float32
BF16 = mybir.dt.bfloat16
AF = mybir.ActivationFunctionType
ALU = mybir.AluOpType
AX = mybir.AxisListType
BF = ml_dtypes.bfloat16

D = 2048
NT = 1024
DE = 8192
DI = 4096
CONVD = 6144
NF = 81
NCH = 8
KD = 16
KE = 64
KI = 32
LN_EPS = 1e-5
GN_EPS = 1e-5
TERN_EPS = 1e-8
NW = float(D * DE)

STAGE = 3
LNCUT = 4


def build(stage=STAGE):
    nc = bass.Bass("TRN2", target_bir_lowering=False, debug=False, detect_race_conditions=True)

    u_fm = nc.dram_tensor("u_fm", [D, NT], F32, kind="ExternalInput").ap()
    w_in_t = nc.dram_tensor("w_in_t", [NF * 16 * 128, 128], F32, kind="ExternalInput").ap()
    w_out_t = nc.dram_tensor("w_out_t", [16 * 32 * 128, 128], F32, kind="ExternalInput").ap()
    wup_sh = nc.dram_tensor("wup_sh", [D, 1024], F32, kind="ExternalInput").ap()
    wg_sh = nc.dram_tensor("wg_sh", [D, 1024], F32, kind="ExternalInput").ap()
    wd_sh = nc.dram_tensor("wd_sh", [DE, 256], F32, kind="ExternalInput").ap()
    conv_w = nc.dram_tensor("conv_w", [CONVD, 4], F32, kind="ExternalInput").ap()
    conv_b = nc.dram_tensor("conv_b", [CONVD, 1], F32, kind="ExternalInput").ap()
    dt_bias = nc.dram_tensor("dt_bias", [64, 1], F32, kind="ExternalInput").ap()
    a_log = nc.dram_tensor("a_log", [64, 1], F32, kind="ExternalInput").ap()
    d_exp = nc.dram_tensor("d_exp", [128, 32], F32, kind="ExternalInput").ap()
    norm_w = nc.dram_tensor("norm_w", [128, 32], F32, kind="ExternalInput").ap()
    identb = nc.dram_tensor("identb", [128, 128], BF16, kind="ExternalInput").ap()
    identf = nc.dram_tensor("identf", [128, 128], F32, kind="ExternalInput").ap()
    trilb = nc.dram_tensor("trilb", [128, 128], BF16, kind="ExternalInput").ap()
    onesf_in = nc.dram_tensor("onesf", [128, 128], F32, kind="ExternalInput").ap()
    onesb_in = nc.dram_tensor("onesb", [128, 128], BF16, kind="ExternalInput").ap()
    gn_oh_in = nc.dram_tensor("gn_oh", [128, 32 * 32], BF16, kind="ExternalInput").ap()
    hmask_in = nc.dram_tensor("hmask", [128, 8], F32, kind="ExternalInput").ap()
    fconst_in = nc.dram_tensor("fconst", [128, 8], F32, kind="ExternalInput").ap()
    out = nc.dram_tensor("out", [D, NT], F32, kind="ExternalOutput").ap()
    dbg = nc.dram_tensor("dbg", [128, 16], F32, kind="ExternalOutput").ap()

    ar0_in = nc.dram_tensor("ar0_in", [1, 8], F32)
    ar0_out = nc.dram_tensor("ar0_out", [1, 8], F32)
    ag1_in = nc.dram_tensor("ag1_in", [3 * 16384, 128], BF16)
    ag1a_out = nc.dram_tensor("ag1a_out", [8 * 2 * 16384, 128], BF16, addr_space="Shared")
    ag1b_out = nc.dram_tensor("ag1b_out", [8 * 16384, 128], BF16, addr_space="Shared")
    ag2_in = nc.dram_tensor("ag2_in", [D, 4], F32)
    ag2_out = nc.dram_tensor("ag2_out", [8 * D, 4], F32, addr_space="Shared")
    ag3_in = nc.dram_tensor("ag3_in", [128, 4096], BF16)
    ag3_out = nc.dram_tensor("ag3_out", [8 * 128, 4096], BF16, addr_space="Shared")
    xs1 = nc.dram_tensor("xs1", [D, NT], F32)
    xs2 = nc.dram_tensor("xs2", [D, NT], F32)
    zsc = nc.dram_tensor("zsc", [DI, NT], BF16)
    xbcsc = nc.dram_tensor("xbcsc", [CONVD, NT], BF16)

    rg8 = [[0, 1, 2, 3, 4, 5, 6, 7]]

    def tup_block(e):
        r, el = e // 8, e % 8
        off = r * 32768 + el * 2048
        return ag1a_out.ap()[off:off + 2048, :]

    def tg_block(e):
        r, el = e // 8, e % 8
        off = r * 32768 + 16384 + el * 2048
        return ag1a_out.ap()[off:off + 2048, :]

    def td_block(m):
        r, ml = m // 2, m % 2
        off = r * 16384 + ml * 8192
        return ag1b_out.ap()[off:off + 8192, :]

    def r3(dram_slice, kdim):
        return dram_slice.rearrange("(k p) c -> p k c", p=128)

    def r3p(dram_slice):
        return dram_slice.rearrange("(p k) c -> p k c", p=128)

    def s3(sbuf_ap, kdim):
        return sbuf_ap.rearrange("p (k c) -> p k c", k=kdim)

    ctx = contextlib.ExitStack()
    sems = {}

    def sem(name):
        if name not in sems:
            sems[name] = ctx.enter_context(nc.semaphore(name))
        return sems[name]

    PH = [0]

    with ctx:
        cst = ctx.enter_context(nc.sbuf_tensor("c_identb", [128, 128], BF16))
        c_identb = cst
        c_identf = ctx.enter_context(nc.sbuf_tensor("c_identf", [128, 128], F32))
        c_trilb = ctx.enter_context(nc.sbuf_tensor("c_trilb", [128, 128], BF16))
        c_onesf = ctx.enter_context(nc.sbuf_tensor("c_onesf", [128, 128], F32))
        c_onesb = ctx.enter_context(nc.sbuf_tensor("c_onesb", [128, 128], BF16))
        c_gnoh = ctx.enter_context(nc.sbuf_tensor("c_gnoh", [128, 32 * 32], BF16))
        c_hmask = ctx.enter_context(nc.sbuf_tensor("c_hmask", [128, 8], F32))
        c_convw = ctx.enter_context(nc.sbuf_tensor("c_convw", [128, 48 * 4], F32))
        c_convb = ctx.enter_context(nc.sbuf_tensor("c_convb", [128, 48], F32))
        c_dtb = ctx.enter_context(nc.sbuf_tensor("c_dtb", [64, 1], F32))
        c_av = ctx.enter_context(nc.sbuf_tensor("c_av", [64, 1], F32))
        c_dexp = ctx.enter_context(nc.sbuf_tensor("c_dexp", [128, 32], F32))
        c_normw = ctx.enter_context(nc.sbuf_tensor("c_normw", [128, 32], F32))
        c_diagd = ctx.enter_context(nc.sbuf_tensor("c_diagd", [128, 32 * 128], BF16))
        v_thu = ctx.enter_context(nc.sbuf_tensor("v_thu", [128, 1], F32))
        v_thun = ctx.enter_context(nc.sbuf_tensor("v_thun", [128, 1], F32))
        v_thg = ctx.enter_context(nc.sbuf_tensor("v_thg", [128, 1], F32))
        v_thgn = ctx.enter_context(nc.sbuf_tensor("v_thgn", [128, 1], F32))
        v_thd = ctx.enter_context(nc.sbuf_tensor("v_thd", [128, 1], F32))
        v_thdn = ctx.enter_context(nc.sbuf_tensor("v_thdn", [128, 1], F32))
        v_su = ctx.enter_context(nc.sbuf_tensor("v_su", [128, 1], F32))
        v_sdg = ctx.enter_context(nc.sbuf_tensor("v_sdg", [128, 1], F32))
        s_row = ctx.enter_context(nc.sbuf_tensor("s_row", [1, 16], F32))
        c_fc = ctx.enter_context(nc.sbuf_tensor("c_fc", [128, 8], F32))
        drv = ctx.enter_context(nc.sbuf_tensor("drv", [128, 2], F32))
        drs = ctx.enter_context(nc.sbuf_tensor("drs", [128, 2], F32))
        drg = ctx.enter_context(nc.sbuf_tensor("drg", [128, 2], F32))
        block = ctx.enter_context(nc.Block())

        consts = dict(identb=c_identb, identf=c_identf, trilb=c_trilb, onesf=c_onesf,
                      onesb=c_onesb, gnoh=c_gnoh, hmask=c_hmask, convw=c_convw,
                      convb=c_convb, dtb=c_dtb, av=c_av, dexp=c_dexp, normw=c_normw,
                      diagd=c_diagd)

        ph_sems = {e: sem("ph_" + e) for e in ("sy", "g", "t", "v", "s")}

        def phase_end(sync_wait=None, gp_wait=None, scalar_wait=None):
            tgt = PH[0] + 1

            def _aslist(x):
                if x is None:
                    return []
                if isinstance(x, list):
                    return x
                return [x]

            def wait_all(eng):
                for e in ("sy", "g", "t", "v", "s"):
                    eng.wait_ge(ph_sems[e], tgt)

            @block.sync
            def _(sy):
                for sw in _aslist(sync_wait):
                    sy.wait_ge(sw[0], sw[1])
                sy.sem_inc(ph_sems["sy"], 1)
                wait_all(sy)

            @block.gpsimd
            def _(g):
                for sw in _aslist(gp_wait):
                    g.wait_ge(sw[0], sw[1])
                g.tensor_copy(drg[:, 1:2], drg[:, 0:1]).then_inc(ph_sems["g"], 1)
                wait_all(g)

            @block.tensor
            def _(t):
                t.ldweights(c_onesb[:, 0:128]).then_inc(ph_sems["t"], 1)
                wait_all(t)

            @block.vector
            def _(v):
                v.tensor_copy(drv[:, 1:2], drv[:, 0:1]).then_inc(ph_sems["v"], 1)
                wait_all(v)

            @block.scalar
            def _(s):
                for sw in _aslist(scalar_wait):
                    s.wait_ge(sw[0], sw[1])
                s.copy(drs[:, 1:2], drs[:, 0:1]).then_inc(ph_sems["s"], 1)
                wait_all(s)

            PH[0] = tgt

        s_cld = sem("cld")
        s_e0s = sem("e0s")
        s_e0v = sem("e0v")
        s_e0g = sem("e0g")

        NCLD = [0]

        @block.sync
        def _(sy):
            for dst, srcc in ((c_identb, identb), (c_identf, identf),
                              (c_trilb, trilb), (c_onesf, onesf_in),
                              (c_onesb, onesb_in), (c_gnoh, gn_oh_in),
                              (c_hmask, hmask_in), (c_fc, fconst_in), (c_dtb, dt_bias),
                              (c_av, a_log), (c_dexp, d_exp), (c_normw, norm_w)):
                sy.dma_start(out=dst[:], in_=srcc).then_inc(s_cld, 16)
                NCLD[0] += 1
            for cf in range(48):
                sy.dma_start(out=c_convw[:, cf * 4:cf * 4 + 4],
                             in_=conv_w[cf * 128:cf * 128 + 128, :]).then_inc(s_cld, 16)
                sy.dma_start(out=c_convb[:, cf:cf + 1],
                             in_=conv_b[cf * 128:cf * 128 + 128, :]).then_inc(s_cld, 16)
                NCLD[0] += 2

        @block.scalar
        def _(s):
            s.memzero(drs[:])
            s.wait_ge(s_cld, 16 * NCLD[0])
            s.activation(c_av[:], c_av[:], AF.Exp).then_inc(s_e0s, 1)
            s.wait_ge(s_e0s, 1)
            s.mul(c_av[:], c_av[:], -1.0).then_inc(s_e0s, 1)

        @block.vector
        def _(v):
            v.memset(s_row[:], 0.0)
            v.memset(drv[:], 0.0)
            v.wait_ge(s_cld, 16 * NCLD[0])
            s_dg0 = sem("dg0")
            for j in range(32):
                v.tensor_scalar(c_diagd[:, j * 128:(j + 1) * 128], c_onesb[:],
                                c_dexp[:, j:j + 1], None, ALU.mult).then_inc(s_dg0, 1)
            v.wait_ge(s_dg0, 32)
            v.tensor_copy(drv[:, 1:2], drv[:, 0:1]).then_inc(s_e0v, 1)

        @block.gpsimd
        def _(g):
            g.memset(drg[:], 0.0)
            g.wait_ge(s_e0v, 1)
            for j in range(32):
                g.affine_select(c_diagd[:, j * 128:(j + 1) * 128],
                                c_diagd[:, j * 128:(j + 1) * 128],
                                pattern=[[1, 128]], base=0, channel_multiplier=-1,
                                compare_op=ALU.is_equal, fill=0.0).then_inc(s_e0g, 1)

        shard_plan = [("u", wup_sh, 16, 1024), ("g", wg_sh, 16, 1024),
                      ("d", wd_sh, 64, 256)]
        s_wl = (sem("wl0"), sem("wl1"))
        s_ar = sem("ar")
        s_wred = sem("wred")
        s_p0t = sem("p0t")
        s_p0v = sem("p0v")
        s_ag = sem("ag")
        s_tv = sem("tv")
        s_ts = (sem("ts0"), sem("ts1"))
        win_per = []
        for nm, dram, nl, ncols in shard_plan:
            win_per += [4096 // ncols] * 4
        win_per = win_per * 2
        wcum = [[0] * 24, [0] * 24]
        c0 = c1 = 0
        for w, per in enumerate(win_per):
            if w % 2 == 0:
                c0 += per
            else:
                c1 += per
            wcum[0][w] = c0
            wcum[1][w] = c1

        with (
            nc.sbuf_tensor("tw_win", [128, 2 * 4096], F32) as tw_win,
            nc.sbuf_tensor("tw_acc", [128, 12], F32) as tw_acc,
            nc.sbuf_tensor("tw_red", [128, 3], F32) as tw_red,
            nc.psum_tensor("tw_ps", [1, 4], F32) as tw_ps,
            nc.psum_tensor("tw_pb", [128, 16], F32) as tw_pb,
        ):
            @block.sync
            def _(sy):
                w = 0
                for nm, dram, nl, ncols in shard_plan:
                    per = 4096 // ncols
                    for w0 in range(0, nl, per):
                        if w >= 2:
                            sy.wait_ge(s_wred, w - 1)
                        buf = tw_win[:, (w % 2) * 4096:(w % 2) * 4096 + 4096]
                        for i in range(per):
                            k = w0 + i
                            sy.dma_start(out=buf[:, i * ncols:(i + 1) * ncols],
                                         in_=dram[k * 128:(k + 1) * 128, :]).then_inc(s_wl[w % 2], 16)
                        w += 1

            @block.vector
            def _(v):
                w = 0
                for nm, dram, nl, ncols in shard_plan:
                    per = 4096 // ncols
                    for wi in range(4):
                        v.wait_ge(s_wl[w % 2], 16 * wcum[w % 2][w])
                        buf = tw_win[:, (w % 2) * 4096:(w % 2) * 4096 + 4096]
                        v.tensor_reduce(tw_acc[:, w:w + 1], buf[:], AX.X, ALU.add,
                                        apply_absolute_value=True).then_inc(s_wred, 1)
                        w += 1
                v.wait_ge(s_wred, 12)
                for mi in range(3):
                    v.tensor_reduce(tw_red[:, mi:mi + 1], tw_acc[:, mi * 4:mi * 4 + 4],
                                    AX.X, ALU.add).then_inc(s_wred, 1)

            @block.tensor
            def _(t):
                t.wait_ge(s_wred, 15)
                t.matmul(tw_ps[0:1, 0:3], c_onesf[:, 0:1], tw_red[:, 0:3],
                         start=True, stop=True).then_inc(s_p0t, 1)

            @block.vector
            def _(v):
                v.wait_ge(s_p0t, 1)
                s_ms = sem("ms")
                v.memset(s_row[:, 4:12], 0.0).then_inc(s_ms, 1)
                v.wait_ge(s_ms, 1)
                v.tensor_copy(s_row[:, 4:7], tw_ps[0:1, 0:3]).then_inc(s_p0v, 1)

            @block.sync
            def _(sy):
                sy.wait_ge(s_p0v, 1)
                sy.dma_start(out=ar0_in.ap()[0:1, 0:8], in_=s_row[:, 4:12]).then_inc(s_ar, 16)

            @block.gpsimd
            def _(g):
                g.wait_ge(s_ar, 16)
                g.collective_compute("AllReduce", ALU.add, ins=[ar0_in.ap().opt()],
                                     outs=[ar0_out.ap().opt()],
                                     replica_groups=rg8).then_inc(s_ag, 1)

            @block.sync
            def _(sy):
                sy.wait_ge(s_ag, 1)
                sy.dma_start(out=s_row[:, 8:11], in_=ar0_out.ap()[0:1, 0:3]).then_inc(s_ar, 16)

            s_chn = sem("chn")

            @block.vector
            def _(v):
                v.wait_ge(s_ar, 32)
                v.tensor_scalar(s_row[:, 8:11], s_row[:, 8:11], 1.0 / NW, TERN_EPS,
                                ALU.mult, ALU.add).then_inc(s_chn, 1)
                v.wait_ge(s_chn, 1)
                v.tensor_scalar(s_row[:, 11:14], s_row[:, 8:11], 0.5, None, ALU.mult)
                v.tensor_tensor(s_row[:, 14:15], s_row[:, 9:10], s_row[:, 10:11],
                                ALU.mult).then_inc(s_chn, 1)
                v.wait_ge(s_chn, 2)
                v.tensor_scalar(s_row[:, 14:15], s_row[:, 14:15], 0.5, None,
                                ALU.mult).then_inc(s_chn, 1)
                v.wait_ge(s_chn, 3)
                v.tensor_copy(drv[:, 1:2], drv[:, 0:1]).then_inc(s_p0v, 1)

            @block.tensor
            def _(t):
                t.wait_ge(s_p0v, 2)
                cols5 = (11, 12, 13, 8, 14)
                for i, col in enumerate(cols5):
                    mm = t.matmul(tw_pb[:, i:i + 1], c_onesf[0:1, 0:128],
                                  s_row[:, col:col + 1], start=True, stop=True)
                    if i == len(cols5) - 1:
                        mm.then_inc(s_p0t, 1)

            @block.vector
            def _(v):
                v.wait_ge(s_p0t, 2)
                s_thv = sem("thv")
                for i, dst in enumerate((v_thu, v_thg, v_thd, v_su, v_sdg)):
                    v.tensor_copy(dst[:], tw_pb[:, i:i + 1]).then_inc(s_thv, 1)
                for i, dst in enumerate((v_thun, v_thgn, v_thdn)):
                    v.tensor_scalar(dst[:], tw_pb[:, i:i + 1], -1.0, None,
                                    ALU.mult).then_inc(s_thv, 1)
                v.wait_ge(s_thv, 8)
                v.tensor_copy(drv[:, 1:2], drv[:, 0:1]).then_inc(s_p0v, 1)

            with (
                nc.sbuf_tensor("tn_a", [128, 4096], F32) as tn_a,
                nc.sbuf_tensor("tn_t", [128, 2 * 4096], BF16) as tn_t,
            ):
                thvec = {"u": (v_thu, v_thun), "g": (v_thg, v_thgn), "d": (v_thd, v_thdn)}
                tern_inc = [16 * 8] * 8 + [16 * 2] * 4
                TS_CUM = [[0] * 12, [0] * 12]
                _c = [0, 0]
                for _wt in range(12):
                    _c[_wt % 2] += tern_inc[_wt]
                    TS_CUM[_wt % 2][_wt] = _c[_wt % 2]

                @block.sync
                def _(sy):
                    sy.wait_ge(s_wred, 15)
                    w = 12
                    for nm, dram, nl, ncols in shard_plan:
                        per = 4096 // ncols
                        for w0 in range(0, nl, per):
                            wt = w - 12
                            if wt >= 2:
                                sy.wait_ge(s_tv, 2 * (wt - 1))
                            buf = tw_win[:, (w % 2) * 4096:(w % 2) * 4096 + 4096]
                            for i in range(per):
                                k = w0 + i
                                sy.dma_start(out=buf[:, i * ncols:(i + 1) * ncols],
                                             in_=dram[k * 128:(k + 1) * 128, :]).then_inc(s_wl[w % 2], 16)
                            w += 1

                @block.vector
                def _(v):
                    v.wait_ge(s_p0v, 3)
                    w = 12
                    for nm, dram, nl, ncols in shard_plan:
                        per = 4096 // ncols
                        thp, thn = thvec[nm]
                        for wi in range(4):
                            wt = w - 12
                            v.wait_ge(s_wl[w % 2], 16 * wcum[w % 2][w])
                            if wt >= 2:
                                v.wait_ge(s_ts[wt % 2], TS_CUM[wt % 2][wt - 2])
                            buf = tw_win[:, (w % 2) * 4096:(w % 2) * 4096 + 4096]
                            tb = tn_t[:, (w % 2) * 4096:(w % 2) * 4096 + 4096]
                            if wt >= 1:
                                v.wait_ge(s_tv, 2 * wt)
                            v.tensor_scalar(tn_a[:], buf[:], thn[:], None,
                                            ALU.is_lt).then_inc(s_tv, 1)
                            v.wait_ge(s_tv, 2 * wt + 1)
                            v.scalar_tensor_tensor(tb[:], buf[:], thp[:], tn_a[:],
                                                   ALU.is_gt, ALU.subtract).then_inc(s_tv, 1)
                            w += 1

                reg_off = {"u": 0, "g": 16384, "d": 32768}
                reg_elstride = {"u": 2048, "g": 2048, "d": 8192}
                reg_k = {"u": 16, "g": 16, "d": 64}

                @block.scalar
                def _(s):
                    w = 0
                    for nm, dram, nl, ncols in shard_plan:
                        per = 4096 // ncols
                        nel = ncols // 128
                        nk = reg_k[nm]
                        for wi in range(4):
                            s.wait_ge(s_tv, 2 * (w + 1))
                            tb3 = tn_t[:, (w % 2) * 4096:(w % 2) * 4096 + 4096].rearrange(
                                "p (i ec) -> p i ec", i=per)
                            for el in range(nel):
                                roff = reg_off[nm] + el * reg_elstride[nm]
                                blk = ag1_in.ap()[roff:roff + nk * 128, :].rearrange(
                                    "(p k) c -> p k c", p=128)
                                s.dma_start(
                                    out=blk[:, wi * per:(wi + 1) * per, :],
                                    in_=tb3[:, :, el * 128:(el + 1) * 128]
                                    ).then_inc(s_ts[w % 2], 16)
                            w += 1

                @block.gpsimd
                def _(g):
                    g.wait_ge(s_ts[0], 512)
                    g.wait_ge(s_ts[1], 512)
                    g.collective_compute("AllGather", ALU.bypass,
                                         ins=[ag1_in.ap()[0:32768, :].opt()],
                                         outs=[ag1a_out.ap().opt()],
                                         replica_groups=rg8).then_inc(s_ag, 1)

        phase_end(scalar_wait=[(s_ts[0], 576), (s_ts[1], 576)])

        def emit_dummy_out():
            s_do = sem("dout")

            @block.sync
            def _(sy):
                for k in range(KD):
                    sy.dma_start(out=out[k * 128:(k + 1) * 128, :],
                                 in_=u_fm[k * 128:(k + 1) * 128, :]).then_inc(s_do, 16)
                sy.wait_ge(s_do, 16 * KD)
            phase_end()

        if stage < 0.4:
            emit_dummy_out()
            return nc

        SEMB = {}

        def takebase(name, final):
            b = SEMB.get(name, 0)
            SEMB[name] = b + final
            return b

        def emit_ln(tag, src_dram, xh, ncols=1024, colstride=None, dst_off=0):
            if colstride is None:
                colstride = ncols
            s_lt = sem("lnlt")
            s_lv = sem("lnlv")
            s_ls = sem("lnls")
            s_lw = (sem("lnlw0"), sem("lnlw1"))
            blt = takebase("lnlt", KD + 1)
            blv = takebase("lnlv", 6 * KD + 3)
            bls = takebase("lnls", 1)
            blw = (takebase("lnlw0", 16 * 16), takebase("lnlw1", 16 * 16))
            nsp = 2
            with (
                nc.sbuf_tensor(tag + "win", [128, 2 * 1024], F32) as win,
                nc.sbuf_tensor(tag + "wb", [128, 2 * 1024], BF16) as winb,
                nc.sbuf_tensor(tag + "sq", [128, 2 * 1024], BF16) as sqb,
                nc.sbuf_tensor(tag + "mr", [1, 2 * ncols], F32) as mrow,
                nc.sbuf_tensor(tag + "dd", [128, 1024], F32) as dbuf,
                nc.psum_tensor(tag + "psxA", [1, 512], F32) as psxA,
                nc.psum_tensor(tag + "psxB", [1, 512], F32) as psxB,
                nc.psum_tensor(tag + "psqA", [1, 512], F32) as psqA,
                nc.psum_tensor(tag + "psqB", [1, 512], F32) as psqB,
                nc.psum_tensor(tag + "pmuA", [128, 512], F32) as pmuA,
                nc.psum_tensor(tag + "pmuB", [128, 512], F32) as pmuB,
                nc.psum_tensor(tag + "prA", [128, 512], F32) as prA,
                nc.psum_tensor(tag + "prB", [128, 512], F32) as prB,
            ):
                psx = (psxA, psxB)
                psq = (psqA, psqB)
                pmu = (pmuA, pmuB)
                pr = (prA, prB)
                splits = [(0, 512), (512, 1024)]

                @block.sync
                def _(sy):
                    for k in range(KD):
                        if k >= 2:
                            sy.wait_ge(s_lt, blt + k - 1)
                        sy.dma_start(out=win[:, (k % 2) * 1024:(k % 2) * 1024 + ncols],
                                     in_=src_dram[k * 128:(k + 1) * 128, :]).then_inc(s_lw[k % 2], 16)

                @block.vector
                def _(v):
                    for k in range(KD):
                        v.wait_ge(s_lw[k % 2], blw[k % 2] + 16 * (k // 2 + 1))
                        if k >= 2:
                            v.wait_ge(s_lt, blt + k - 1)
                        v.tensor_copy(winb[:, (k % 2) * 1024:(k % 2) * 1024 + ncols],
                                      win[:, (k % 2) * 1024:(k % 2) * 1024 + ncols]).then_inc(s_lv, 1)
                        v.tensor_tensor(sqb[:, (k % 2) * 1024:(k % 2) * 1024 + ncols],
                                        win[:, (k % 2) * 1024:(k % 2) * 1024 + ncols],
                                        win[:, (k % 2) * 1024:(k % 2) * 1024 + ncols],
                                        ALU.mult).then_inc(s_lv, 1)

                @block.tensor
                def _(t):
                    for k in range(KD):
                        t.wait_ge(s_lv, blv + 2 * (k + 1))
                        for sp, (lo, hi) in enumerate(splits):
                            t.matmul(psx[sp][0:1, :], c_onesb[:, 0:1],
                                     winb[:, (k % 2) * 1024 + lo:(k % 2) * 1024 + hi],
                                     start=(k == 0), stop=(k == KD - 1))
                            mm = t.matmul(psq[sp][0:1, :], c_onesb[:, 0:1],
                                          sqb[:, (k % 2) * 1024 + lo:(k % 2) * 1024 + hi],
                                          start=(k == 0), stop=(k == KD - 1))
                            if sp == len(splits) - 1:
                                mm.then_inc(s_lt, 1)

                if LNCUT < 2:
                    return

                s_lx = sem("lnlx")
                blx = takebase("lnlx", 4)

                @block.vector
                def _(v):
                    v.wait_ge(s_lt, blt + KD)
                    for sp, (lo, hi) in enumerate(splits):
                        v.tensor_scalar(mrow[:, lo:hi], psx[sp][0:1, :], 1.0 / D,
                                        None, ALU.mult).then_inc(s_lx, 1)
                    v.wait_ge(s_lx, blx + 2)
                    for sp, (lo, hi) in enumerate(splits):
                        v.tensor_tensor(mrow[:, ncols + lo:ncols + hi], mrow[:, lo:hi],
                                        mrow[:, lo:hi], ALU.mult).then_inc(s_lx, 1)
                    v.wait_ge(s_lx, blx + 4)
                    for sp, (lo, hi) in enumerate(splits):
                        v.scalar_tensor_tensor(mrow[:, ncols + lo:ncols + hi], psq[sp][0:1, :],
                                               1.0 / D, mrow[:, ncols + lo:ncols + hi],
                                               ALU.mult, ALU.subtract).then_inc(s_lv, 1)

                @block.scalar
                def _(s):
                    s.wait_ge(s_lv, blv + 2 * KD + 2)
                    s.activation(mrow[:, ncols:2 * ncols], mrow[:, ncols:2 * ncols],
                                 AF.Sqrt, bias=c_fc[0:1, 0:1]).then_inc(s_ls, 1)

                @block.vector
                def _(v):
                    v.wait_ge(s_ls, bls + 1)
                    v.reciprocal(mrow[:, ncols:2 * ncols],
                                 mrow[:, ncols:2 * ncols]).then_inc(s_lv, 1)

                if LNCUT < 3:
                    return

                @block.tensor
                def _(t):
                    t.wait_ge(s_lv, blv + 2 * KD + 3)
                    for sp, (lo, hi) in enumerate(splits):
                        t.matmul(pmu[sp][:, :], c_onesf[0:1, 0:128],
                                 mrow[:, lo:hi], start=True, stop=True)
                        mm = t.matmul(pr[sp][:, :], c_onesf[0:1, 0:128],
                                      mrow[:, ncols + lo:ncols + hi], start=True, stop=True)
                        if sp == len(splits) - 1:
                            mm.then_inc(s_lt, 1)

                if LNCUT < 4:
                    return

                @block.sync
                def _(sy):
                    for k in range(KD):
                        if k == 0:
                            sy.wait_ge(s_lt, blt + KD)
                        if k >= 2:
                            sy.wait_ge(s_lv, blv + 2 * KD + 3 + 4 * (k - 1))
                        sy.dma_start(out=win[:, (k % 2) * 1024:(k % 2) * 1024 + ncols],
                                     in_=src_dram[k * 128:(k + 1) * 128, :]).then_inc(s_lw[k % 2], 16)

                @block.vector
                def _(v):
                    v.wait_ge(s_lt, blt + KD + 1)
                    base = blv + 2 * KD + 3
                    for k in range(KD):
                        v.wait_ge(s_lw[k % 2], blw[k % 2] + 16 * (KD // 2 + k // 2 + 1))
                        if k >= 1:
                            v.wait_ge(s_lv, base + 4 * k)
                        for sp, (lo, hi) in enumerate(splits):
                            v.tensor_tensor(dbuf[:, sp * 512:sp * 512 + hi - lo],
                                            win[:, (k % 2) * 1024 + lo:(k % 2) * 1024 + hi],
                                            pmu[sp][:, :], ALU.subtract).then_inc(s_lv, 1)
                        v.wait_ge(s_lv, base + 4 * k + 2)
                        for sp, (lo, hi) in enumerate(splits):
                            v.tensor_tensor(xh[:, k * colstride + dst_off + lo:k * colstride + dst_off + hi],
                                            dbuf[:, sp * 512:sp * 512 + hi - lo],
                                            pr[sp][:, :], ALU.mult).then_inc(s_lv, 1)

        def emit_ffn_u(tag, xh, hid):
            s_fw = (sem("fuw0"), sem("fuw1"))
            s_ft = sem("fut")
            s_fs = sem("fus")
            s_fv = sem("fuv")
            bfw = (takebase("fuw0", 16 * 64), takebase("fuw1", 16 * 64))
            bft = takebase("fut", 4 * KE)
            bfs = takebase("fus", 2 * KE)
            bfv = takebase("fuv", 2 * KE)
            with (
                nc.sbuf_tensor(tag + "tu", [128, 2 * 2048], BF16) as tub,
                nc.sbuf_tensor(tag + "tg", [128, 2 * 2048], BF16) as tgb,
                nc.sbuf_tensor(tag + "sil", [128, 2048], F32) as sil,
                nc.psum_tensor(tag + "pu0a", [128, 512], F32) as pu0a,
                nc.psum_tensor(tag + "pu0b", [128, 512], F32) as pu0b,
                nc.psum_tensor(tag + "pu1a", [128, 512], F32) as pu1a,
                nc.psum_tensor(tag + "pu1b", [128, 512], F32) as pu1b,
                nc.psum_tensor(tag + "pg0a", [128, 512], F32) as pg0a,
                nc.psum_tensor(tag + "pg0b", [128, 512], F32) as pg0b,
                nc.psum_tensor(tag + "pg1a", [128, 512], F32) as pg1a,
                nc.psum_tensor(tag + "pg1b", [128, 512], F32) as pg1b,
            ):
                pu = ((pu0a, pu0b), (pu1a, pu1b))
                pg = ((pg0a, pg0b), (pg1a, pg1b))

                @block.sync
                def _(sy):
                    sy.wait_ge(s_ag, 2)
                    for e in range(KE):
                        if e >= 2:
                            sy.wait_ge(s_ft, bft + 4 * (e - 1))
                        sy.dma_start(out=s3(tub[:, (e % 2) * 2048:(e % 2) * 2048 + 2048], 16),
                                     in_=r3p(tup_block(e))).then_inc(s_fw[e % 2], 16)
                        sy.dma_start(out=s3(tgb[:, (e % 2) * 2048:(e % 2) * 2048 + 2048], 16),
                                     in_=r3p(tg_block(e))).then_inc(s_fw[e % 2], 16)

                @block.tensor
                def _(t):
                    for e in range(KE):
                        t.wait_ge(s_fw[e % 2], bfw[e % 2] + 32 * (e // 2 + 1))
                        if e >= 2:
                            t.wait_ge(s_fs, bfs + 2 * (e - 1))
                            t.wait_ge(s_fv, bfv + 2 * (e - 1))
                        tu = tub[:, (e % 2) * 2048:(e % 2) * 2048 + 2048]
                        tg = tgb[:, (e % 2) * 2048:(e % 2) * 2048 + 2048]
                        for sp in range(2):
                            for k in range(KD):
                                mm = t.matmul(pu[e % 2][sp][:, :],
                                              tu[:, k * 128:(k + 1) * 128],
                                              xh[:, k * 1024 + sp * 512:k * 1024 + sp * 512 + 512],
                                              start=(k == 0), stop=(k == KD - 1))
                            mm.then_inc(s_ft, 1)
                        for sp in range(2):
                            for k in range(KD):
                                mm = t.matmul(pg[e % 2][sp][:, :],
                                              tg[:, k * 128:(k + 1) * 128],
                                              xh[:, k * 1024 + sp * 512:k * 1024 + sp * 512 + 512],
                                              start=(k == 0), stop=(k == KD - 1))
                            mm.then_inc(s_ft, 1)

                @block.scalar
                def _(s):
                    for e in range(KE):
                        if e >= 2:
                            s.wait_ge(s_fv, bfv + 2 * (e - 1))
                        for sp in range(2):
                            s.wait_ge(s_ft, bft + 4 * e + sp + 1)
                            s.activation(sil[:, (e % 2) * 1024 + sp * 512:(e % 2) * 1024 + sp * 512 + 512],
                                         pu[e % 2][sp][:, :],
                                         AF.Silu, scale=v_su[:]).then_inc(s_fs, 1)

                @block.vector
                def _(v):
                    for e in range(KE):
                        for sp in range(2):
                            v.wait_ge(s_fs, bfs + 2 * e + sp + 1)
                            v.wait_ge(s_ft, bft + 4 * e + 2 + sp + 1)
                            v.tensor_tensor(hid[:, e * 1024 + sp * 512:e * 1024 + sp * 512 + 512],
                                            sil[:, (e % 2) * 1024 + sp * 512:(e % 2) * 1024 + sp * 512 + 512],
                                            pg[e % 2][sp][:, :],
                                            ALU.mult).then_inc(s_fv, 1)

                phase_end()

        def emit_ffn_d(tag, hid, resid_dram, dst_dram, also_out, save_tail):
            s_dw = (sem("fdw0"), sem("fdw1"))
            s_du = (sem("fdu0"), sem("fdu1"))
            s_dt = sem("fdt")
            s_dv = sem("fdv")
            s_dst = (sem("fdst0"), sem("fdst1"))
            s_done = sem("fddone")
            per_m = 1 + (1 if also_out else 0) + (1 if save_tail else 0)
            stc = [[0] * 16, [0] * 16]
            c0 = c1 = 0
            for m in range(16):
                if m % 2 == 0:
                    c0 += per_m
                else:
                    c1 += per_m
                stc[0][m] = c0
                stc[1][m] = c1
            bdw = (takebase("fdw0", 16 * 8), takebase("fdw1", 16 * 8))
            bdu = (takebase("fdu0", 16 * 8), takebase("fdu1", 16 * 8))
            bdt = takebase("fdt", 32)
            bdv = takebase("fdv", 32)
            bdst = (takebase("fdst0", 16 * stc[0][-1]), takebase("fdst1", 16 * stc[1][-1]))
            bdone = takebase("fddone", 1)
            with (
                nc.sbuf_tensor(tag + "td", [128, 2 * 8192], BF16) as tdb,
                nc.sbuf_tensor(tag + "ub", [128, 2 * 1024], F32) as ub,
                nc.sbuf_tensor(tag + "x1b", [128, 2 * 1024], F32) as x1b,
                nc.psum_tensor(tag + "pd0a", [128, 512], F32) as pd0a,
                nc.psum_tensor(tag + "pd0b", [128, 512], F32) as pd0b,
                nc.psum_tensor(tag + "pd1a", [128, 512], F32) as pd1a,
                nc.psum_tensor(tag + "pd1b", [128, 512], F32) as pd1b,
            ):
                pd = ((pd0a, pd0b), (pd1a, pd1b))

                @block.sync
                def _(sy):
                    sy.wait_ge(s_ag, 3)
                    for m in range(16):
                        if m >= 2:
                            sy.wait_ge(s_dt, bdt + 2 * (m - 1))
                        sy.dma_start(out=s3(tdb[:, (m % 2) * 8192:(m % 2) * 8192 + 8192], 64),
                                     in_=r3p(td_block(m))).then_inc(s_dw[m % 2], 16)
                        if m >= 2:
                            sy.wait_ge(s_dv, bdv + 2 * (m - 1))
                        sy.dma_start(out=ub[:, (m % 2) * 1024:(m % 2) * 1024 + 1024],
                                     in_=resid_dram[m * 128:(m + 1) * 128, :]).then_inc(s_du[m % 2], 16)

                @block.tensor
                def _(t):
                    for m in range(16):
                        t.wait_ge(s_dw[m % 2], bdw[m % 2] + 16 * (m // 2 + 1))
                        if m >= 2:
                            t.wait_ge(s_dv, bdv + 2 * (m - 1))
                        td = tdb[:, (m % 2) * 8192:(m % 2) * 8192 + 8192]
                        for sp in range(2):
                            for e in range(KE):
                                mm = t.matmul(pd[m % 2][sp][:, :],
                                              td[:, e * 128:(e + 1) * 128],
                                              hid[:, e * 1024 + sp * 512:e * 1024 + sp * 512 + 512],
                                              start=(e == 0), stop=(e == KE - 1))
                            mm.then_inc(s_dt, 1)

                @block.vector
                def _(v):
                    for m in range(16):
                        v.wait_ge(s_du[m % 2], bdu[m % 2] + 16 * (m // 2 + 1))
                        if m >= 2:
                            v.wait_ge(s_dst[m % 2], bdst[m % 2] + 16 * stc[m % 2][m - 2])
                        for sp in range(2):
                            v.wait_ge(s_dt, bdt + 2 * m + sp + 1)
                            v.scalar_tensor_tensor(
                                x1b[:, (m % 2) * 1024 + sp * 512:(m % 2) * 1024 + sp * 512 + 512],
                                pd[m % 2][sp][:, :],
                                v_sdg[:],
                                ub[:, (m % 2) * 1024 + sp * 512:(m % 2) * 1024 + sp * 512 + 512],
                                ALU.mult, ALU.add).then_inc(s_dv, 1)

                @block.scalar
                def _(s):
                    for m in range(16):
                        s.wait_ge(s_dv, bdv + 2 * (m + 1))
                        xb = x1b[:, (m % 2) * 1024:(m % 2) * 1024 + 1024]
                        s.dma_start(out=dst_dram[m * 128:(m + 1) * 128, :],
                                    in_=xb).then_inc(s_dst[m % 2], 16)
                        if also_out:
                            s.dma_start(out=out[m * 128:(m + 1) * 128, :],
                                        in_=xb).then_inc(s_dst[m % 2], 16)
                        if save_tail:
                            s.dma_start(out=ag2_in.ap()[m * 128:(m + 1) * 128, 0:4],
                                        in_=xb[:, 1020:1024]).then_inc(s_dst[m % 2], 16)
                    s.wait_ge(s_dst[0], bdst[0] + 16 * stc[0][-1])
                    s.wait_ge(s_dst[1], bdst[1] + 16 * stc[1][-1])
                    s.sem_inc(s_done, 1)
            return (s_done, bdone + 1)

        with nc.sbuf_tensor("hid1", [128, KE * 1024], BF16) as hid1:
            with nc.sbuf_tensor("xh1", [128, KD * 1024], BF16) as xh1:
                emit_ln("l1", u_fm, xh1)
                phase_end()
                if stage < 0.7:
                    emit_dummy_out()
                    return nc
                emit_ffn_u("f1", xh1, hid1)

                @block.gpsimd
                def _(g):
                    g.wait_ge(s_ts[0], 576)
                    g.wait_ge(s_ts[1], 576)
                    g.collective_compute("AllGather", ALU.bypass,
                                         ins=[ag1_in.ap()[32768:49152, :].opt()],
                                         outs=[ag1b_out.ap().opt()],
                                         replica_groups=rg8).then_inc(s_ag, 1)

                if stage < 0.9:
                    emit_dummy_out()
                    return nc
            f1done, f1cnt = emit_ffn_d("f1", hid1, u_fm, xs1, stage == 1, True)

        @block.gpsimd
        def _(g):
            g.wait_ge(f1done, f1cnt)
            g.collective_compute("AllGather", ALU.bypass, ins=[ag2_in.ap().opt()],
                                 outs=[ag2_out.ap().opt()],
                                 replica_groups=rg8).then_inc(s_ag, 1)

        phase_end()

        s_dbg = sem("dbg")

        @block.sync
        def _(sy):
            sy.dma_start(out=dbg[0:1, :], in_=s_row[:, :]).then_inc(s_dbg, 16)
            sy.dma_start(out=dbg[1:2, 0:1], in_=v_su[0:1, :]).then_inc(s_dbg, 16)
            sy.dma_start(out=dbg[2:3, 0:1], in_=v_sdg[0:1, :]).then_inc(s_dbg, 16)
            sy.dma_start(out=dbg[3:4, 0:1], in_=v_thu[0:1, :]).then_inc(s_dbg, 16)
            sy.wait_ge(s_dbg, 64)

        def emit_scan(stage):
            import collections
            CNT = collections.defaultdict(int)

            def E(instr, s, key, n=1):
                CNT[key] += n
                instr.then_inc(s, n)
                return CNT[key]

            s2v = sem("s2v")
            s2s = sem("s2s")
            s2t = sem("s2t")
            s2l = sem("s2l")

            @block.vector
            def _(v):
                E(v.tensor_scalar(dAf[:], dtf[:], c_av[:], None, ALU.mult), s2v, "2v")
                v.wait_ge(s2v, CNT["2v"])
                for c in range(NCH):
                    E(v.tensor_tensor_scan(dAcs[:, 128 * c:128 * c + 128],
                                           dAf[:, 128 * c:128 * c + 128],
                                           dAf[:, 128 * c:128 * c + 128],
                                           0.0, ALU.add, ALU.bypass), s2v, "2v")
                v.wait_ge(s2v, CNT["2v"])
                dv3 = dAcs[:].rearrange("h (c l) -> h c l", c=NCH)
                E(v.tensor_copy(lasts[:, 0:8].rearrange("h (c o) -> h c o", o=1),
                                dv3[:, :, 127:128]), s2v, "2v")
                v.wait_ge(s2v, CNT["2v"])
                E(v.tensor_tensor_scan(lasts[:, 8:16], lasts[:, 0:8], lasts[:, 0:8],
                                       0.0, ALU.add, ALU.bypass), s2v, "2v")
                v.wait_ge(s2v, CNT["2v"])
                E(v.tensor_scalar(lasts[:, 16:24], lasts[:, 8:16], -1.0,
                                  lasts[:, 15:16], ALU.mult, ALU.add), s2v, "2v")
            V_DACS = 9
            V_TAIL = CNT["2v"]

            @block.scalar
            def _(s):
                s.wait_ge(s2v, V_DACS)
                for c in range(NCH):
                    E(s.activation(decayf[:, 128 * c:128 * c + 128],
                                   dAcs[:, 128 * c:128 * c + 128], AF.Exp,
                                   bias=dAcs[:, 128 * c + 127:128 * c + 128],
                                   scale=c_fc[0:64, 3:4]), s2s, "2s")
                s.wait_ge(s2v, V_TAIL)
                E(s.activation(tpf[:], lasts[:, 16:24], AF.Exp), s2s, "2s")
                E(s.activation(cdf[:], lasts[:, 0:8], AF.Exp), s2s, "2s")
            S_DEC = 8
            S_CD = CNT["2s"]

            @block.vector
            def _(v):
                v.wait_ge(s2s, S_DEC)
                E(v.tensor_tensor(dtdyf[:], dtf[:], decayf[:], ALU.mult), s2v, "2v")
                v.wait_ge(s2v, CNT["2v"])
                v.wait_ge(s2s, S_DEC + 2)
                for c in range(NCH):
                    E(v.tensor_scalar(dtdtf[:, 128 * c:128 * c + 128],
                                      dtdyf[:, 128 * c:128 * c + 128],
                                      tpf[:, c:c + 1], None, ALU.mult), s2v, "2v")
            V_DTDT = CNT["2v"]

            @block.sync
            def _(sy):
                sy.wait_ge(s2s, S_CD)
                sy.dma_start(out=cdrow[:], in_=cdf[:]).then_inc(s2l, 16)

            with (
                nc.psum_tensor("p2c", [128, 512], F32) as p2c,
                nc.psum_tensor("p2t0", [128, 512], F32) as p2t0,
                nc.psum_tensor("p2t1", [128, 512], F32) as p2t1,
            ):
                p2t = (p2t0, p2t1)

                @block.tensor
                def _(t):
                    t.wait_ge(s2l, 16)
                    E(t.matmul(p2c[:], c_onesf[0:1, 0:128], cdrow[:],
                               start=True, stop=True), s2t, "2t")
                    t.wait_ge(s2v, V_DTDT)
                    for c in range(NCH):
                        if c >= 2:
                            t.wait_ge(s2v, V_DTDT + 1 + 4 * (c - 1))
                        for si, srcm in enumerate((dAcs, dtf, dtdyf, dtdtf)):
                            E(t.matmul(p2t[c % 2][:, 128 * si:128 * si + 64],
                                       srcm[:, 128 * c:128 * c + 128],
                                       c_identf[0:64, 0:64], start=True, stop=True,
                                       is_transpose=True), s2t, "2t")

                @block.vector
                def _(v):
                    v.wait_ge(s2t, 1)
                    E(v.tensor_copy(cdb[:], p2c[:]), s2v, "2v")
                    for c in range(NCH):
                        v.wait_ge(s2t, 1 + 4 * (c + 1))
                        for si, dstm in enumerate((datok, dttok, dctok, ddtok)):
                            E(v.tensor_copy(dstm[:, 64 * c:64 * c + 64],
                                            p2t[c % 2][:, 128 * si:128 * si + 64]),
                              s2v, "2v")
                phase_end()

            s3l = (sem("s3l0"), sem("s3l1"))
            s3t = sem("s3t")
            s3e = sem("s3e")
            s3v2 = sem("s3v2")
            s3t2 = sem("s3t2")
            s3a = sem("s3a")
            with (
                nc.sbuf_tensor("xbw", [128, 2 * 2048], BF16) as xbw,
                nc.sbuf_tensor("xdb1", [128, 2 * 512], BF16) as xdb1,
                nc.sbuf_tensor("scac", [128, 4096], F32) as scac,
                nc.sbuf_tensor("ag3st", [128, 4096], BF16) as ag3st,
                nc.psum_tensor("pT0", [128, 512], BF16) as pT0,
                nc.psum_tensor("pT1", [128, 512], BF16) as pT1,
                nc.psum_tensor("p3a0", [128, 512], F32) as p3a0,
                nc.psum_tensor("p3a1", [128, 512], F32) as p3a1,
            ):
                pT = (pT0, pT1)
                p3a = (p3a0, p3a1)

                @block.sync
                def _(sy):
                    for c in range(NCH):
                        if c >= 2:
                            sy.wait_ge(s3t, 16 * (c - 1))
                        sy.dma_start(
                            out=xbw[:, (c % 2) * 2048:(c % 2) * 2048 + 2048].rearrange(
                                "p (j w) -> p j w", j=16),
                            in_=xbcsc.ap()[0:2048, 128 * c:128 * c + 128].rearrange(
                                "(j p) w -> p j w", p=128)).then_inc(s3l[c % 2], 16)

                @block.tensor
                def _(t):
                    for c in range(NCH):
                        t.wait_ge(s3l[c % 2], 16 * (c // 2 + 1))
                        for f in range(4):
                            Tf = 4 * c + f
                            if Tf >= 2:
                                t.wait_ge(s3e, Tf - 1)
                            for i in range(4):
                                E(t.matmul(pT[Tf % 2][:, 128 * i:128 * i + 128],
                                           xbw[:, (c % 2) * 2048 + 128 * (4 * f + i):
                                               (c % 2) * 2048 + 128 * (4 * f + i) + 128],
                                           c_identb[:], start=True, stop=True,
                                           is_transpose=True), s3t, "3t")

                @block.vector
                def _(v):
                    for c in range(NCH):
                        for f in range(4):
                            Tf = 4 * c + f
                            v.wait_ge(s3t, 4 * (Tf + 1))
                            if f < 2:
                                dst = xtokb[:, 1024 * c + 512 * f:1024 * c + 512 * f + 512]
                            else:
                                dst = btokb[:, 1024 * c + 512 * (f - 2):
                                            1024 * c + 512 * (f - 2) + 512]
                            E(v.tensor_copy(dst, pT[Tf % 2][:]), s3e, "3e")

                def emit_accum(it2):
                    c, hg = it2 // 8, it2 % 8

                    @block.vector
                    def _(v):
                        v.wait_ge(s3t2, it2 + 1)
                        if c == 0:
                            E(v.tensor_copy(scac[:, 512 * hg:512 * hg + 512],
                                            p3a[it2 % 2][:]), s3a, "3a")
                        else:
                            v.wait_ge(s3a, 8 * (c - 1) + hg + 1)
                            E(v.scalar_tensor_tensor(scac[:, 512 * hg:512 * hg + 512],
                                                     p3a[it2 % 2][:], 1.0,
                                                     scac[:, 512 * hg:512 * hg + 512],
                                                     ALU.mult, ALU.add), s3a, "3a")

                for it2 in range(64):
                    c, hg = it2 // 8, it2 % 8

                    @block.vector
                    def _(v):
                        if it2 == 0:
                            v.wait_ge(s3e, 32)
                        if it2 >= 2:
                            v.wait_ge(s3t2, it2 - 1)
                        for hh in range(8):
                            h = 8 * hg + hh
                            v.tensor_scalar(
                                xdb1[:, (it2 % 2) * 512 + 64 * hh:(it2 % 2) * 512 + 64 * hh + 64],
                                xtokb[:, 1024 * c + 128 * hg + 64 * (hh % 2):
                                      1024 * c + 128 * hg + 64 * (hh % 2) + 64],
                                ddtok[:, 64 * c + h:64 * c + h + 1], None,
                                ALU.mult).then_inc(s3v2, 1)

                    @block.tensor
                    def _(t):
                        t.wait_ge(s3v2, 8 * (it2 + 1))
                        if it2 >= 2:
                            t.wait_ge(s3a, it2 - 1)
                        for hh in range(8):
                            mm = t.matmul(p3a[it2 % 2][:, 64 * hh:64 * hh + 64],
                                          btokb[:, 1024 * c + 128 * hg:
                                                1024 * c + 128 * hg + 128],
                                          xdb1[:, (it2 % 2) * 512 + 64 * hh:
                                               (it2 % 2) * 512 + 64 * hh + 64],
                                          start=True, stop=True)
                        mm.then_inc(s3t2, 1)

                    if it2 >= 1:
                        emit_accum(it2 - 1)
                emit_accum(63)

                @block.vector
                def _(v):
                    v.wait_ge(s3a, 64)
                    E(v.tensor_copy(ag3st[:], scac[:]), s3a, "3a")

                @block.sync
                def _(sy):
                    sy.wait_ge(s3a, 65)
                    sy.dma_start(out=ag3_in.ap(), in_=ag3st[:]).then_inc(s3l[0], 16)

                @block.gpsimd
                def _(g):
                    g.wait_ge(s3l[0], 16 * 5)
                    g.collective_compute("AllGather", ALU.bypass, ins=[ag3_in.ap().opt()],
                                         outs=[ag3_out.ap().opt()],
                                         replica_groups=rg8).then_inc(s_ag, 1)
                phase_end()

            s4v = sem("s4v")
            s3l_base = (16 * 5, 16 * 4)
            with nc.sbuf_tensor("agw", [128, 2 * 4096], BF16) as agw:
                @block.sync
                def _(sy):
                    sy.wait_ge(s_ag, 5)
                    for r in range(8):
                        if r >= 2:
                            sy.wait_ge(s4v, r - 1)
                        sy.dma_start(out=agw[:, (r % 2) * 4096:(r % 2) * 4096 + 4096],
                                     in_=ag3_out.ap()[128 * r:128 * r + 128, :]
                                     ).then_inc(s3l[r % 2], 16)

                @block.vector
                def _(v):
                    for r in range(8):
                        v.wait_ge(s3l[r % 2], s3l_base[r % 2] + 16 * (r // 2 + 1))
                        if r == 0:
                            E(v.tensor_scalar(carry[:], agw[:, 0:4096],
                                              c_hmask[:, 0:1], None, ALU.mult),
                              s4v, "4v")
                        else:
                            v.wait_ge(s4v, r)
                            E(v.scalar_tensor_tensor(carry[:],
                                                     agw[:, (r % 2) * 4096:(r % 2) * 4096 + 4096],
                                                     c_hmask[:, r:r + 1], carry[:],
                                                     ALU.mult, ALU.add), s4v, "4v")
                    v.wait_ge(s4v, 8)
                    E(v.tensor_copy(carryb[:], carry[:]), s4v, "4v")
                phase_end()

            s5l = sem("s5l")
            s5g = (sem("s5g0"), sem("s5g1"))
            s5t = sem("s5t")
            s5v = sem("s5v")
            s5s = sem("s5s")
            s5p = sem("s5p")
            s5r = sem("s5r")
            s5st = sem("s5st")
            s5w = (sem("s5w0"), sem("s5w1"))
            m5ctx = contextlib.ExitStack()
            with m5ctx:
                xw = m5ctx.enter_context(nc.sbuf_tensor("xw", [128, 48 * 128], BF16))
                szb = m5ctx.enter_context(nc.sbuf_tensor("szb", [128, 4096], BF16))
                x1w = m5ctx.enter_context(nc.sbuf_tensor("x1w", [128, 16 * 128], F32))
                dar = m5ctx.enter_context(nc.sbuf_tensor("dar", [1, 2 * 1024], F32))
                negf = m5ctx.enter_context(nc.sbuf_tensor("negf", [1, 128], F32))
                rcb = m5ctx.enter_context(nc.sbuf_tensor("rcb", [128, 2 * 512], F32))
                ebb = m5ctx.enter_context(nc.sbuf_tensor("ebb", [128, 2 * 512], BF16))
                e2b = m5ctx.enter_context(nc.sbuf_tensor("e2b", [128, 2 * 512], BF16))
                gmb = m5ctx.enter_context(nc.sbuf_tensor("gmb", [128, 2 * 256], BF16))
                glb = m5ctx.enter_context(nc.sbuf_tensor("glb", [128, 2 * 512], BF16))
                cvb = m5ctx.enter_context(nc.sbuf_tensor("cvb", [128, 2 * 512], BF16))
                xdbv = m5ctx.enter_context(nc.sbuf_tensor("xdbv", [128, 2 * 256], BF16))
                sqb2 = m5ctx.enter_context(nc.sbuf_tensor("sqb2", [128, 2 * 256], BF16))
                ybb = m5ctx.enter_context(nc.sbuf_tensor("ybb", [128, 4096], BF16))
                yhb = m5ctx.enter_context(nc.sbuf_tensor("yhb", [128, 4096], BF16))
                sdb = m5ctx.enter_context(nc.sbuf_tensor("sdb", [32, 128], F32))
                sdbb = m5ctx.enter_context(nc.sbuf_tensor("sdbb", [32, 128], BF16))
                rrow = m5ctx.enter_context(nc.sbuf_tensor("rrow", [1, 4096], BF16))
                wob = m5ctx.enter_context(nc.sbuf_tensor("wob", [128, 2 * 4096], BF16))
                x2w = m5ctx.enter_context(nc.sbuf_tensor("x2w", [128, 16 * 128], F32))
                pR = (m5ctx.enter_context(nc.psum_tensor("pR0", [128, 512], F32)),
                      m5ctx.enter_context(nc.psum_tensor("pR1", [128, 512], F32)))
                pEB = (m5ctx.enter_context(nc.psum_tensor("pEB0", [128, 512], F32)),
                       m5ctx.enter_context(nc.psum_tensor("pEB1", [128, 512], F32)))
                pG2 = m5ctx.enter_context(nc.psum_tensor("pG2", [128, 512], F32))
                pY2 = m5ctx.enter_context(nc.psum_tensor("pY2", [128, 512], F32))
                pS2 = m5ctx.enter_context(nc.psum_tensor("pS2", [128, 512], F32))
                pGN = m5ctx.enter_context(nc.psum_tensor("pGN", [32, 512], F32))

                @block.vector
                def _(v):
                    E(v.tensor_scalar(negf[:], c_onesf[0:1, 0:128], -1.0, None,
                                      ALU.mult), s5v, "5v")
                NEG_V = CNT["5v"]

                for c in range(NCH):
                    @block.sync
                    def _(sy):
                        sy.dma_start(
                            out=xw[:].rearrange("p (j w) -> p j w", j=48),
                            in_=xbcsc.ap()[:, 128 * c:128 * c + 128].rearrange(
                                "(j p) w -> p j w", p=128)).then_inc(s5l, 16)
                        sy.dma_start(
                            out=szb[:].rearrange("p (j w) -> p j w", j=32),
                            in_=zsc.ap()[:, 128 * c:128 * c + 128].rearrange(
                                "(j p) w -> p j w", p=128)).then_inc(s5l, 16)
                        sy.dma_start(
                            out=x1w[:].rearrange("p (j w) -> p j w", j=16),
                            in_=xs1.ap()[:, 128 * c:128 * c + 128].rearrange(
                                "(j p) w -> p j w", p=128)).then_inc(s5l, 16)
                    CNT["5l"] += 3
                    lw_all = CNT["5l"]

                    @block.scalar
                    def _(s):
                        s.wait_ge(s5l, 16 * lw_all)
                        E(s.activation(szb[:], szb[:], AF.Silu), s5s, "5s")
                    sz_cnt = CNT["5s"]

                    t1r = {}
                    t1x = {}
                    v1c = {}
                    v2a = {}
                    s1a = {}
                    s1c = {}
                    gpc = {}
                    t2c = {}
                    v3c = {}
                    v3y = {}
                    v2b = {}
                    gp2 = {}
                    darn = {}

                    def emit_T1(g):
                        s_ = g % 2
                        hg = g // 2
                        if g % 2 == 0:
                            gate = t1x.get(2 * (hg - 2) + 1)

                            @block.sync
                            def _(sy):
                                if gate is not None:
                                    sy.wait_ge(s5t, gate)
                                sy.dma_start(
                                    out=dar[0:1, (hg % 2) * 1024:(hg % 2) * 1024 + 1024],
                                    in_=dAcs[8 * hg:8 * hg + 8, 128 * c:128 * c + 128]
                                    ).then_inc(s5g[hg % 2], 16)
                            CNT["5g%d" % (hg % 2)] += 1
                            darn[hg] = CNT["5g%d" % (hg % 2)]

                        @block.tensor
                        def _(t):
                            t.wait_ge(s5g[hg % 2], 16 * darn[hg])
                            t.wait_ge(s5l, 16 * lw_all)
                            t.wait_ge(s5v, v2b.get(g - 2, NEG_V))
                            ws = s1c.get(g - 2)
                            if ws is not None:
                                t.wait_ge(s5s, ws)
                            mm = None
                            for i in range(4):
                                hh = 4 * s_ + i
                                dsl = dar[0:1, (hg % 2) * 1024 + 128 * hh:
                                          (hg % 2) * 1024 + 128 * hh + 128]
                                t.matmul(pR[s_][:, 128 * i:128 * i + 128],
                                         c_onesf[0:1, 0:128], dsl,
                                         start=True, stop=False)
                                mm = t.matmul(pR[s_][:, 128 * i:128 * i + 128],
                                              dsl, negf[0:1, 0:128],
                                              start=False, stop=True)
                            E(mm, s5t, "5t")
                            t1r[g] = CNT["5t"]
                            for i in range(4):
                                hh = 4 * s_ + i
                                dsl = dar[0:1, (hg % 2) * 1024 + 128 * hh:
                                          (hg % 2) * 1024 + 128 * hh + 128]
                                mm = t.matmul(pEB[s_][:, 128 * i:128 * i + 128],
                                              c_onesf[0:1, 0:128], dsl,
                                              start=True, stop=True)
                            for jj in range(2):
                                j = 2 * g + jj
                                mm = t.matmul(pG2[:, 256 * s_ + 128 * jj:
                                                  256 * s_ + 128 * jj + 128],
                                              xw[:, 128 * (8 + hg):128 * (8 + hg) + 128],
                                              xw[:, 128 * (16 + j):128 * (16 + j) + 128],
                                              start=True, stop=True)
                            E(mm, s5t, "5t")
                            t1x[g] = CNT["5t"]

                    def emit_V1(g):
                        s_ = g % 2
                        hg = g // 2

                        @block.vector
                        def _(v):
                            v.wait_ge(s5t, t1r[g])
                            E(v.tensor_scalar(rcb[:, 512 * s_:512 * s_ + 512],
                                              pR[s_][:], 0.0, None, ALU.min),
                              s5v, "5v")
                            v1c[g] = CNT["5v"]
                            wg = t2c.get(g - 2)
                            if wg is not None:
                                v.wait_ge(s5t, wg)
                            for i in range(4):
                                h = 4 * g + i
                                ro = 64 * (i % 2)
                                E(v.tensor_scalar(xdbv[:, 256 * s_ + 64 * i:
                                                       256 * s_ + 64 * i + 64],
                                                  xtokb[:, 1024 * c + 128 * hg + ro:
                                                        1024 * c + 128 * hg + ro + 64],
                                                  dctok[:, 64 * c + h:64 * c + h + 1],
                                                  None, ALU.mult), s5v, "5v")
                            v2a[g] = CNT["5v"]

                    def emit_S1(g):
                        s_ = g % 2

                        @block.scalar
                        def _(s):
                            s.wait_ge(s5v, v1c[g])
                            s.wait_ge(s5t, t1x[g])
                            E(s.activation(ebb[:, 512 * s_:512 * s_ + 512],
                                           rcb[:, 512 * s_:512 * s_ + 512], AF.Exp),
                              s5s, "5s")
                            s1a[g] = CNT["5s"]
                            E(s.activation(e2b[:, 512 * s_:512 * s_ + 512],
                                           pEB[s_][:], AF.Exp), s5s, "5s")
                            s1c[g] = CNT["5s"]

                    def emit_GPc(g):
                        s_ = g % 2

                        @block.gpsimd
                        def _(gp):
                            gp.wait_ge(s5s, s1c[g])
                            wg = t2c.get(g - 2)
                            if wg is not None:
                                gp.wait_ge(s5t, wg)
                            for i in range(4):
                                j = 2 * g + i // 2
                                E(gp.tensor_tensor(cvb[:, 512 * s_ + 128 * i:
                                                       512 * s_ + 128 * i + 128],
                                                   xw[:, 128 * (16 + j):128 * (16 + j) + 128],
                                                   e2b[:, 512 * s_ + 128 * i:
                                                       512 * s_ + 128 * i + 128],
                                                   ALU.mult), s5p, "5p")
                            gpc[g] = CNT["5p"]

                    def emit_V3(g):
                        s_ = g % 2

                        @block.vector
                        def _(v):
                            v.wait_ge(s5t, t2c[g])
                            for i in range(4):
                                h = 4 * g + i
                                E(v.tensor_scalar(carry[:, 64 * h:64 * h + 64],
                                                  carry[:, 64 * h:64 * h + 64],
                                                  cdb[:, 8 * h + c:8 * h + c + 1],
                                                  None, ALU.mult), s5v, "5v")
                            v.wait_ge(s5v, CNT["5v"])
                            E(v.scalar_tensor_tensor(carry[:, 256 * g:256 * g + 256],
                                                     pS2[:, 256 * s_:256 * s_ + 256], 1.0,
                                                     carry[:, 256 * g:256 * g + 256],
                                                     ALU.mult, ALU.add), s5v, "5v")
                            v3c[g] = CNT["5v"]
                            v.wait_ge(s5s, sz_cnt)
                            E(v.tensor_tensor(ybb[:, 256 * g:256 * g + 256],
                                              pY2[:, 256 * s_:256 * s_ + 256],
                                              szb[:, 256 * g:256 * g + 256],
                                              ALU.mult), s5v, "5v")
                            v3y[g] = CNT["5v"]

                    def emit_V2b(g):
                        s_ = g % 2

                        @block.vector
                        def _(v):
                            v.wait_ge(s5s, s1a[g])
                            v.wait_ge(s5t, t1x[g])
                            for jj in range(2):
                                E(v.tensor_tensor(gmb[:, 256 * s_ + 128 * jj:
                                                      256 * s_ + 128 * jj + 128],
                                                  pG2[:, 256 * s_ + 128 * jj:
                                                      256 * s_ + 128 * jj + 128],
                                                  c_trilb[:], ALU.mult), s5v, "5v")
                            v.wait_ge(s5v, CNT["5v"])
                            for i in range(4):
                                h = 4 * g + i
                                jj = i // 2
                                E(v.scalar_tensor_tensor(
                                    glb[:, 512 * s_ + 128 * i:512 * s_ + 128 * i + 128],
                                    ebb[:, 512 * s_ + 128 * i:512 * s_ + 128 * i + 128],
                                    dttok[:, 64 * c + h:64 * c + h + 1],
                                    gmb[:, 256 * s_ + 128 * jj:256 * s_ + 128 * jj + 128],
                                    ALU.mult, ALU.mult), s5v, "5v")
                            v2b[g] = CNT["5v"]

                    def emit_T2(g):
                        s_ = g % 2
                        hg = g // 2

                        @block.tensor
                        def _(t):
                            t.wait_ge(s5v, v2b[g])
                            t.wait_ge(s5p, gpc[g])
                            wg = gp2.get(g - 1)
                            if wg is not None:
                                t.wait_ge(s5p, wg)
                            if g >= 1:
                                for jj in range(2):
                                    jp = 2 * (g - 1) + jj
                                    t.matmul(pGN[:, 0:128], c_gnoh[:, 32 * jp:32 * jp + 32],
                                             sqb2[:, 256 * ((g - 1) % 2) + 128 * jj:
                                                  256 * ((g - 1) % 2) + 128 * jj + 128],
                                             start=(jp == 0), stop=(jp == 31))
                            mm = None
                            for i in range(4):
                                h = 4 * g + i
                                jj = i // 2
                                j = 2 * g + jj
                                ro = 64 * (i % 2)
                                yt = pY2[ro:ro + 64, 256 * s_ + 128 * jj:
                                         256 * s_ + 128 * jj + 128]
                                t.matmul(yt,
                                         xtokb[:, 1024 * c + 128 * hg + ro:
                                               1024 * c + 128 * hg + ro + 64],
                                         glb[:, 512 * s_ + 128 * i:512 * s_ + 128 * i + 128],
                                         start=True, stop=False)
                                t.matmul(yt, carryb[:, 64 * h:64 * h + 64],
                                         cvb[:, 512 * s_ + 128 * i:512 * s_ + 128 * i + 128],
                                         start=False, stop=False)
                                t.matmul(yt,
                                         c_diagd[ro:ro + 64, 128 * j + ro:128 * j + ro + 64],
                                         xw[ro:ro + 64, 128 * hg:128 * hg + 128],
                                         start=False, stop=True)
                                mm = t.matmul(pS2[:, 256 * s_ + 64 * i:256 * s_ + 64 * i + 64],
                                              btokb[:, 1024 * c + 128 * hg:
                                                    1024 * c + 128 * hg + 128],
                                              xdbv[:, 256 * s_ + 64 * i:256 * s_ + 64 * i + 64],
                                              start=True, stop=True)
                            E(mm, s5t, "5t")
                            t2c[g] = CNT["5t"]

                    def emit_GP2(g):
                        s_ = g % 2

                        @block.gpsimd
                        def _(gp):
                            gp.wait_ge(s5v, v3y[g])
                            E(gp.tensor_copy(carryb[:, 256 * g:256 * g + 256],
                                             carry[:, 256 * g:256 * g + 256]), s5p, "5p")
                            E(gp.tensor_tensor(sqb2[:, 256 * s_:256 * s_ + 256],
                                               ybb[:, 256 * g:256 * g + 256],
                                               ybb[:, 256 * g:256 * g + 256],
                                               ALU.mult), s5p, "5p")
                            gp2[g] = CNT["5p"]

                    for tk in range(17):
                        gA = tk
                        gB = tk - 1
                        if gA <= 15:
                            emit_T1(gA)
                            emit_V1(gA)
                            emit_S1(gA)
                            emit_GPc(gA)
                        if gB >= 0:
                            emit_T2(gB)
                            emit_V3(gB)
                        if gA <= 15:
                            emit_V2b(gA)
                        if gB >= 0:
                            emit_GP2(gB)

                    @block.tensor
                    def _(t):
                        t.wait_ge(s5p, gp2[15])
                        mm = None
                        for jj in range(2):
                            jp = 30 + jj
                            mm = t.matmul(pGN[:, 0:128], c_gnoh[:, 32 * jp:32 * jp + 32],
                                          sqb2[:, 256 * (15 % 2) + 128 * jj:
                                               256 * (15 % 2) + 128 * jj + 128],
                                          start=False, stop=(jp == 31))
                        E(mm, s5t, "5t")
                    tgn = CNT["5t"]

                    @block.scalar
                    def _(s):
                        s.wait_ge(s5t, tgn)
                        E(s.activation(sdb[:], pGN[:, 0:128], AF.Sqrt,
                                       bias=c_fc[0:32, 0:1], scale=c_fc[0:32, 1:2]),
                          s5s, "5s")
                    srs = CNT["5s"]

                    @block.vector
                    def _(v):
                        v.wait_ge(s5s, srs)
                        E(v.reciprocal(sdb[:], sdb[:]), s5v, "5v")
                        v.wait_ge(s5v, CNT["5v"])
                        E(v.tensor_copy(sdbb[:], sdb[:]), s5v, "5v")
                    vrs = CNT["5v"]

                    @block.sync
                    def _(sy):
                        sy.wait_ge(s5v, vrs)
                        sy.dma_start(out=rrow[:], in_=sdbb[:]).then_inc(s5r, 16)
                    CNT["5r"] += 1

                    yh_after = {}
                    for q in range(8):
                        pv = yh_after.get(q - 2)

                        @block.tensor
                        def _(t):
                            t.wait_ge(s5r, 16 * CNT["5r"])
                            if pv is not None:
                                t.wait_ge(s5v, pv)
                            mm = None
                            for qq in range(4):
                                g2 = 4 * q + qq
                                mm = t.matmul(pEB[q % 2][:, 128 * qq:128 * qq + 128],
                                              c_onesb[0:1, 0:128],
                                              rrow[0:1, 128 * g2:128 * g2 + 128],
                                              start=True, stop=True)
                            E(mm, s5t, "5t")
                        trb = CNT["5t"]

                        @block.vector
                        def _(v):
                            v.wait_ge(s5t, trb)
                            for qq in range(4):
                                g2 = 4 * q + qq
                                E(v.scalar_tensor_tensor(yhb[:, 128 * g2:128 * g2 + 128],
                                                         ybb[:, 128 * g2:128 * g2 + 128],
                                                         c_normw[:, g2:g2 + 1],
                                                         pEB[q % 2][:, 128 * qq:128 * qq + 128],
                                                         ALU.mult, ALU.mult), s5v, "5v")
                        yh_after[q] = CNT["5v"]

                    yv_all = CNT["5v"]
                    tom_after = {}
                    oev_after = {}
                    for m in range(16):
                        tp = tom_after.get(m - 2)

                        @block.gpsimd
                        def _(gp):
                            if tp is not None:
                                gp.wait_ge(s5t, tp)
                            gp.dma_start(
                                out=s3(wob[:, (m % 2) * 4096:(m % 2) * 4096 + 4096], 32),
                                in_=r3(w_out_t[m * 4096:(m + 1) * 4096, :], 32)
                                ).then_inc(s5w[m % 2], 16)
                        CNT["5w%d" % (m % 2)] += 1
                        wn = CNT["5w%d" % (m % 2)]
                        ev = oev_after.get(m - 2)

                        @block.tensor
                        def _(t):
                            t.wait_ge(s5w[m % 2], 16 * wn)
                            t.wait_ge(s5v, yv_all)
                            if ev is not None:
                                t.wait_ge(s5v, ev)
                            for kc in range(32):
                                mm = t.matmul(pR[m % 2][:, 0:128],
                                              wob[:, (m % 2) * 4096 + 128 * kc:
                                                  (m % 2) * 4096 + 128 * (kc + 1)],
                                              yhb[:, 128 * kc:128 * kc + 128],
                                              start=(kc == 0), stop=(kc == 31))
                            E(mm, s5t, "5t")
                        tom_after[m] = CNT["5t"]

                        @block.vector
                        def _(v):
                            v.wait_ge(s5t, tom_after[m])
                            E(v.tensor_tensor(x2w[:, 128 * m:128 * m + 128],
                                              pR[m % 2][:, 0:128], x1w[:, 128 * m:128 * m + 128],
                                              ALU.add), s5v, "5v")
                        oev_after[m] = CNT["5v"]

                    vall = CNT["5v"]

                    @block.sync
                    def _(sy):
                        sy.wait_ge(s5v, vall)
                        sy.dma_start(out=xs2.ap()[:, 128 * c:128 * c + 128].rearrange(
                                         "(j p) w -> p j w", p=128),
                                     in_=x2w[:].rearrange("p (j w) -> p j w", j=16)
                                     ).then_inc(s5st, 16)
                        if stage == 2:
                            sy.dma_start(out=out[:, 128 * c:128 * c + 128].rearrange(
                                             "(j p) w -> p j w", p=128),
                                         in_=x2w[:].rearrange("p (j w) -> p j w", j=16)
                                         ).then_inc(s5st, 16)
                    CNT["5st"] += 2 if stage == 2 else 1

                    phase_end(sync_wait=(s5st, 16 * CNT["5st"]))

        if stage >= 2:
            mctx = contextlib.ExitStack()
            xctx = contextlib.ExitStack()
            dtf = mctx.enter_context(nc.sbuf_tensor("dtf", [64, 1024], F32))
            dAf = mctx.enter_context(nc.sbuf_tensor("dAf", [64, 1024], F32))
            dAcs = mctx.enter_context(nc.sbuf_tensor("dAcs", [64, 1024], F32))
            decayf = mctx.enter_context(nc.sbuf_tensor("decayf", [64, 1024], F32))
            dtdyf = mctx.enter_context(nc.sbuf_tensor("dtdyf", [64, 1024], F32))
            dtdtf = mctx.enter_context(nc.sbuf_tensor("dtdtf", [64, 1024], F32))
            lasts = mctx.enter_context(nc.sbuf_tensor("lasts", [64, 24], F32))
            tpf = mctx.enter_context(nc.sbuf_tensor("tpf", [64, 8], F32))
            cdf = mctx.enter_context(nc.sbuf_tensor("cdf", [64, 8], F32))
            cdb = mctx.enter_context(nc.sbuf_tensor("cdb", [128, 512], F32))
            cdrow = mctx.enter_context(nc.sbuf_tensor("cdrow", [1, 512], F32))
            datok = mctx.enter_context(nc.sbuf_tensor("datok", [128, 512], F32))
            dttok = mctx.enter_context(nc.sbuf_tensor("dttok", [128, 512], F32))
            dctok = mctx.enter_context(nc.sbuf_tensor("dctok", [128, 512], F32))
            ddtok = mctx.enter_context(nc.sbuf_tensor("ddtok", [128, 512], F32))
            xtokb = mctx.enter_context(nc.sbuf_tensor("xtokb", [128, 8 * 1024], BF16))
            btokb = mctx.enter_context(nc.sbuf_tensor("btokb", [128, 8 * 1024], BF16))
            carry = mctx.enter_context(nc.sbuf_tensor("carry", [128, 4096], F32))
            carryb = mctx.enter_context(nc.sbuf_tensor("carryb", [128, 4096], BF16))
            xh2 = xctx.enter_context(nc.sbuf_tensor("xh2", [128, KD * 1027], BF16))

            s_h = sem("mh")
            s_hs = sem("mhs")
            s_ht = sem("mht")
            s_hl = sem("mhl")
            with (
                nc.sbuf_tensor("hg", [128, 512], F32) as hg,
                nc.sbuf_tensor("h64", [128, 64], F32) as h64,
                nc.sbuf_tensor("hb", [128, 64], BF16) as hb,
                nc.sbuf_tensor("hsq", [128, 64], BF16) as hsq,
                nc.sbuf_tensor("hrow", [1, 16], F32) as hrow,
                nc.sbuf_tensor("hd", [128, 4], F32) as hd,
                nc.psum_tensor("phSx", [1, 512], F32) as phSx,
                nc.psum_tensor("phSq", [1, 512], F32) as phSq,
                nc.psum_tensor("phM", [128, 8], F32) as phM,
            ):
                @block.sync
                def _(sy):
                    sy.wait_ge(s_ag, 4)
                    sy.dma_start(out=hg[:].rearrange("p (r k c) -> p r k c", r=8, k=16),
                                 in_=ag2_out.ap().rearrange("(r k p) c -> p r k c", p=128, k=16)
                                 ).then_inc(s_hl, 16)

                @block.vector
                def _(v):
                    v.wait_ge(s_hl, 16)
                    v.tensor_scalar(h64[:], hg[:, 0:64], c_hmask[:, 0:1], None,
                                    ALU.mult).then_inc(s_h, 1)
                    for r in range(1, 8):
                        v.wait_ge(s_h, r)
                        v.scalar_tensor_tensor(h64[:], hg[:, 64 * r:64 * r + 64],
                                               c_hmask[:, r:r + 1], h64[:],
                                               ALU.mult, ALU.add).then_inc(s_h, 1)
                    v.wait_ge(s_h, 8)
                    v.tensor_copy(hb[:], h64[:]).then_inc(s_h, 1)
                    v.tensor_tensor(hsq[:], h64[:], h64[:], ALU.mult).then_inc(s_h, 1)

                @block.tensor
                def _(t):
                    t.wait_ge(s_h, 10)
                    for k in range(KD):
                        t.matmul(phSx[0:1, 0:4], c_onesb[:, 0:1], hb[:, 4 * k:4 * k + 4],
                                 start=(k == 0), stop=(k == KD - 1))
                        mm = t.matmul(phSq[0:1, 0:4], c_onesb[:, 0:1], hsq[:, 4 * k:4 * k + 4],
                                      start=(k == 0), stop=(k == KD - 1))
                    mm.then_inc(s_ht, 1)

                @block.vector
                def _(v):
                    v.wait_ge(s_ht, 1)
                    v.tensor_scalar(hrow[:, 0:4], phSx[0:1, 0:4], 1.0 / D, None,
                                    ALU.mult).then_inc(s_h, 1)
                    v.wait_ge(s_h, 11)
                    v.tensor_tensor(hrow[:, 8:12], hrow[:, 0:4], hrow[:, 0:4],
                                    ALU.mult).then_inc(s_h, 1)
                    v.wait_ge(s_h, 12)
                    v.scalar_tensor_tensor(hrow[:, 4:8], phSq[0:1, 0:4], 1.0 / D,
                                           hrow[:, 8:12], ALU.mult,
                                           ALU.subtract).then_inc(s_h, 1)

                @block.scalar
                def _(s):
                    s.wait_ge(s_h, 13)
                    s.activation(hrow[:, 4:8], hrow[:, 4:8], AF.Sqrt,
                                 bias=c_fc[0:1, 0:1]).then_inc(s_hs, 1)

                @block.vector
                def _(v):
                    v.wait_ge(s_hs, 1)
                    v.reciprocal(hrow[:, 4:8], hrow[:, 4:8]).then_inc(s_h, 1)

                @block.tensor
                def _(t):
                    t.wait_ge(s_h, 14)
                    t.matmul(phM[:, 0:4], c_onesf[0:1, 0:128], hrow[:, 0:4],
                             start=True, stop=True)
                    t.matmul(phM[:, 4:8], c_onesf[0:1, 0:128], hrow[:, 4:8],
                             start=True, stop=True).then_inc(s_ht, 1)

                @block.vector
                def _(v):
                    v.wait_ge(s_ht, 2)
                    for k in range(KD):
                        if k >= 1:
                            v.wait_ge(s_h, 14 + 2 * k)
                        v.tensor_tensor(hd[:], h64[:, 4 * k:4 * k + 4],
                                        phM[:, 0:4], ALU.subtract).then_inc(s_h, 1)
                        v.wait_ge(s_h, 14 + 2 * k + 1)
                        v.tensor_tensor(xh2[:, k * 1027:k * 1027 + 3],
                                        hd[:, 1:4], phM[:, 5:8], ALU.mult).then_inc(s_h, 1)

            phase_end()
            emit_ln("l2", xs1.ap(), xh2, ncols=1024, colstride=1027, dst_off=3)
            phase_end()

            s_mw = (sem("mw0"), sem("mw1"))
            s_mt = sem("mt")
            s_mv = sem("mv")
            s_msc = sem("msc")
            s_mz = (sem("mz0"), sem("mz1"))
            s_mx = (sem("mx0"), sem("mx1"))
            Tc = [0] * (NF + 1)
            Vc = [0] * (NF + 1)
            Sc = [0] * (NF + 1)
            for f in range(NF):
                Tc[f + 1] = Tc[f] + (2 if (f < 32 or f == 80) else 3)
                Vc[f + 1] = Vc[f] + (2 if f < 32 else (4 if f < 80 else 0))
                Sc[f + 1] = Sc[f] + (0 if f < 32 else (4 if f < 80 else 3))
            zcum = [[0] * NF, [0] * NF]
            xcum = [[0] * NF, [0] * NF]
            a = b = c2 = d2 = 0
            for f in range(NF):
                if f < 32:
                    if f % 2 == 0:
                        a += 1
                    else:
                        b += 1
                elif f < 80:
                    if f % 2 == 0:
                        c2 += 1
                    else:
                        d2 += 1
                zcum[0][f] = a
                zcum[1][f] = b
                xcum[0][f] = c2
                xcum[1][f] = d2

            with (
                nc.sbuf_tensor("wib", [128, 2 * 2048], BF16) as wib,
                nc.sbuf_tensor("zstb", [128, 2 * 1024], BF16) as zstb,
                nc.sbuf_tensor("xbcf", [128, 2 * 1027], F32) as xbcf,
                nc.sbuf_tensor("accf", [128, 2 * 1024], F32) as accf,
                nc.sbuf_tensor("xbst", [128, 2 * 1024], BF16) as xbst,
                nc.sbuf_tensor("ef", [64, 1024], F32) as ef,
                nc.psum_tensor("mpA0", [128, 512], F32) as mpA0,
                nc.psum_tensor("mpA1", [128, 512], F32) as mpA1,
                nc.psum_tensor("mpB0", [128, 512], F32) as mpB0,
                nc.psum_tensor("mpB1", [128, 512], F32) as mpB1,
                nc.psum_tensor("mpC0", [128, 4], F32) as mpC0,
                nc.psum_tensor("mpC1", [128, 4], F32) as mpC1,
            ):
                mpA = (mpA0, mpA1)
                mpB = (mpB0, mpB1)
                mpC = (mpC0, mpC1)

                @block.gpsimd
                def _(g):
                    for f in range(NF):
                        if f >= 2:
                            g.wait_ge(s_mt, Tc[f - 1])
                        g.dma_start(out=s3(wib[:, (f % 2) * 2048:(f % 2) * 2048 + 2048], 16),
                                    in_=r3(w_in_t[f * 2048:(f + 1) * 2048, :], 16)
                                    ).then_inc(s_mw[f % 2], 16)

                @block.tensor
                def _(t):
                    for f in range(NF):
                        t.wait_ge(s_mw[f % 2], 16 * (f // 2 + 1))
                        if f >= 2:
                            t.wait_ge(s_mv, Vc[f - 1])
                            t.wait_ge(s_msc, Sc[f - 1])
                        wt = wib[:, (f % 2) * 2048:(f % 2) * 2048 + 2048]
                        if f < 32 or f == 80:
                            spl = [(3, 515, mpA), (515, 1027, mpB)]
                        else:
                            spl = [(0, 512, mpA), (512, 1024, mpB), (1024, 1027, mpC)]
                        for si, (lo, hi, ps) in enumerate(spl):
                            for k in range(KD):
                                mm = t.matmul(ps[f % 2][:, 0:hi - lo],
                                              wt[:, k * 128:(k + 1) * 128],
                                              xh2[:, k * 1027 + lo:k * 1027 + hi],
                                              start=(k == 0), stop=(k == KD - 1))
                            mm.then_inc(s_mt, 1)

                @block.vector
                def _(v):
                    for f in range(NF):
                        if f < 32:
                            if f >= 2:
                                v.wait_ge(s_mz[f % 2], 16 * zcum[f % 2][f - 2])
                            for si, ps in ((0, mpA), (1, mpB)):
                                v.wait_ge(s_mt, Tc[f] + si + 1)
                                v.tensor_copy(zstb[:, (f % 2) * 1024 + si * 512:(f % 2) * 1024 + si * 512 + 512],
                                              ps[f % 2][:, :]).then_inc(s_mv, 1)
                        elif f < 80:
                            cf = f - 32
                            base = Vc[f]
                            v.wait_ge(s_msc, Sc[f] + 3)
                            if f >= 34:
                                v.wait_ge(s_msc, Sc[f - 1])
                            xb = xbcf[:, (f % 2) * 1027:(f % 2) * 1027 + 1027]
                            ac = accf[:, (f % 2) * 1024:(f % 2) * 1024 + 1024]
                            v.tensor_scalar(ac[:, :], xb[:, 0:1024],
                                            c_convw[:, cf * 4:cf * 4 + 1],
                                            c_convb[:, cf:cf + 1], ALU.mult,
                                            ALU.add).then_inc(s_mv, 1)
                            for kk in range(1, 4):
                                v.wait_ge(s_mv, base + kk)
                                v.scalar_tensor_tensor(ac[:, :], xb[:, kk:kk + 1024],
                                                       c_convw[:, cf * 4 + kk:cf * 4 + kk + 1],
                                                       ac[:, :], ALU.mult,
                                                       ALU.add).then_inc(s_mv, 1)

                @block.scalar
                def _(s):
                    for f in range(32, NF):
                        if f < 80:
                            if f >= 34:
                                s.wait_ge(s_mv, Vc[f - 1])
                            for si, (lo, hi, ps) in enumerate(((0, 512, mpA), (512, 1024, mpB),
                                                               (1024, 1027, mpC))):
                                s.wait_ge(s_mt, Tc[f] + si + 1)
                                s.copy(xbcf[:, (f % 2) * 1027 + lo:(f % 2) * 1027 + hi],
                                       ps[f % 2][:, 0:hi - lo]).then_inc(s_msc, 1)
                            s.wait_ge(s_mv, Vc[f + 1])
                            if f >= 34:
                                s.wait_ge(s_mx[f % 2], 16 * xcum[f % 2][f - 2])
                            s.activation(xbst[:, (f % 2) * 1024:(f % 2) * 1024 + 1024],
                                         accf[:, (f % 2) * 1024:(f % 2) * 1024 + 1024],
                                         AF.Silu).then_inc(s_msc, 1)
                        else:
                            for si, ps in ((0, mpA), (1, mpB)):
                                s.wait_ge(s_mt, Tc[f] + si + 1)
                                s.activation(ef[:, si * 512:si * 512 + 512],
                                             ps[f % 2][0:64, :], AF.Exp,
                                             bias=c_dtb[:]).then_inc(s_msc, 1)
                            s.wait_ge(s_msc, Sc[f] + 2)
                            s.activation(dtf[:], ef[:], AF.Ln,
                                         bias=c_fc[0:64, 2:3]).then_inc(s_msc, 1)

                @block.sync
                def _(sy):
                    for f in range(NF):
                        if f < 32:
                            sy.wait_ge(s_mv, Vc[f + 1])
                            sy.dma_start(out=zsc.ap()[f * 128:(f + 1) * 128, :],
                                         in_=zstb[:, (f % 2) * 1024:(f % 2) * 1024 + 1024]
                                         ).then_inc(s_mz[f % 2], 16)
                        elif f < 80:
                            cf = f - 32
                            sy.wait_ge(s_msc, Sc[f + 1])
                            sy.dma_start(out=xbcsc.ap()[cf * 128:(cf + 1) * 128, :],
                                         in_=xbst[:, (f % 2) * 1024:(f % 2) * 1024 + 1024]
                                         ).then_inc(s_mx[f % 2], 16)

            xctx.close()
            phase_end(sync_wait=[(s_mz[0], 16 * 16), (s_mz[1], 16 * 16),
                                 (s_mx[0], 16 * 24), (s_mx[1], 16 * 24)])
            emit_scan(stage)
            mctx.close()

        if stage >= 3:
            with nc.sbuf_tensor("hid2", [128, KE * 1024], BF16) as hid2:
                with nc.sbuf_tensor("xh3", [128, KD * 1024], BF16) as xh3:
                    emit_ln("l3", xs2.ap(), xh3)
                    phase_end()
                    emit_ffn_u("f2", xh3, hid2)
                emit_ffn_d("f2", hid2, xs2.ap(), out, False, False)

        phase_end()

    return nc


def emit_mamba(*a, **k):
    raise NotImplementedError("stage >= 2 not implemented yet")


def _stage_inputs(inputs):
    u = np.asarray(inputs["L0_attn_u"], dtype=np.float32)
    W_in = np.asarray(inputs["L0_attn_W_in"], dtype=np.float32)
    conv_w = np.asarray(inputs["L0_attn_conv_w"], dtype=np.float32)
    conv_b = np.asarray(inputs["L0_attn_conv_b"], dtype=np.float32)
    dt_bias = np.asarray(inputs["L0_attn_dt_bias"], dtype=np.float32)
    A_log = np.asarray(inputs["L0_attn_A_log"], dtype=np.float32)
    Dv = np.asarray(inputs["L0_attn_D"], dtype=np.float32)
    norm_w = np.asarray(inputs["L0_attn_norm_weight"], dtype=np.float32)
    W_out = np.asarray(inputs["L0_attn_W_out"], dtype=np.float32)
    W_up = np.asarray(inputs["L0_ffn_W_up"], dtype=np.float32)
    W_g = np.asarray(inputs["L0_ffn_W_gate"], dtype=np.float32)
    W_d = np.asarray(inputs["L0_ffn_W_down"], dtype=np.float32)

    WinT = np.ascontiguousarray(W_in.T)
    WinT_pad = np.zeros((D, NF * 128), dtype=np.float32)
    WinT_pad[:, :10304] = WinT
    w_in_t = np.zeros((NF, 16, 128, 128), dtype=np.float32)
    for f in range(NF):
        for k in range(16):
            w_in_t[f, k] = WinT_pad[k * 128:(k + 1) * 128, f * 128:(f + 1) * 128]
    w_in_t = w_in_t.reshape(NF * 16 * 128, 128)

    WoutT = np.ascontiguousarray(W_out.T)
    w_out_t = np.zeros((16, 32, 128, 128), dtype=np.float32)
    for m in range(16):
        for kc in range(32):
            w_out_t[m, kc] = WoutT[kc * 128:(kc + 1) * 128, m * 128:(m + 1) * 128]
    w_out_t = w_out_t.reshape(16 * 32 * 128, 128)

    WupT = np.ascontiguousarray(W_up.T)
    WgT = np.ascontiguousarray(W_g.T)
    WdT = np.ascontiguousarray(W_d.T)

    identb = np.eye(128, dtype=BF)
    identf = np.eye(128, dtype=np.float32)
    trilb = np.triu(np.ones((128, 128), dtype=np.float32)).astype(BF)
    onesf = np.ones((128, 128), dtype=np.float32)
    onesb = np.ones((128, 128), dtype=BF)
    gn_oh = np.zeros((128, 32, 32), dtype=np.float32)
    for g in range(32):
        gn_oh[:, g, g] = 1.0
    gn_oh = gn_oh.reshape(128, 32 * 32).astype(BF)
    d_expv = np.repeat(Dv, 64).reshape(32, 128).T.copy()
    norm_wt = norm_w.reshape(32, 128).T.copy()

    common = dict(
        w_in_t=w_in_t, w_out_t=w_out_t,
        conv_w=conv_w, conv_b=conv_b[:, None].copy(),
        dt_bias=dt_bias[:, None].copy(), a_log=A_log[:, None].copy(),
        d_exp=d_expv, norm_w=norm_wt,
        identb=identb, identf=identf, trilb=trilb,
        onesf=onesf, onesb=onesb, gn_oh=gn_oh,
    )
    in_maps = []
    for c in range(8):
        s, h = c // 2, c % 2
        a = h * 1024
        m = dict(common)
        m["u_fm"] = np.ascontiguousarray(u[s, a:a + 1024, :].T)
        m["wup_sh"] = np.ascontiguousarray(WupT[:, 1024 * c:1024 * (c + 1)])
        m["wg_sh"] = np.ascontiguousarray(WgT[:, 1024 * c:1024 * (c + 1)])
        m["wd_sh"] = np.ascontiguousarray(WdT[:, 256 * c:256 * (c + 1)])
        fc = np.zeros((128, 8), dtype=np.float32)
        fc[:, 0] = 1e-5
        fc[:, 1] = 1.0 / 128.0
        fc[:, 2] = 1.0
        fc[:, 3] = -1.0
        m["fconst"] = fc
        hm = np.zeros((128, 8), dtype=np.float32)
        if h == 1:
            hm[:, c - 1] = 1.0
        m["hmask"] = hm
        in_maps.append(m)
    return in_maps


def kernel(**inputs):
    in_maps = _stage_inputs(inputs)
    nc = build(STAGE)
    res = run_bass_kernel_spmd(nc, in_maps, core_ids=list(range(8)))
    outp = np.empty((4, 2048, 2048), dtype=np.float32)
    for c in range(8):
        s, h = c // 2, c % 2
        a = h * 1024
        outp[s, a:a + 1024, :] = res.results[c]["out"].T
    return outp
